# version 01
# baseline (speedup 1.0000x reference)
import numpy as np
import ml_dtypes
from contextlib import ExitStack
from operator import add as _op_add

import concourse.bass as bass
import concourse.mybir as mybir
from concourse.bass_utils import run_bass_kernel_spmd

import concourse.dve_ops as dvo
from concourse.dve_op import DveOp
from concourse.dve_spec import Spec, AluOp, Bin, Src0, C0, C1, Zero, lower, sq
from concourse.dve_ops_raw import _compute_dtype

F32 = mybir.dt.float32
BF16 = mybir.dt.bfloat16
I32 = mybir.dt.int32
AX = mybir.AxisListType.X
ALU = mybir.AluOpType
ACT = mybir.ActivationFunctionType

NB = 2
N = 2048
D = 1024
H = 16
DH = 64
HID = 3072
NQB = N // 128
KT = 512
QPRE = float(2.0 ** 27)
SPOST = float(2.0 ** -30)

K_R = -4.25
K2 = float(np.float32(K_R * K_R))
C0_R = float(np.float32(2.002 * K_R))
C1_R = float(np.float32(2.0 * K_R * K_R))
K3 = float(np.float64(K_R) ** 3)
NEG_KR = float(-K_R)


def _ref_recip7(in0, in1, c0, c1, c2):
    f = _compute_dtype(in0)
    seed = np.asarray(c1, f).reshape(-1, 1) if hasattr(c1, "reshape") else c1
    if f is np.float64:
        out = np.float64(K3 * K_R) / in0
        return out, seed + out.reshape(out.shape[0], -1).sum(
            axis=-1, keepdims=True)
    x = np.ascontiguousarray(in0, dtype=np.float32)
    n = (~x.view(np.int32)).view(np.float32)
    z1 = (n * (np.float32(c0) - x * n).astype(np.float32)).astype(np.float32)
    out = (z1 * (np.float32(c2) - x * z1).astype(np.float32)).astype(np.float32)
    return out, seed + out.reshape(out.shape[0], -1).astype(np.float32).sum(
        axis=-1, keepdims=True)


def _ref_sq_sum(in0, in1, c0, c1, c2):
    f = _compute_dtype(in0)
    seed = np.asarray(c0, f).reshape(-1, 1) if hasattr(c0, "reshape") else c0
    b = np.square(in0.astype(f))
    return b, seed + b.reshape(b.shape[0], -1).sum(axis=-1, keepdims=True)


def _mk_ops():
    from concourse.dve_spec import C2
    n = Bin(AluOp.BITWISE_NOT, Src0, Src0)
    z1 = n * (C0 - Src0 * n)
    body = z1 * (C2 - Src0 * z1)
    spec_r = Spec(body=body, accum=_op_add, accum_init=C1)
    sha_r = lower(spec_r, ver="v3", perf_en=False, name="RECIP7_SUM").sha("v3")
    op_r = DveOp("RECIP7_SUM", spec_r, subdim=False, uops_sha={"v3": sha_r},
                 reference=_ref_recip7)
    spec_s = Spec(body=sq(Src0), accum=_op_add, accum_init=C0)
    sha_s = lower(spec_s, ver="v3", perf_en=False, name="SQ_SUM").sha("v3")
    op_s = DveOp("SQ_SUM", spec_s, subdim=False, uops_sha={"v3": sha_s},
                 reference=_ref_sq_sum)
    return op_r, op_s


def _register(op):
    if op.name in dvo._SUB_OPCODE_FOR_NAME:
        return
    dvo.OPS.append(op)
    dvo._SUB_OPCODE_FOR_NAME[op.name] = max(dvo._SUB_OPCODE_FOR_NAME.values()) + 1
    assert dvo._SUB_OPCODE_FOR_NAME[op.name] < 0x20
    dvo.CUSTOM_DVE_SPECS[op.name] = op.spec
    dvo.CUSTOM_DVE_REFS[op.name] = op.ref


RECIP7_SUM, SQ_SUM = _mk_ops()
_register(RECIP7_SUM)
_register(SQ_SUM)


def _host_consts():
    ident = np.eye(128).astype(ml_dtypes.bfloat16)
    identf = np.eye(128).astype(np.float32)
    lens = np.zeros((128, NQB * 4 + 1), dtype=np.float32)
    p = np.arange(128)
    for qb in range(NQB):
        for kt in range(4):
            lens[:, qb * 4 + kt] = np.clip(128 * qb + p + 1 - KT * kt, 0, KT)
    lens[:, NQB * 4] = 1e-5
    return ident, identf, lens


class Sem:
    __slots__ = ("h", "n")

    def __init__(self, handle):
        self.h = handle
        self.n = 0

    def inc(self, k=1):
        self.n += k
        return self.n


def tiles_of(qb):
    return qb // 4 + 1


def _run_plans(nc, PT, PV, PS, PY, PG=None):
    with nc.Block() as blk:
        blk.tensor(lambda t: [f(t) for f in PT])
        blk.vector(lambda v: [f(v) for f in PV])
        blk.scalar(lambda s: [f(s) for f in PS])
        blk.sync(lambda y: [f(y) for f in PY])
        if PG:
            blk.gpsimd(lambda g_: [f(g_) for f in PG])


def build(dbg=False, phases=3):
    nc = bass.Bass("TRN2", target_bir_lowering=False,
                   dynamic_dma_scratch_size=2048)

    x_in = nc.dram_tensor("x", [NB, N, D], F32, kind="ExternalInput")
    wqkv = nc.dram_tensor("wqkv", [3 * D, D], F32, kind="ExternalInput")
    wo = nc.dram_tensor("wo", [D, D], F32, kind="ExternalInput")
    wg = nc.dram_tensor("wg", [HID, D], I32, kind="ExternalInput")
    wu = nc.dram_tensor("wu", [HID, D], I32, kind="ExternalInput")
    wd = nc.dram_tensor("wd", [D, HID], I32, kind="ExternalInput")
    ident_d = nc.dram_tensor("ident", [128, 128], BF16, kind="ExternalInput")
    identf_d = nc.dram_tensor("identf", [128, 128], F32, kind="ExternalInput")
    lens_d = nc.dram_tensor("lens", [128, NQB * 4 + 1], F32, kind="ExternalInput")
    out_d = nc.dram_tensor("out", [NB, N, D], F32, kind="ExternalOutput")

    dk = "ExternalOutput" if dbg else "Internal"
    oT_sc = nc.dram_tensor("oT_sc", [NB, H, DH, N], BF16, kind=dk)
    wqkvT_sc = nc.dram_tensor("wqkvT_sc", [D, 3 * D], BF16, kind=dk)
    woT_sc = nc.dram_tensor("woT_sc", [D, D], BF16, kind=dk)
    wgT_sc = nc.dram_tensor("wgT_sc", [D, HID], BF16, kind=dk)
    wuT_sc = nc.dram_tensor("wuT_sc", [D, HID], BF16, kind=dk)
    wdT_sc = nc.dram_tensor("wdT_sc", [HID, D], BF16, kind=dk)
    if dbg:
        xn_dbg = nc.dram_tensor("xn_dbg", [128, 8 * NB * N], BF16,
                                kind="ExternalOutput")
        dbg_st_d = nc.dram_tensor("dbg_st", [128, 32 * 8], F32,
                                  kind="ExternalOutput")
        dbg_sgrp = nc.dram_tensor("dbg_sgrp", [2, 128, 28 * KT], F32,
                                  kind="ExternalOutput")
        dbg_rmax = nc.dram_tensor("dbg_rmax", [2, 128, 32], F32,
                                  kind="ExternalOutput")
        dbg_lam = nc.dram_tensor("dbg_lam", [2, 128, 8], F32,
                                 kind="ExternalOutput")
        dbg_rn = nc.dram_tensor("dbg_rn", [2, 128, 8], F32,
                                kind="ExternalOutput")
        dbg_q = nc.dram_tensor("dbg_q", [128, N], BF16, kind="ExternalOutput")
        dbg_it = nc.dram_tensor("dbg_it", [2, 6, 128, 16], F32,
                                kind="ExternalOutput")
        dbg_p = nc.dram_tensor("dbg_p", [128, KT], BF16, kind="ExternalOutput")
        dbg_pt = nc.dram_tensor("dbg_pt", [128, KT], BF16, kind="ExternalOutput")
        dbg_pp = nc.dram_tensor("dbg_pp", [2, 6, 128, 16], F32,
                                kind="ExternalOutput")
        dbg_k = nc.dram_tensor("dbg_k", [128, N], BF16, kind="ExternalOutput")
        dbg_v = nc.dram_tensor("dbg_v", [128, N], BF16, kind="ExternalOutput")

    with ExitStack() as outer:
        oe = outer.enter_context
        xn = oe(nc.sbuf_tensor("xn", [128, 8 * NB * N], BF16))
        ident = oe(nc.sbuf_tensor("ident_sb", [128, 128], BF16))
        identf = oe(nc.sbuf_tensor("identf_sb", [128, 128], F32))
        lens_sb = oe(nc.sbuf_tensor("lens_sb", [128, NQB * 4 + 1], F32))

        def mk_sems(names):
            return {k: Sem(oe(nc.semaphore(k))) for k in names}

        S1 = mk_sems(["p1_ld", "p1_v", "p1_s", "p1_t", "p1_st"])
        S2 = mk_sems(["p2_ld", "p2_qmm", "p2_qev", "p2_smm", "p2_sfree",
                      "p2_pr", "p2_tpd", "p2_pte", "p2_pvd", "p2_opv",
                      "p2_oev", "p2_odma"])
        S3 = mk_sems(["p3_ld", "p3_gmm", "p3_hdone", "p3_sgd", "p3_womm",
                      "p3_dnmm", "p3_gsil", "p3_vout", "p3_st"])

        with ExitStack() as p1:
            e1 = p1.enter_context
            stg_f = [e1(nc.sbuf_tensor(f"stg_f{i}", [128, D], F32)) for i in range(2)]
            stg_i = [e1(nc.sbuf_tensor(f"stg_i{i}", [128, HID], I32)) for i in range(2)]
            wcast = [e1(nc.sbuf_tensor(f"wcast{i}", [128, HID], BF16)) for i in range(2)]
            wstage = [e1(nc.sbuf_tensor(f"wstage{i}", [128, 1024], BF16)) for i in range(2)]
            xt = [e1(nc.sbuf_tensor(f"xt{i}", [128, D], F32)) for i in range(8)]
            xn_tok = e1(nc.sbuf_tensor("xn_tok", [128, D], BF16))
            sqw = e1(nc.sbuf_tensor("sqw1", [128, D], F32))
            stat = e1(nc.sbuf_tensor("stat", [128, 64], F32))
            dbg_sb = e1(nc.sbuf_tensor("dbg_sb", [128, 32 * 8], F32)) if dbg else None
            xbb = [e1(nc.sbuf_tensor(f"xbb{i}", [128, D], BF16)) for i in range(2)]
            xfmb = [e1(nc.sbuf_tensor(f"xfmb{i}", [128, D], BF16)) for i in range(2)]
            x2fmb = [e1(nc.sbuf_tensor(f"x2fmb{i}", [128, D], BF16)) for i in range(2)]
            strow = [e1(nc.sbuf_tensor(f"strow{i}", [1, 256], F32)) for i in range(2)]
            onesb = e1(nc.sbuf_tensor("onesb", [128, 2], BF16))
            ps_t = [e1(nc.psum_tensor(f"ps_t{i}", [128, 1024], BF16)) for i in range(2)]
            ps_t2 = [e1(nc.psum_tensor(f"ps_t2{i}", [128, 1024], BF16)) for i in range(2)]
            ps_st = [e1(nc.psum_tensor(f"ps_st{i}", [2, 256], F32)) for i in range(2)]
            ps_tt = [e1(nc.psum_tensor(f"ps_tt{i}", [128, 2], F32)) for i in range(2)]

            ld, vv, ss, tt, st = (S1["p1_ld"], S1["p1_v"], S1["p1_s"],
                                  S1["p1_t"], S1["p1_st"])
            PT, PV, PS, PY = [], [], [], []

            PY.append(lambda y: y.dma_start(out=ident[:, :],
                                            in_=ident_d[:, :]).then_inc(ld.h, 16))
            ld.inc(16)
            PY.append(lambda y: y.dma_start(out=lens_sb[:, :],
                                            in_=lens_d[:, :]).then_inc(ld.h, 16))
            ld.inc(16)
            PY.append(lambda y: y.dma_start(out=identf[:, :],
                                            in_=identf_d[:, :]).then_inc(ld.h, 16))
            ld.inc(16)
            PT.append(lambda t: t.wait_ge(ld.h, 48))

            def _swarm(s):
                s.wait_ge(ld.h, 48)
                s.memzero(sqw[:, :])
                s.activation(out=sqw[:, :], in_=sqw[:, :], func=ACT.Copy,
                             bias=0.0, scale=1.0)
                s.activation(out=sqw[:, :], in_=sqw[:, :], func=ACT.Square,
                             bias=0.0, scale=1.0)
                s.activation(out=sqw[:, :], in_=sqw[:, :], func=ACT.Exp,
                             bias=0.0, scale=0.0)
                s.activation(out=sqw[:, :], in_=sqw[:, :], func=ACT.Ln,
                             bias=lens_sb[:, NQB * 4:NQB * 4 + 1], scale=0.0)
                s.activation(out=sqw[:, :], in_=sqw[:, :], func=ACT.Exp,
                             bias=0.0, scale=0.0)
                s.memzero(sqw[:, :])
            PS.append(_swarm)

            def _vmo(v):
                v.memset(onesb[:, 0:1], 1.0)
                v.memset(onesb[:, 1:2], 0.0).then_inc(vv.h, 1)
            PV.append(_vmo)
            vv.inc()
            PT.append(lambda t: t.wait_ge(vv.h, 1))
            def _twarm(t):
                t.wait_ge(vv.h, 1)
                for pb in range(2):
                    for c in range(4):
                        t.matmul(ps_st[pb][0:2, 0:128], onesb[:, :],
                                 ident[:, 0:128], start=(c == 0), stop=(c == 3))
                    for c in range(4):
                        t.matmul(ps_st[pb][0:2, 128:256], onesb[:, :],
                                 ident[:, 0:128], start=(c == 0), stop=(c == 3))
                    t.transpose(ps_t[pb][:, 0:128], ident[:, :], ident[:, :])
                    t.transpose(ps_t2[pb][:, 0:128], ident[:, :], ident[:, :])
                    t.transpose(ps_tt[pb][:, 0:1], identf[0:1, 0:128],
                                identf[0:1, 0:1])
                ins = t.matmul(ps_st[0][0:2, 0:128], onesb[:, :],
                               ident[:, 0:128], start=True, stop=True)
                ins.then_inc(tt.h, 1)
            PT.append(_twarm)
            tt.inc()
            PV.append(lambda v: v.wait_ge(tt.h, 1))

            v_done = {i: 0 for i in range(8)}
            xb_free = {0: 0, 1: 0}
            s_evacA = {0: 0, 1: 0}
            s_strow = {0: 0, 1: 0}
            s0_done = {0: 0, 1: 0}
            stat_free = {i: 0 for i in range(8)}
            s_evac2 = {0: 0, 1: 0}
            xn_free = [0]
            ntile = NB * N // 128
            w_s3_of = {}

            def ln_front(j):
                b, t0 = j // (N // 128), (j % (N // 128)) * 128
                buf = j % 8
                pb = j % 2
                base = (j % 8) * 8
                if v_done[buf]:
                    PY.append(lambda y, w=v_done[buf]: y.wait_ge(vv.h, w))
                PY.append(lambda y, b=b, t0=t0, buf=buf: y.dma_start(
                    out=xt[buf][:, :],
                    in_=x_in[b, t0:t0 + 128, :]).then_inc(ld.h, 16))
                myld = ld.inc(16)

                def _vc(v, buf=buf, pb=pb, myld=myld, wtf=xb_free[pb]):
                    v.wait_ge(ld.h, myld)
                    if wtf:
                        v.wait_ge(tt.h, wtf)
                    v.tensor_copy(out=xbb[pb][:, :],
                                  in_=xt[buf][:, :]).then_inc(vv.h, 1)
                PV.append(_vc)
                w_cast = vv.inc()

                def _t1(t, pb=pb, w=w_cast, wse=s_evacA[pb]):
                    t.wait_ge(vv.h, w)
                    if wse:
                        t.wait_ge(ss.h, wse)
                    for c in range(8):
                        ins = t.transpose(ps_t[pb][:, c * 128:(c + 1) * 128],
                                          xbb[pb][:, c * 128:(c + 1) * 128],
                                          ident[:, :])
                    ins.then_inc(tt.h, 1)
                PT.append(_t1)
                w_t1 = tt.inc()
                xb_free[pb] = w_t1

                def _s1(s, pb=pb, w=w_t1):
                    s.wait_ge(tt.h, w)
                    s.copy(out=xfmb[pb][:, :], in_=ps_t[pb][:, :])
                    s.activation(out=x2fmb[pb][:, :], in_=xfmb[pb][:, :],
                                 func=ACT.Square, bias=0.0,
                                 scale=1.0).then_inc(ss.h, 1)
                PS.append(_s1)
                w_s1 = ss.inc()
                s_evacA[pb] = w_s1

                def _t2(t, pb=pb, w=w_s1, wsr=s_strow[pb]):
                    t.wait_ge(ss.h, w)
                    if wsr:
                        t.wait_ge(ss.h, wsr)
                    for c in range(8):
                        t.matmul(ps_st[pb][0:2, 0:128], onesb[:, :],
                                 xfmb[pb][:, c * 128:(c + 1) * 128],
                                 start=(c == 0), stop=(c == 7))
                    for c in range(8):
                        ins = t.matmul(ps_st[pb][0:2, 128:256], onesb[:, :],
                                       x2fmb[pb][:, c * 128:(c + 1) * 128],
                                       start=(c == 0), stop=(c == 7))
                    ins.then_inc(tt.h, 1)
                PT.append(_t2)
                w_t2 = tt.inc()

                def _s2(s, pb=pb, w=w_t2):
                    s.wait_ge(tt.h, w)
                    s.copy(out=strow[pb][0:1, :],
                           in_=ps_st[pb][0:1, :]).then_inc(ss.h, 1)
                PS.append(_s2)
                w_s2 = ss.inc()
                s_strow[pb] = w_s2

                def _t3(t, pb=pb, w=w_s2, wst=s0_done[pb]):
                    t.wait_ge(ss.h, w)
                    if wst:
                        t.wait_ge(ss.h, wst)
                    t.transpose(ps_tt[pb][:, 0:1], strow[pb][0:1, 0:128],
                                identf[0:1, 0:1])
                    t.transpose(ps_tt[pb][:, 1:2], strow[pb][0:1, 128:256],
                                identf[0:1, 0:1]).then_inc(tt.h, 1)
                PT.append(_t3)
                w_t3 = tt.inc()

                def _s0(s, pb=pb, w=w_t3, base=base, wsf=stat_free[j % 8]):
                    s.wait_ge(tt.h, w)
                    if wsf:
                        s.wait_ge(vv.h, wsf)
                    s.copy(out=stat[:, base + 0:base + 2],
                           in_=ps_tt[pb][:, 0:2]).then_inc(ss.h, 1)
                PS.append(_s0)
                s0_done[pb] = ss.inc()

            def ln_stage(j, k):
                base = (j % 8) * 8

                def _sk(s, base=base, k=k, j=j):
                    if k == 1:
                        s.activation(out=stat[:, base + 2:base + 3],
                                     in_=stat[:, base + 0:base + 1],
                                     func=ACT.Copy, bias=0.0, scale=1.0 / D)
                    elif k == 2:
                        s.activation(out=stat[:, base + 3:base + 4],
                                     in_=stat[:, base + 2:base + 3],
                                     func=ACT.Square, bias=0.0, scale=1.0)
                    elif k == 3:
                        s.activation(out=stat[:, base + 4:base + 5],
                                     in_=stat[:, base + 3:base + 4],
                                     func=ACT.Copy, bias=0.0, scale=-1.0)
                    elif k == 4:
                        s.activation(out=stat[:, base + 5:base + 6],
                                     in_=stat[:, base + 1:base + 2],
                                     func=ACT.Identity,
                                     bias=stat[:, base + 4:base + 5],
                                     scale=1.0 / D)
                    elif k == 5:
                        s.activation(out=stat[:, base + 6:base + 7],
                                     in_=stat[:, base + 5:base + 6], func=ACT.Ln,
                                     bias=lens_sb[:, NQB * 4:NQB * 4 + 1],
                                     scale=1.0)
                    else:
                        ins = s.activation(out=stat[:, base + 7:base + 8],
                                           in_=stat[:, base + 6:base + 7],
                                           func=ACT.Exp, bias=0.0, scale=-0.5)
                        if dbg:
                            s.copy(out=dbg_sb[:, j * 8:(j + 1) * 8],
                                   in_=stat[:, base:base + 8])
                        ins.then_inc(ss.h, 1)
                PS.append(_sk)
                if k == 6:
                    w_s3_of[j] = ss.inc()

            def ln_back(j):
                b, t0 = j // (N // 128), (j % (N // 128)) * 128
                buf = j % 8
                pb = j % 2
                base = (j % 8) * 8

                def _vn(v, buf=buf, w=w_s3_of[j], base=base, wxf=xn_free[0]):
                    v.wait_ge(ss.h, w)
                    if wxf:
                        v.wait_ge(tt.h, wxf)
                    v.tensor_scalar(out=xn_tok[:, :], in0=xt[buf][:, :],
                                    scalar1=stat[:, base + 2:base + 3],
                                    scalar2=stat[:, base + 7:base + 8],
                                    op0=ALU.subtract,
                                    op1=ALU.mult).then_inc(vv.h, 1)
                PV.append(_vn)
                v_done[buf] = vv.inc()

                def _t4(t, pb=pb, wv=v_done[buf], wse=s_evac2[pb]):
                    t.wait_ge(vv.h, wv)
                    if wse:
                        t.wait_ge(ss.h, wse)
                    for c in range(8):
                        ins = t.transpose(ps_t2[pb][:, c * 128:(c + 1) * 128],
                                          xn_tok[:, c * 128:(c + 1) * 128],
                                          ident[:, :])
                    ins.then_inc(tt.h, 1)
                PT.append(_t4)
                w_t4 = tt.inc()
                xn_free[0] = w_t4

                def _s4(s, pb=pb, w=w_t4, b=b, t0=t0):
                    s.wait_ge(tt.h, w)
                    for c in range(8):
                        col = c * (NB * N) + b * N + t0
                        ins = s.copy(out=xn[:, col:col + 128],
                                     in_=ps_t2[pb][:, c * 128:(c + 1) * 128])
                    ins.then_inc(ss.h, 1)
                PS.append(_s4)
                s_evac2[pb] = ss.inc()
                stat_free[buf] = v_done[buf]

            for jj in range(ntile + 7):
                if jj < ntile:
                    ln_front(jj)
                for k in range(1, 7):
                    j2 = jj - k
                    if 0 <= j2 < ntile:
                        ln_stage(j2, k)
                if 0 <= jj - 6 < ntile:
                    ln_back(jj - 6)

            wv_done = {0: 0, 1: 0}
            wst_done = {0: 0, 1: 0}

            def prep_weight(src, Rr, Cc, is_int, dst):
                nrc = Rr // 128
                ncc = Cc // 128
                for rc in range(nrc):
                    buf = rc % 2
                    stg = stg_i[buf] if is_int else stg_f[buf]
                    if wv_done[buf]:
                        PY.append(lambda y, w=wv_done[buf]: y.wait_ge(vv.h, w))
                    PY.append(lambda y, src=src, rc=rc, stg=stg, Cc=Cc:
                              y.dma_start(out=stg[:, 0:Cc],
                                          in_=src[rc * 128:(rc + 1) * 128, :])
                              .then_inc(ld.h, 16))
                    myld = ld.inc(16)

                    def _vc(v, stg=stg, buf=buf, myld=myld, is_int=is_int, Cc=Cc):
                        v.wait_ge(ld.h, myld)
                        v.tensor_scalar(out=wcast[buf][:, 0:Cc], in0=stg[:, 0:Cc],
                                        scalar1=(1.0 / 127.0) if is_int else 1.0,
                                        scalar2=None, op0=ALU.mult).then_inc(vv.h, 1)
                    PV.append(_vc)
                    wv_done[buf] = vv.inc()

                    for cg in range(0, ncc, 8):
                        nblk = min(8, ncc - cg)

                        def _tt(t, buf=buf, wvc=wv_done[buf], cg=cg, nblk=nblk,
                                wse=s_evacA[buf]):
                            t.wait_ge(vv.h, wvc)
                            if wse:
                                t.wait_ge(ss.h, wse)
                            for j in range(nblk):
                                c = cg + j
                                ins = t.transpose(
                                    ps_t[buf][:, j * 128:(j + 1) * 128],
                                    wcast[buf][:, c * 128:(c + 1) * 128],
                                    ident[:, :])
                            ins.then_inc(tt.h, 1)
                        PT.append(_tt)
                        w_tp = tt.inc()

                        def _se(s, buf=buf, w=w_tp, nblk=nblk, wst=wst_done[buf]):
                            s.wait_ge(tt.h, w)
                            if wst:
                                s.wait_ge(st.h, wst)
                            s.copy(out=wstage[buf][:, 0:nblk * 128],
                                   in_=ps_t[buf][:, 0:nblk * 128]).then_inc(ss.h, 1)
                        PS.append(_se)
                        s_evacA[buf] = ss.inc()

                        def _yd(y, buf=buf, w=s_evacA[buf], cg=cg, nblk=nblk,
                                rc=rc, dst=dst):
                            y.wait_ge(ss.h, w)
                            for j in range(nblk):
                                c = cg + j
                                y.dma_start(
                                    out=dst[c * 128:(c + 1) * 128,
                                            rc * 128:(rc + 1) * 128],
                                    in_=wstage[buf][:, j * 128:(j + 1) * 128]
                                ).then_inc(st.h, 16)
                        PY.append(_yd)
                        wst_done[buf] = st.inc(16 * nblk)

            prep_weight(wqkv, 3 * D, D, False, wqkvT_sc)
            prep_weight(wo, D, D, False, woT_sc)
            prep_weight(wg, HID, D, True, wgT_sc)
            prep_weight(wu, HID, D, True, wuT_sc)
            prep_weight(wd, D, HID, True, wdT_sc)

            if dbg:
                PY.append(lambda y: y.wait_ge(ss.h, ss.n))
                PY.append(lambda y: y.dma_start(out=xn_dbg[:, :],
                                                in_=xn[:, :]).then_inc(st.h, 16))
                st.inc(16)
                PY.append(lambda y: y.dma_start(out=dbg_st_d[:, :],
                                                in_=dbg_sb[:, :]).then_inc(st.h, 16))
                st.inc(16)
            PY.append(lambda y: y.wait_ge(st.h, st.n))
            PY.append(lambda y: y.wait_ge(ss.h, ss.n))
            _run_plans(nc, PT, PV, PS, PY)

        if phases < 2:
            return _finish(nc)
        with ExitStack() as p2:
            e2 = p2.enter_context
            wqkvT = e2(nc.sbuf_tensor("wqkvT", [128, 8 * 3 * D], BF16))
            qpair = e2(nc.sbuf_tensor("qpair", [128, N], BF16))
            kpair = e2(nc.sbuf_tensor("kpair", [128, N], BF16))
            vpair = e2(nc.sbuf_tensor("vpair", [128, N], BF16))
            sgrp = e2(nc.sbuf_tensor("sgrp", [128, 28 * KT], F32))
            dscr = e2(nc.sbuf_tensor("dscr", [128, KT], F32))
            sqw2 = e2(nc.sbuf_tensor("sqw2", [128, KT], F32))
            rstrip = e2(nc.sbuf_tensor("rstrip", [128, 12 * KT], F32))
            pbf = [e2(nc.sbuf_tensor(f"pbf{i}", [128, KT], BF16)) for i in range(2)]
            ptsb = [e2(nc.sbuf_tensor(f"ptsb{i}", [128, KT], BF16)) for i in range(2)]
            othead = [e2(nc.sbuf_tensor(f"othead{i}", [64, N], BF16)) for i in range(2)]
            rowmaxP = e2(nc.sbuf_tensor("rowmaxP", [128, 32], F32))
            lam = e2(nc.sbuf_tensor("lam", [128, 16], F32))
            lamK = e2(nc.sbuf_tensor("lamK", [128, 8], F32))
            srd = e2(nc.sbuf_tensor("srd", [128, 8], F32))
            sr2 = e2(nc.sbuf_tensor("sr2", [128, 8], F32))
            nt1 = e2(nc.sbuf_tensor("nt1", [128, 8], F32))
            nt2 = e2(nc.sbuf_tensor("nt2", [128, 8], F32))
            partR = e2(nc.sbuf_tensor("partR", [128, 8], F32))
            partR2 = e2(nc.sbuf_tensor("partR2", [128, 8], F32))
            partF = e2(nc.sbuf_tensor("partF", [128, 12], F32))
            junk1 = e2(nc.sbuf_tensor("junk1", [128, 1], F32))
            rnorm = e2(nc.sbuf_tensor("rnorm", [128, 8], F32))
            rnsc = e2(nc.sbuf_tensor("rnsc", [128, 4], F32))

            ps_q = [e2(nc.psum_tensor(f"ps_q{i}", [128, KT], F32)) for i in range(2)]
            ps_s = [e2(nc.psum_tensor(f"ps_s{i}", [128, KT], F32)) for i in range(2)]
            ps_pt = [e2(nc.psum_tensor(f"ps_pt{i}", [128, KT], BF16)) for i in range(2)]
            ps_o = [e2(nc.psum_tensor(f"ps_o{i}", [64, 128], F32)) for i in range(2)]

            ld, qmm, qev, smm, sfree = (S2["p2_ld"], S2["p2_qmm"], S2["p2_qev"],
                                        S2["p2_smm"], S2["p2_sfree"])
            pr, tpd, pte, pvd, opv = (S2["p2_pr"], S2["p2_tpd"], S2["p2_pte"],
                                      S2["p2_pvd"], S2["p2_opv"])
            oev, odma = S2["p2_oev"], S2["p2_odma"]
            PT, PV, PS, PY = [], [], [], []
            dbgs = Sem(e2(nc.semaphore("p2_dbg")))
            dbgv = Sem(e2(nc.semaphore("p2_dbgv")))
            uvm = Sem(e2(nc.semaphore("p2_uvm")))
            gup = Sem(e2(nc.semaphore("p2_gup")))
            gup_need = {0: 0, 1: 0}
            PG = []

            def _ldw(y):
                for c in range(8):
                    y.dma_start(
                        out=wqkvT[:, c * (3 * D):(c + 1) * (3 * D)],
                        in_=wqkvT_sc[c * 128:(c + 1) * 128, :]).then_inc(ld.h, 16)
            PY.append(_ldw)
            w_wqkv = ld.inc(16 * 8)

            WARMUP = False
            def _vwarm(v):
                v.memset(dscr[:, :], 1.0)
                v.memset(partR2[:, 0:8], 1.0)
                v.memset(lam[:, :], 1.0)
                v.memset(nt1[:, :], 1.0)
                v.memset(nt2[:, :], 1.0)
                v.tensor_mask_reduce(
                    out=rstrip[:, 0:KT], in_=dscr[:, :], mask_start=0.0,
                    mask_end=lens_sb[:, 3:4], scale=1.0, accum_in=-1e30,
                    op=ALU.max, accum_out=partF[:, 0:1])
                v._custom_dve(RECIP7_SUM, out=rstrip[:, 0:KT], in0=dscr[:, :],
                              s0=C0_R, s1=0.0, imm2=C1_R,
                              accum_out=partR[:, 0:1])
                v._custom_dve(SQ_SUM, out=rstrip[:, KT:2 * KT],
                              in0=rstrip[:, 0:KT], s0=0.0, s1=0.0,
                              accum_out=partR[:, 1:2])
                v.reciprocal_approx_accurate(out=nt2[:, 0:4],
                                             in_=partR2[:, 0:4],
                                             scratch=srd[:, 0:4])
                v.tensor_copy(out=junk1[:, :],
                              in_=junk1[:, :]).then_inc(uvm.h, 1)
            if WARMUP:
                PV.append(_vwarm)
                uvm.inc()

            def _gwarm(g_):
                g_.wait_ge(uvm.h, 1)
                g_.tensor_tensor(out=sr2[:, 0:4], in0=nt1[:, 0:4],
                                 in1=nt2[:, 0:4], op=ALU.mult)
                g_.tensor_tensor(out=lam[:, 8:12], in0=lam[:, 0:4],
                                 in1=sr2[:, 0:4], op=ALU.add)
                g_.tensor_scalar(out=lamK[:, 0:4], in0=lam[:, 8:12],
                                 scalar1=K_R, scalar2=None,
                                 op0=ALU.mult).then_inc(gup.h, 1)
            if WARMUP:
                PG.append(_gwarm)
                gup.inc()
                PV.append(lambda v: v.wait_ge(gup.h, 1))

            qev_bank = {0: 0, 1: 0}
            pte_bank = {0: 0, 1: 0}
            pvd_slot = {0: 0, 1: 0}
            sfree_tile = {0: 0, 1: 0}
            oev_row = {0: 0, 1: 0}
            odma_buf = {0: 0, 1: 0}
            tpd_slot = {0: 0, 1: 0}

            state = {"qkv_tile": 0, "s_tile": 0, "p_tile": 0}

            def qkv_pair(b, hp):
                for jbase, scl, dstt in ((128 * hp, QPRE, qpair),
                                         (D + 128 * hp, 1.0, kpair)):
                    for tc in range(4):
                        bank = state["qkv_tile"] % 2
                        t0 = tc * KT

                        def _tm(t, jbase=jbase, t0=t0, bank=bank, b=b,
                                wfree=qev_bank[bank]):
                            t.wait_ge(ld.h, w_wqkv)
                            if wfree:
                                t.wait_ge(qev.h, wfree)
                            for c in range(8):
                                ins = t.matmul(
                                    ps_q[bank][:, :],
                                    wqkvT[:, c * 3 * D + jbase:
                                          c * 3 * D + jbase + 128],
                                    xn[:, c * NB * N + b * N + t0:
                                       c * NB * N + b * N + t0 + KT],
                                    start=(c == 0), stop=(c == 7))
                            ins.then_inc(qmm.h, 1)
                        PT.append(_tm)
                        w_mm = qmm.inc()

                        def _sev(s, bank=bank, w=w_mm, dstt=dstt, t0=t0, scl=scl):
                            s.wait_ge(qmm.h, w)
                            s.activation(out=dstt[:, t0:t0 + KT],
                                         in_=ps_q[bank][:, :], func=ACT.Copy,
                                         bias=0.0, scale=scl).then_inc(qev.h, 1)
                        PS.append(_sev)
                        qev_bank[bank] = qev.inc()
                        state["qkv_tile"] += 1
                jbase = 2 * D + 128 * hp
                for tc in range(NQB):
                    bank = state["qkv_tile"] % 2
                    t0 = tc * 128

                    def _tv(t, jbase=jbase, t0=t0, bank=bank, b=b,
                            wfree=qev_bank[bank]):
                        if wfree:
                            t.wait_ge(qev.h, wfree)
                        for c in range(8):
                            ins = t.matmul(
                                ps_q[bank][:, 0:128],
                                xn[:, c * NB * N + b * N + t0:
                                   c * NB * N + b * N + t0 + 128],
                                wqkvT[:, c * 3 * D + jbase:
                                      c * 3 * D + jbase + 128],
                                start=(c == 0), stop=(c == 7))
                        ins.then_inc(qmm.h, 1)
                    PT.append(_tv)
                    w_mm = qmm.inc()

                    def _svev(s, bank=bank, w=w_mm, tc=tc):
                        s.wait_ge(qmm.h, w)
                        s.activation(out=vpair[:, tc * 128:(tc + 1) * 128],
                                     in_=ps_q[bank][:, 0:128], func=ACT.Copy,
                                     bias=0.0, scale=1.0).then_inc(qev.h, 1)
                    PS.append(_svev)
                    qev_bank[bank] = qev.inc()
                    state["qkv_tile"] += 1

            def attn_group(b, h, g, oth_buf, w_pair):
                par = h % 2
                qb0 = g * 8
                rows = list(range(qb0, qb0 + 8))
                halves = [rows[0:4], rows[4:8]]
                tiles_rm = [(qb, kt) for qb in rows for kt in range(tiles_of(qb))]
                slot = {t: i for i, t in enumerate(tiles_rm)}

                tiles_cm = [(qb, kt) for kt in range(4) for qb in rows
                            if kt < tiles_of(qb)]
                for qb, kt in tiles_cm:
                    bank = state["s_tile"] % 2
                    sl = slot[(qb, kt)]

                    def _ts(t, qb=qb, kt=kt, bank=bank, par=par,
                            wfree=sfree_tile[bank], w_pair=w_pair):
                        t.wait_ge(qev.h, w_pair)
                        if wfree:
                            t.wait_ge(sfree.h, wfree)
                        t.matmul(ps_s[bank][:, :],
                                 qpair[64 * par:64 * par + 64,
                                       qb * 128:(qb + 1) * 128],
                                 kpair[64 * par:64 * par + 64,
                                       kt * KT:(kt + 1) * KT],
                                 start=True, stop=True).then_inc(smm.h, 1)
                    PT.append(_ts)
                    w_smm = smm.inc()
                    first = (kt == 0)
                    rc = qb - qb0

                    def _vp(v, bank=bank, w=w_smm, sl=sl, qb=qb, kt=kt,
                            rc=rc):
                        v.wait_ge(smm.h, w)
                        v.tensor_mask_reduce(
                            out=sgrp[:, sl * KT:(sl + 1) * KT],
                            in_=ps_s[bank][:, :],
                            mask_start=0.0,
                            mask_end=lens_sb[:, qb * 4 + kt:qb * 4 + kt + 1],
                            scale=SPOST,
                            accum_in=-1e30,
                            op=ALU.max,
                            accum_out=rowmaxP[:, rc * 4 + kt:rc * 4 + kt + 1]
                        ).then_inc(sfree.h, 1)
                    PV.append(_vp)
                    sfree_tile[bank] = sfree.inc()
                    state["s_tile"] += 1

                def emit_lambda_init(hf):
                    hb = hf * 4
                    nkt = tiles_of(qb0 + hb)

                    def _vl(v, hb=hb, nkt=nkt):
                        pass
                    def _vl_real(v, hb=hb, nkt=nkt):
                        v.drain()
                        for rc in range(hb, hb + 4):
                            c0 = rc * 4
                            if nkt == 1:
                                v.tensor_scalar(out=lam[:, rc:rc + 1],
                                                in0=rowmaxP[:, c0:c0 + 1],
                                                scalar1=1.0, scalar2=None,
                                                op0=ALU.add)
                            elif nkt == 2:
                                v.tensor_scalar(out=lam[:, rc:rc + 1],
                                                in0=rowmaxP[:, c0:c0 + 1],
                                                scalar1=rowmaxP[:, c0 + 1:c0 + 2],
                                                scalar2=1.0,
                                                op0=ALU.max, op1=ALU.add)
                            else:
                                v.tensor_scalar(out=lam[:, rc:rc + 1],
                                                in0=rowmaxP[:, c0:c0 + 1],
                                                scalar1=rowmaxP[:, c0 + 1:c0 + 2],
                                                scalar2=rowmaxP[:, c0 + 2:c0 + 3],
                                                op0=ALU.max, op1=ALU.max)
                                if nkt == 4:
                                    v.tensor_scalar(
                                        out=lam[:, rc:rc + 1],
                                        in0=lam[:, rc:rc + 1],
                                        scalar1=rowmaxP[:, c0 + 3:c0 + 4],
                                        scalar2=1.0,
                                        op0=ALU.max, op1=ALU.add)
                                else:
                                    v.tensor_scalar(
                                        out=lam[:, rc:rc + 1],
                                        in0=lam[:, rc:rc + 1],
                                        scalar1=rowmaxP[:, c0:c0 + 1],
                                        scalar2=1.0,
                                        op0=ALU.max, op1=ALU.add)
                        v.tensor_copy(out=junk1[:, :],
                                      in_=junk1[:, :]).then_inc(uvm.h, 1)
                    PV.append(_vl_real)
                    wv = uvm.inc()

                    def _gl(g_, hb=hb, wv=wv):
                        g_.wait_ge(uvm.h, wv)
                        g_.tensor_scalar(out=lamK[:, hb:hb + 4],
                                         in0=lam[:, hb:hb + 4], scalar1=K_R,
                                         scalar2=None,
                                         op0=ALU.mult).then_inc(gup.h, 1)
                    PG.append(_gl)
                    gup_need[hf] = gup.inc()

                def emit_iter_tiles(hf):
                    half = halves[hf]
                    nkt = tiles_of(half[0])
                    PV.append(lambda v, w=gup_need[hf]: v.wait_ge(gup.h, w))
                    for kt in range(nkt):
                        for qb in half:
                            sl = slot[(qb, kt)]
                            rc = qb - qb0

                            def _vi(v, sl=sl, rc=rc, kt=kt):
                                v.tensor_scalar(out=dscr[:, :],
                                                in0=sgrp[:, sl * KT:(sl + 1) * KT],
                                                scalar1=NEG_KR,
                                                scalar2=lamK[:, rc:rc + 1],
                                                op0=ALU.mult, op1=ALU.add)
                                v._custom_dve(
                                    RECIP7_SUM,
                                    out=rstrip[:, kt * KT:(kt + 1) * KT],
                                    in0=dscr[:, :], s0=C0_R,
                                    s1=(0.0 if kt == 0 else partR[:, rc:rc + 1]),
                                    imm2=C1_R,
                                    accum_out=partR[:, rc:rc + 1])
                                v._custom_dve(
                                    SQ_SUM, out=sqw2[:, :],
                                    in0=rstrip[:, kt * KT:(kt + 1) * KT],
                                    s0=(0.0 if kt == 0 else partR2[:, rc:rc + 1]),
                                    s1=0.0,
                                    accum_out=partR2[:, rc:rc + 1])
                            PV.append(_vi)

                def emit_update(it, hf):
                    hb = hf * 4
                    sb = 8 * (it % 2)
                    db = 8 * ((it + 1) % 2)

                    def _vn(v, hb=hb):
                        v.drain()
                        v.tensor_scalar(out=nt1[:, hb:hb + 4],
                                        in0=partR[:, hb:hb + 4],
                                        scalar1=-K3, scalar2=K3,
                                        op0=ALU.add, op1=ALU.mult)
                        v.reciprocal_approx_accurate(
                            out=nt2[:, hb:hb + 4],
                            in_=partR2[:, hb:hb + 4],
                            scratch=srd[:, hb:hb + 4]).then_inc(uvm.h, 1)
                    PV.append(_vn)
                    wv = uvm.inc()

                    def _gn(g_, hb=hb, sb=sb, db=db, wv=wv):
                        g_.wait_ge(uvm.h, wv)
                        g_.tensor_tensor(out=sr2[:, hb:hb + 4],
                                         in0=nt1[:, hb:hb + 4],
                                         in1=nt2[:, hb:hb + 4], op=ALU.mult)
                        g_.tensor_tensor(out=lam[:, db + hb:db + hb + 4],
                                         in0=lam[:, sb + hb:sb + hb + 4],
                                         in1=sr2[:, hb:hb + 4], op=ALU.add)
                        g_.tensor_scalar(out=lamK[:, hb:hb + 4],
                                         in0=lam[:, db + hb:db + hb + 4],
                                         scalar1=K_R, scalar2=None,
                                         op0=ALU.mult).then_inc(gup.h, 1)
                    PG.append(_gn)
                    gup_need[hf] = gup.inc()

                def emit_f1_row(qb):
                    rc = qb - qb0
                    nkt = tiles_of(qb)
                    ps = (qb % 3) * 4
                    PV.append(lambda v, w=gup_need[rc // 4]: v.wait_ge(gup.h, w))
                    for kt in range(nkt):
                        sl = slot[(qb, kt)]

                        def _vf1(v, sl=sl, rc=rc, kt=kt, ps=ps):
                            v.tensor_scalar(out=dscr[:, :],
                                            in0=sgrp[:, sl * KT:(sl + 1) * KT],
                                            scalar1=NEG_KR,
                                            scalar2=lamK[:, rc:rc + 1],
                                            op0=ALU.mult, op1=ALU.add)
                            v._custom_dve(
                                RECIP7_SUM,
                                out=rstrip[:, (ps + kt) * KT:(ps + kt + 1) * KT],
                                in0=dscr[:, :], s0=C0_R, s1=0.0, imm2=C1_R,
                                accum_out=partF[:, ps + kt:ps + kt + 1])
                        PV.append(_vf1)

                def emit_rsum(qb):
                    rc = qb - qb0
                    nkt = tiles_of(qb)
                    ps = (qb % 3) * 4
                    sl = qb % 2

                    def _vr(v, rc=rc, nkt=nkt, ps=ps, sl=sl):
                        v.drain()
                        if nkt == 1:
                            v.tensor_copy(out=rnsc[:, sl:sl + 1],
                                          in_=partF[:, ps:ps + 1])
                        elif nkt == 2:
                            v.tensor_scalar(out=rnsc[:, sl:sl + 1],
                                            in0=partF[:, ps:ps + 1],
                                            scalar1=partF[:, ps + 1:ps + 2],
                                            scalar2=None, op0=ALU.add)
                        else:
                            v.tensor_scalar(out=rnsc[:, sl:sl + 1],
                                            in0=partF[:, ps:ps + 1],
                                            scalar1=partF[:, ps + 1:ps + 2],
                                            scalar2=partF[:, ps + 2:ps + 3],
                                            op0=ALU.add, op1=ALU.add)
                            if nkt == 4:
                                v.tensor_scalar(out=rnsc[:, sl:sl + 1],
                                                in0=rnsc[:, sl:sl + 1],
                                                scalar1=partF[:, ps + 3:ps + 4],
                                                scalar2=None, op0=ALU.add)
                    PV.append(_vr)

                def emit_rrecip(qb):
                    rc = qb - qb0
                    sl = qb % 2

                    def _vr2(v, rc=rc, sl=sl):
                        v.drain()
                        v.reciprocal_approx_accurate(out=rnorm[:, rc:rc + 1],
                                                     in_=rnsc[:, sl:sl + 1],
                                                     scratch=rnsc[:, 2:3])
                    PV.append(_vr2)

                dbg_here = dbg and b == 0 and h == 2

                def dbg_dump(dst, srcap):
                    def _vm(v):
                        v.tensor_copy(out=junk1[:, :],
                                      in_=junk1[:, :]).then_inc(dbgv.h, 1)
                    PV.append(_vm)
                    wv = dbgv.inc()

                    def _gd(g_, dst=dst, srcap=srcap, wv=wv):
                        g_.wait_ge(dbgv.h, wv)
                        g_.dma_start(out=dst, in_=srcap).then_inc(dbgs.h, 16)
                    PG.append(_gd)
                    wd = dbgs.inc(16)
                    PV.append(lambda v, wd=wd: v.wait_ge(dbgs.h, wd))

                if dbg_here:
                    ntot = sum(tiles_of(q) for q in rows)
                    dbg_dump(dbg_sgrp[g, :, 0:ntot * KT], sgrp[:, 0:ntot * KT])
                    dbg_dump(dbg_rmax[g, :, :], rowmaxP[:, :])
                emit_lambda_init(0)
                for i in range(4):
                    emit_iter_tiles(0)
                    if i == 0:
                        emit_lambda_init(1)
                    else:
                        emit_update(i - 1, 1)
                    emit_iter_tiles(1)
                    emit_update(i, 0)


                def emit_f2_row(qb):
                    rc = qb - qb0
                    nkt = tiles_of(qb)
                    ps = (qb % 3) * 4
                    w_opv = None
                    for kt in range(nkt):
                        pslot = state["p_tile"] % 2

                        def _vf2(v, kt=kt, pslot=pslot, wtp=tpd_slot[pslot],
                                 rc=rc, ps=ps):
                            if wtp:
                                v.wait_ge(tpd.h, wtp)
                            v.tensor_scalar(out=pbf[pslot][:, :],
                                            in0=rstrip[:, (ps + kt) * KT:
                                                       (ps + kt + 1) * KT],
                                            scalar1=rnorm[:, rc:rc + 1],
                                            scalar2=None,
                                            op0=ALU.mult).then_inc(pr.h, 1)
                        PV.append(_vf2)
                        w_pr = pr.inc()

                        def _ttp(t, pslot=pslot, w=w_pr, wpe=pte_bank[pslot]):
                            t.wait_ge(pr.h, w)
                            if wpe:
                                t.wait_ge(pte.h, wpe)
                            for c in range(4):
                                ins = t.transpose(
                                    ps_pt[pslot][:, c * 128:(c + 1) * 128],
                                    pbf[pslot][:, c * 128:(c + 1) * 128],
                                    ident[:, :])
                            ins.then_inc(tpd.h, 1)
                        PT.append(_ttp)
                        tpd_slot[pslot] = tpd.inc()

                        def _spt(s, pslot=pslot, w=tpd_slot[pslot],
                                 wpv=pvd_slot[pslot]):
                            s.wait_ge(tpd.h, w)
                            if wpv:
                                s.wait_ge(pvd.h, wpv)
                            s.copy(out=ptsb[pslot][:, :],
                                   in_=ps_pt[pslot][:, :]).then_inc(pte.h, 1)
                        PS.append(_spt)
                        pte_bank[pslot] = pte.inc()

                        obank = qb % 2

                        def _tpv(t, pslot=pslot, w=pte_bank[pslot], kt=kt,
                                 par=par, obank=obank, first=(kt == 0),
                                 last=(kt == nkt - 1),
                                 wo_free=oev_row[obank]):
                            t.wait_ge(pte.h, w)
                            if first and wo_free:
                                t.wait_ge(oev.h, wo_free)
                            for c in range(4):
                                tc = kt * 4 + c
                                ins = t.matmul(
                                    ps_o[obank][:, :],
                                    vpair[:, tc * 128 + 64 * par:
                                          tc * 128 + 64 * par + 64],
                                    ptsb[pslot][:, c * 128:(c + 1) * 128],
                                    start=(first and c == 0),
                                    stop=(last and c == 3))
                            ins.then_inc(pvd.h, 1)
                        PT.append(_tpv)
                        pvd_slot[pslot] = pvd.inc()
                        if kt == nkt - 1:
                            w_opv = pvd.n
                        state["p_tile"] += 1

                    def _soev(s, qb=qb, w=w_opv, obank=qb % 2, oth_buf=oth_buf):
                        s.wait_ge(pvd.h, w)
                        s.activation(out=othead[oth_buf][:, qb * 128:(qb + 1) * 128],
                                     in_=ps_o[obank][:, :], func=ACT.Copy,
                                     bias=0.0, scale=1.0).then_inc(oev.h, 1)
                    PS.append(_soev)
                    oev_row[qb % 2] = oev.inc()

                emit_f1_row(rows[0])
                for j in range(1, 8):
                    if j == 4:
                        emit_update(3, 1)
                        if dbg_here:
                            dbg_dump(dbg_lam[g, :, :], lam[:, 0:8])
                    emit_f1_row(rows[j])
                    emit_rsum(rows[j - 1])
                    emit_rrecip(rows[j - 1])
                    if j >= 2:
                        emit_f2_row(rows[j - 2])
                PV.append(lambda v: v.tensor_scalar(
                    out=dscr[:, :], in0=dscr[:, :], scalar1=1.0,
                    scalar2=None, op0=ALU.mult))
                emit_rsum(rows[7])
                emit_rrecip(rows[7])
                emit_f2_row(rows[6])
                emit_f2_row(rows[7])
                if dbg_here:
                    dbg_dump(dbg_rn[g, :, :], rnorm[:, :])

            bh = 0
            for b in range(NB):
                for h in range(H):
                    if h % 2 == 0:
                        qkv_pair(b, h // 2)
                        if dbg and b == 0 and h == 2:
                            def _gdq(g_, w=qev.n):
                                g_.wait_ge(qev.h, w)
                                g_.dma_start(out=dbg_q[:, :],
                                             in_=qpair[:, :]).then_inc(dbgs.h, 16)
                                g_.dma_start(out=dbg_k[:, :],
                                             in_=kpair[:, :]).then_inc(dbgs.h, 16)
                                g_.dma_start(out=dbg_v[:, :],
                                             in_=vpair[:, :]).then_inc(dbgs.h, 16)
                            PG.append(_gdq)
                            dbgs.inc(48)
                    w_pair = qev.n
                    oth_buf = bh % 2
                    if odma_buf[oth_buf]:
                        def _sw(s, w=odma_buf[oth_buf]):
                            s.wait_ge(odma.h, w)
                        PS.append(_sw)
                    for g in range(2):
                        attn_group(b, h, g, oth_buf, w_pair)
                    w_all_oev = oev.n

                    def _sdma(s, b=b, h=h, oth_buf=oth_buf, w=w_all_oev):
                        s.wait_ge(oev.h, w)
                        s.dma_start(out=oT_sc[b, h, :, :],
                                    in_=othead[oth_buf][:, :]).then_inc(odma.h, 16)
                    PS.append(_sdma)
                    odma_buf[oth_buf] = odma.inc(16)
                    bh += 1

            PS.append(lambda s: s.wait_ge(odma.h, odma.n))
            if dbg:
                PY.append(lambda y: y.wait_ge(dbgs.h, dbgs.n))
            _run_plans(nc, PT, PV, PS, PY, PG)

        if phases < 3:
            return _finish(nc)
        with ExitStack() as p3:
            e3 = p3.enter_context
            woT = e3(nc.sbuf_tensor("woT", [128, 8 * D], BF16))
            wdT = e3(nc.sbuf_tensor("wdT", [128, 24 * D], BF16))
            oT_ld = [e3(nc.sbuf_tensor(f"oT_ld{i}", [128, 8 * KT], BF16))
                     for i in range(2)]
            wgld = [e3(nc.sbuf_tensor(f"wgld{i}", [128, 8 * 128], BF16))
                    for i in range(2)]
            wuld = [e3(nc.sbuf_tensor(f"wuld{i}", [128, 8 * 128], BF16))
                    for i in range(2)]
            hbuf = e3(nc.sbuf_tensor("hbuf", [128, 24 * KT], BF16))
            sgate = [e3(nc.sbuf_tensor(f"sgate{i}", [128, KT], F32))
                     for i in range(2)]
            gbuf = [e3(nc.sbuf_tensor(f"gbuf{i}", [128, D], F32))
                    for i in range(2)]
            sgbuf = e3(nc.sbuf_tensor("sgbuf", [128, D], F32))
            xres = [e3(nc.sbuf_tensor(f"xres{i}", [128, D], F32)) for i in range(2)]
            outb = [e3(nc.sbuf_tensor(f"outb{i}", [128, D], F32)) for i in range(2)]

            ps_g = [e3(nc.psum_tensor(f"ps_g{i}", [128, KT], F32)) for i in range(2)]
            ps_u = [e3(nc.psum_tensor(f"ps_u{i}", [128, KT], F32)) for i in range(2)]
            ps_wo = e3(nc.psum_tensor("ps_wo", [128, D], F32))
            ps_dn = e3(nc.psum_tensor("ps_dn", [128, D], F32))

            ld, gmm, hdone, sgd = (S3["p3_ld"], S3["p3_gmm"], S3["p3_hdone"],
                                   S3["p3_sgd"])
            womm, dnmm, gsil, vout, st = (S3["p3_womm"], S3["p3_dnmm"],
                                          S3["p3_gsil"], S3["p3_vout"],
                                          S3["p3_st"])
            PT, PV, PS, PY = [], [], [], []

            def _lw(y):
                for c in range(8):
                    y.dma_start(out=woT[:, c * D:(c + 1) * D],
                                in_=woT_sc[c * 128:(c + 1) * 128, :]).then_inc(ld.h, 16)
                for c in range(24):
                    y.dma_start(out=wdT[:, c * D:(c + 1) * D],
                                in_=wdT_sc[c * 128:(c + 1) * 128, :]).then_inc(ld.h, 16)
            PY.append(_lw)
            w_w = ld.inc(16 * 32)

            wg_free = {0: 0, 1: 0}
            oT_free = {0: 0, 1: 0}
            x_free = {0: 0, 1: 0}
            out_free = {0: 0, 1: 0}
            h_bank = {0: 0, 1: 0}
            sg_free = {0: 0, 1: 0}
            g_free = {0: 0, 1: 0}
            wo_prev = 0
            dn_prev = 0

            nspan = NB * N // KT
            subt = 0
            for sp in range(nspan):
                b, t0 = sp // (N // KT), (sp % (N // KT)) * KT
                obuf = sp % 2
                if oT_free[obuf]:
                    PY.append(lambda y, w=oT_free[obuf]: y.wait_ge(womm.h, w))

                def _lo(y, b=b, t0=t0, obuf=obuf):
                    for c in range(8):
                        y.dma_start(out=oT_ld[obuf][0:64, c * KT:(c + 1) * KT],
                                    in_=oT_sc[b, 2 * c, :, t0:t0 + KT]
                                    ).then_inc(ld.h, 16)
                        y.dma_start(out=oT_ld[obuf][64:128, c * KT:(c + 1) * KT],
                                    in_=oT_sc[b, 2 * c + 1, :, t0:t0 + KT]
                                    ).then_inc(ld.h, 16)
                PY.append(_lo)
                w_oT = ld.inc(16 * 16)

                for ht in range(24):
                    wbuf = ht % 2
                    if wg_free[wbuf]:
                        PY.append(lambda y, w=wg_free[wbuf]: y.wait_ge(gmm.h, w))

                    def _lwg(y, ht=ht, wbuf=wbuf):
                        ap_g = bass.AP(wgT_sc, ht * 128,
                                       [[HID, 128], [128 * HID, 8], [1, 128]])
                        ap_u = bass.AP(wuT_sc, ht * 128,
                                       [[HID, 128], [128 * HID, 8], [1, 128]])
                        y.dma_start(out=wgld[wbuf][:, :], in_=ap_g).then_inc(ld.h, 16)
                        y.dma_start(out=wuld[wbuf][:, :], in_=ap_u).then_inc(ld.h, 16)
                    PY.append(_lwg)
                    w_wg = ld.inc(32)
                    pbank = ht % 2

                    def _tg(t, wbuf=wbuf, b=b, t0=t0, w=w_wg, pbank=pbank,
                            whd=h_bank[pbank]):
                        t.wait_ge(ld.h, w)
                        if whd:
                            t.wait_ge(hdone.h, whd)
                        for c in range(8):
                            t.matmul(ps_g[pbank][:, :],
                                     wgld[wbuf][:, c * 128:(c + 1) * 128],
                                     xn[:, c * NB * N + b * N + t0:
                                        c * NB * N + b * N + t0 + KT],
                                     start=(c == 0), stop=(c == 7))
                        for c in range(8):
                            ins = t.matmul(ps_u[pbank][:, :],
                                           wuld[wbuf][:, c * 128:(c + 1) * 128],
                                           xn[:, c * NB * N + b * N + t0:
                                              c * NB * N + b * N + t0 + KT],
                                           start=(c == 0), stop=(c == 7))
                        ins.then_inc(gmm.h, 1)
                    PT.append(_tg)
                    wg_free[wbuf] = gmm.inc()

                    def _ssil(s, w=wg_free[wbuf], pbank=pbank,
                              wsg=sg_free[pbank]):
                        s.wait_ge(gmm.h, w)
                        if wsg:
                            s.wait_ge(hdone.h, wsg)
                        s.activation(out=sgate[pbank][:, :], in_=ps_g[pbank][:, :],
                                     func=ACT.Silu, bias=0.0,
                                     scale=1.0).then_inc(sgd.h, 1)
                    PS.append(_ssil)
                    w_sg = sgd.inc()

                    def _vh(v, ht=ht, w=w_sg, pbank=pbank):
                        v.wait_ge(sgd.h, w)
                        v.tensor_tensor(out=hbuf[:, ht * KT:(ht + 1) * KT],
                                        in0=sgate[pbank][:, :],
                                        in1=ps_u[pbank][:, :],
                                        op=ALU.mult).then_inc(hdone.h, 1)
                    PV.append(_vh)
                    h_bank[pbank] = hdone.inc()
                    sg_free[pbank] = h_bank[pbank]
                w_h_span = hdone.n

                for stt in range(4):
                    ts0 = stt * 128
                    xbuf = subt % 2
                    if x_free[xbuf]:
                        PY.append(lambda y, w=x_free[xbuf]: y.wait_ge(vout.h, w))

                    def _lx(y, b=b, t0=t0, ts0=ts0, xbuf=xbuf):
                        y.dma_start(out=xres[xbuf][:, :],
                                    in_=x_in[b, t0 + ts0:t0 + ts0 + 128, :]
                                    ).then_inc(ld.h, 16)
                    PY.append(_lx)
                    w_x = ld.inc(16)

                    def _two(t, obuf=obuf, ts0=ts0, w_oT=w_oT, wv=wo_prev):
                        t.wait_ge(ld.h, max(w_oT, w_w))
                        if wv:
                            t.wait_ge(vout.h, wv)
                        for dh2 in range(2):
                            for c in range(8):
                                ins = t.matmul(
                                    ps_wo[:, dh2 * 512:(dh2 + 1) * 512],
                                    oT_ld[obuf][:, c * KT + ts0:
                                                c * KT + ts0 + 128],
                                    woT[:, c * D + dh2 * 512:
                                        c * D + dh2 * 512 + 512],
                                    start=(c == 0), stop=(c == 7))
                        ins.then_inc(womm.h, 1)
                    PT.append(_two)
                    oT_free[obuf] = womm.inc()

                    def _tdn(t, ts0=ts0, w=w_h_span, wg=dn_prev):
                        t.wait_ge(hdone.h, w)
                        if wg:
                            t.wait_ge(gsil.h, wg)
                        for dh2 in range(2):
                            for c in range(24):
                                ins = t.matmul(
                                    ps_dn[:, dh2 * 512:(dh2 + 1) * 512],
                                    hbuf[:, c * KT + ts0:c * KT + ts0 + 128],
                                    wdT[:, c * D + dh2 * 512:
                                        c * D + dh2 * 512 + 512],
                                    start=(c == 0), stop=(c == 23))
                        ins.then_inc(dnmm.h, 1)
                    PT.append(_tdn)
                    w_dn = dnmm.inc()

                    gb = subt % 2

                    def _sgl(s, w=w_dn, gb=gb, wgf=g_free[gb]):
                        s.wait_ge(dnmm.h, w)
                        if wgf:
                            s.wait_ge(vout.h, wgf)
                        s.activation(out=gbuf[gb][:, :], in_=ps_dn[:, :],
                                     func=ACT.Silu, bias=0.0,
                                     scale=1.0).then_inc(gsil.h, 1)
                    PS.append(_sgl)
                    w_gs = gsil.inc()
                    dn_prev = w_gs

                    def _vo(v, xbuf=xbuf, w_gs=w_gs, w_wo=oT_free[obuf],
                            w_x=w_x, wst=out_free[xbuf], gb=gb):
                        v.wait_ge(gsil.h, w_gs)
                        v.wait_ge(womm.h, w_wo)
                        v.wait_ge(ld.h, w_x)
                        if wst:
                            v.wait_ge(st.h, wst)
                        v.tensor_tensor(out=sgbuf[:, :], in0=ps_wo[:, :],
                                        in1=gbuf[gb][:, :], op=ALU.mult)
                        v.tensor_tensor(out=outb[xbuf][:, :], in0=sgbuf[:, :],
                                        in1=xres[xbuf][:, :],
                                        op=ALU.add).then_inc(vout.h, 1)
                    PV.append(_vo)
                    x_free[xbuf] = vout.inc()
                    wo_prev = x_free[xbuf]
                    g_free[gb] = x_free[xbuf]

                    def _so(y, b=b, t0=t0, ts0=ts0, xbuf=xbuf, w=x_free[xbuf]):
                        y.wait_ge(vout.h, w)
                        y.dma_start(out=out_d[b, t0 + ts0:t0 + ts0 + 128, :],
                                    in_=outb[xbuf][:, :]).then_inc(st.h, 16)
                    PY.append(_so)
                    out_free[xbuf] = st.inc(16)
                    subt += 1

            PY.append(lambda y: y.wait_ge(st.h, st.n))
            _run_plans(nc, PT, PV, PS, PY)

    return _finish(nc)


def _finish(nc):
    mybir.codegen_inst_isa_subclasses(nc)
    return nc


_NC_CACHE = {}


def _get_nc():
    if "nc" not in _NC_CACHE:
        _NC_CACHE["nc"] = build()
    return _NC_CACHE["nc"]


def kernel(L0_attn_x, L0_attn_Wqkv, L0_attn_Wo, L0_ffn_w_gate, L0_ffn_w_up,
           L0_ffn_w_down):
    x = np.ascontiguousarray(np.asarray(L0_attn_x, dtype=np.float32))
    wqkvv = np.ascontiguousarray(np.asarray(L0_attn_Wqkv, dtype=np.float32))
    wov = np.ascontiguousarray(np.asarray(L0_attn_Wo, dtype=np.float32))
    wgv = np.ascontiguousarray(np.asarray(L0_ffn_w_gate, dtype=np.int32))
    wuv = np.ascontiguousarray(np.asarray(L0_ffn_w_up, dtype=np.int32))
    wdv = np.ascontiguousarray(np.asarray(L0_ffn_w_down, dtype=np.int32))
    ident, identf, lens = _host_consts()

    nc = _get_nc()
    in_maps = []
    for c in range(8):
        in_maps.append({
            "x": x[c * NB:(c + 1) * NB],
            "wqkv": wqkvv, "wo": wov, "wg": wgv, "wu": wuv, "wd": wdv,
            "ident": ident, "identf": identf, "lens": lens,
        })
    res = run_bass_kernel_spmd(nc, in_maps, core_ids=list(range(8)))
    out = np.concatenate([res.results[c]["out"] for c in range(8)], axis=0)
    return out


# version 03
# speedup vs baseline: 1.0777x; 1.0777x over previous
import numpy as np
import ml_dtypes
from contextlib import ExitStack
from operator import add as _op_add

import concourse.bass as bass
import concourse.mybir as mybir
from concourse.bass_utils import run_bass_kernel_spmd

import concourse.dve_ops as dvo
from concourse.dve_op import DveOp
from concourse.dve_spec import Spec, AluOp, Bin, Src0, C0, C1, Zero, lower, sq
from concourse.dve_ops_raw import _compute_dtype

F32 = mybir.dt.float32
BF16 = mybir.dt.bfloat16
I32 = mybir.dt.int32
AX = mybir.AxisListType.X
ALU = mybir.AluOpType
ACT = mybir.ActivationFunctionType

NB = 2
N = 2048
D = 1024
H = 16
DH = 64
HID = 3072
NQB = N // 128
KT = 512
QPRE = float(2.0 ** 27)
SPOST = float(2.0 ** -30)

K_R = -4.25
K2 = float(np.float32(K_R * K_R))
C0_R = float(np.float32(2.002 * K_R))
C1_R = float(np.float32(2.0 * K_R * K_R))
K3 = float(np.float64(K_R) ** 3)
NEG_KR = float(-K_R)


def _ref_recip7(in0, in1, c0, c1, c2):
    f = _compute_dtype(in0)
    seed = np.asarray(c1, f).reshape(-1, 1) if hasattr(c1, "reshape") else c1
    if f is np.float64:
        out = np.float64(K3 * K_R) / in0
        return out, seed + out.reshape(out.shape[0], -1).sum(
            axis=-1, keepdims=True)
    x = np.ascontiguousarray(in0, dtype=np.float32)
    n = (~x.view(np.int32)).view(np.float32)
    z1 = (n * (np.float32(c0) - x * n).astype(np.float32)).astype(np.float32)
    out = (z1 * (np.float32(c2) - x * z1).astype(np.float32)).astype(np.float32)
    return out, seed + out.reshape(out.shape[0], -1).astype(np.float32).sum(
        axis=-1, keepdims=True)


def _ref_sq_sum(in0, in1, c0, c1, c2):
    f = _compute_dtype(in0)
    seed = np.asarray(c0, f).reshape(-1, 1) if hasattr(c0, "reshape") else c0
    b = np.square(in0.astype(f))
    return b, seed + b.reshape(b.shape[0], -1).sum(axis=-1, keepdims=True)


def _mk_ops():
    from concourse.dve_spec import C2
    n = Bin(AluOp.BITWISE_NOT, Src0, Src0)
    z1 = n * (C0 - Src0 * n)
    body = z1 * (C2 - Src0 * z1)
    spec_r = Spec(body=body, accum=_op_add, accum_init=C1)
    sha_r = lower(spec_r, ver="v3", perf_en=False, name="RECIP7_SUM").sha("v3")
    op_r = DveOp("RECIP7_SUM", spec_r, subdim=False, uops_sha={"v3": sha_r},
                 reference=_ref_recip7)
    spec_s = Spec(body=sq(Src0), accum=_op_add, accum_init=C0)
    sha_s = lower(spec_s, ver="v3", perf_en=False, name="SQ_SUM").sha("v3")
    op_s = DveOp("SQ_SUM", spec_s, subdim=False, uops_sha={"v3": sha_s},
                 reference=_ref_sq_sum)
    return op_r, op_s


def _register(op):
    if op.name in dvo._SUB_OPCODE_FOR_NAME:
        return
    dvo.OPS.append(op)
    dvo._SUB_OPCODE_FOR_NAME[op.name] = max(dvo._SUB_OPCODE_FOR_NAME.values()) + 1
    assert dvo._SUB_OPCODE_FOR_NAME[op.name] < 0x20
    dvo.CUSTOM_DVE_SPECS[op.name] = op.spec
    dvo.CUSTOM_DVE_REFS[op.name] = op.ref


RECIP7_SUM, SQ_SUM = _mk_ops()
_register(RECIP7_SUM)
_register(SQ_SUM)


def _host_consts():
    ident = np.eye(128).astype(ml_dtypes.bfloat16)
    identf = np.eye(128).astype(np.float32)
    lens = np.zeros((128, NQB * 4 + 2), dtype=np.float32)
    p = np.arange(128)
    for qb in range(NQB):
        for kt in range(4):
            lens[:, qb * 4 + kt] = np.clip(128 * qb + p + 1 - KT * kt, 0, KT)
    lens[:, NQB * 4] = 1e-5
    lens[:, NQB * 4 + 1] = -K3
    return ident, identf, lens


class Sem:
    __slots__ = ("h", "n")

    def __init__(self, handle):
        self.h = handle
        self.n = 0

    def inc(self, k=1):
        self.n += k
        return self.n


def tiles_of(qb):
    return qb // 4 + 1


def _run_plans(nc, PT, PV, PS, PY, PG=None):
    with nc.Block() as blk:
        blk.tensor(lambda t: [f(t) for f in PT])
        blk.vector(lambda v: [f(v) for f in PV])
        blk.scalar(lambda s: [f(s) for f in PS])
        blk.sync(lambda y: [f(y) for f in PY])
        if PG:
            blk.gpsimd(lambda g_: [f(g_) for f in PG])


def build(dbg=False, phases=3):
    nc = bass.Bass("TRN2", target_bir_lowering=False,
                   dynamic_dma_scratch_size=2048)

    x_in = nc.dram_tensor("x", [NB, N, D], F32, kind="ExternalInput")
    wqkv = nc.dram_tensor("wqkv", [3 * D, D], F32, kind="ExternalInput")
    wo = nc.dram_tensor("wo", [D, D], F32, kind="ExternalInput")
    wg = nc.dram_tensor("wg", [HID, D], I32, kind="ExternalInput")
    wu = nc.dram_tensor("wu", [HID, D], I32, kind="ExternalInput")
    wd = nc.dram_tensor("wd", [D, HID], I32, kind="ExternalInput")
    ident_d = nc.dram_tensor("ident", [128, 128], BF16, kind="ExternalInput")
    identf_d = nc.dram_tensor("identf", [128, 128], F32, kind="ExternalInput")
    lens_d = nc.dram_tensor("lens", [128, NQB * 4 + 2], F32, kind="ExternalInput")
    out_d = nc.dram_tensor("out", [NB, N, D], F32, kind="ExternalOutput")

    dk = "ExternalOutput" if dbg else "Internal"
    oT_sc = nc.dram_tensor("oT_sc", [NB, H, DH, N], BF16, kind=dk)
    wqkvT_sc = nc.dram_tensor("wqkvT_sc", [D, 3 * D], BF16, kind=dk)
    woT_sc = nc.dram_tensor("woT_sc", [D, D], BF16, kind=dk)
    wgT_sc = nc.dram_tensor("wgT_sc", [D, HID], BF16, kind=dk)
    wuT_sc = nc.dram_tensor("wuT_sc", [D, HID], BF16, kind=dk)
    wdT_sc = nc.dram_tensor("wdT_sc", [HID, D], BF16, kind=dk)
    if dbg:
        xn_dbg = nc.dram_tensor("xn_dbg", [128, 8 * NB * N], BF16,
                                kind="ExternalOutput")
        dbg_st_d = nc.dram_tensor("dbg_st", [128, 32 * 8], F32,
                                  kind="ExternalOutput")
        dbg_sgrp = nc.dram_tensor("dbg_sgrp", [2, 128, 28 * KT], F32,
                                  kind="ExternalOutput")
        dbg_rmax = nc.dram_tensor("dbg_rmax", [2, 128, 32], F32,
                                  kind="ExternalOutput")
        dbg_lam = nc.dram_tensor("dbg_lam", [2, 128, 8], F32,
                                 kind="ExternalOutput")
        dbg_rn = nc.dram_tensor("dbg_rn", [2, 128, 8], F32,
                                kind="ExternalOutput")
        dbg_q = nc.dram_tensor("dbg_q", [128, N], BF16, kind="ExternalOutput")
        dbg_it = nc.dram_tensor("dbg_it", [2, 6, 128, 16], F32,
                                kind="ExternalOutput")
        dbg_p = nc.dram_tensor("dbg_p", [128, KT], BF16, kind="ExternalOutput")
        dbg_pt = nc.dram_tensor("dbg_pt", [128, KT], BF16, kind="ExternalOutput")
        dbg_pp = nc.dram_tensor("dbg_pp", [2, 6, 128, 16], F32,
                                kind="ExternalOutput")
        dbg_k = nc.dram_tensor("dbg_k", [128, N], BF16, kind="ExternalOutput")
        dbg_v = nc.dram_tensor("dbg_v", [128, N], BF16, kind="ExternalOutput")

    with ExitStack() as outer:
        oe = outer.enter_context
        xn = oe(nc.sbuf_tensor("xn", [128, 8 * NB * N], BF16))
        ident = oe(nc.sbuf_tensor("ident_sb", [128, 128], BF16))
        identf = oe(nc.sbuf_tensor("identf_sb", [128, 128], F32))
        lens_sb = oe(nc.sbuf_tensor("lens_sb", [128, NQB * 4 + 2], F32))

        def mk_sems(names):
            return {k: Sem(oe(nc.semaphore(k))) for k in names}

        S1 = mk_sems(["p1_ld", "p1_v", "p1_s", "p1_t", "p1_st"])
        S2 = mk_sems(["p2_ld", "p2_qmm", "p2_qev", "p2_smm", "p2_sfree",
                      "p2_pr", "p2_tpd", "p2_pte", "p2_pvd", "p2_opv",
                      "p2_oev", "p2_odma"])
        S3 = mk_sems(["p3_ld", "p3_gmm", "p3_hdone", "p3_sgd", "p3_womm",
                      "p3_dnmm", "p3_gsil", "p3_vout", "p3_st"])

        with ExitStack() as p1:
            e1 = p1.enter_context
            stg_f = [e1(nc.sbuf_tensor(f"stg_f{i}", [128, D], F32)) for i in range(2)]
            stg_i = [e1(nc.sbuf_tensor(f"stg_i{i}", [128, HID], I32)) for i in range(2)]
            wcast = [e1(nc.sbuf_tensor(f"wcast{i}", [128, HID], BF16)) for i in range(2)]
            wstage = [e1(nc.sbuf_tensor(f"wstage{i}", [128, 1024], BF16)) for i in range(2)]
            xt = [e1(nc.sbuf_tensor(f"xt{i}", [128, D], F32)) for i in range(8)]
            xn_tok = e1(nc.sbuf_tensor("xn_tok", [128, D], BF16))
            sqw = e1(nc.sbuf_tensor("sqw1", [128, D], F32))
            stat = e1(nc.sbuf_tensor("stat", [128, 64], F32))
            dbg_sb = e1(nc.sbuf_tensor("dbg_sb", [128, 32 * 8], F32)) if dbg else None
            xbb = [e1(nc.sbuf_tensor(f"xbb{i}", [128, D], BF16)) for i in range(2)]
            xfmb = [e1(nc.sbuf_tensor(f"xfmb{i}", [128, D], BF16)) for i in range(2)]
            x2fmb = [e1(nc.sbuf_tensor(f"x2fmb{i}", [128, D], BF16)) for i in range(2)]
            strow = [e1(nc.sbuf_tensor(f"strow{i}", [1, 256], F32)) for i in range(2)]
            onesb = e1(nc.sbuf_tensor("onesb", [128, 2], BF16))
            ps_t = [e1(nc.psum_tensor(f"ps_t{i}", [128, 1024], BF16)) for i in range(2)]
            ps_t2 = [e1(nc.psum_tensor(f"ps_t2{i}", [128, 1024], BF16)) for i in range(2)]
            ps_st = [e1(nc.psum_tensor(f"ps_st{i}", [2, 256], F32)) for i in range(2)]
            ps_tt = [e1(nc.psum_tensor(f"ps_tt{i}", [128, 2], F32)) for i in range(2)]

            ld, vv, ss, tt, st = (S1["p1_ld"], S1["p1_v"], S1["p1_s"],
                                  S1["p1_t"], S1["p1_st"])
            PT, PV, PS, PY = [], [], [], []

            PY.append(lambda y: y.dma_start(out=ident[:, :],
                                            in_=ident_d[:, :]).then_inc(ld.h, 16))
            ld.inc(16)
            PY.append(lambda y: y.dma_start(out=lens_sb[:, :],
                                            in_=lens_d[:, :]).then_inc(ld.h, 16))
            ld.inc(16)
            PY.append(lambda y: y.dma_start(out=identf[:, :],
                                            in_=identf_d[:, :]).then_inc(ld.h, 16))
            ld.inc(16)
            PT.append(lambda t: t.wait_ge(ld.h, 48))

            def _swarm(s):
                s.wait_ge(ld.h, 48)
                s.memzero(sqw[:, :])
                s.activation(out=sqw[:, :], in_=sqw[:, :], func=ACT.Copy,
                             bias=0.0, scale=1.0)
                s.activation(out=sqw[:, :], in_=sqw[:, :], func=ACT.Square,
                             bias=0.0, scale=1.0)
                s.activation(out=sqw[:, :], in_=sqw[:, :], func=ACT.Exp,
                             bias=0.0, scale=0.0)
                s.activation(out=sqw[:, :], in_=sqw[:, :], func=ACT.Ln,
                             bias=lens_sb[:, NQB * 4:NQB * 4 + 1], scale=0.0)
                s.activation(out=sqw[:, :], in_=sqw[:, :], func=ACT.Exp,
                             bias=0.0, scale=0.0)
                s.memzero(sqw[:, :])
            PS.append(_swarm)

            def _vmo(v):
                v.memset(onesb[:, 0:1], 1.0)
                v.memset(onesb[:, 1:2], 0.0).then_inc(vv.h, 1)
            PV.append(_vmo)
            vv.inc()
            PT.append(lambda t: t.wait_ge(vv.h, 1))
            def _twarm(t):
                t.wait_ge(vv.h, 1)
                for pb in range(2):
                    for c in range(4):
                        t.matmul(ps_st[pb][0:2, 0:128], onesb[:, :],
                                 ident[:, 0:128], start=(c == 0), stop=(c == 3))
                    for c in range(4):
                        t.matmul(ps_st[pb][0:2, 128:256], onesb[:, :],
                                 ident[:, 0:128], start=(c == 0), stop=(c == 3))
                    t.transpose(ps_t[pb][:, 0:128], ident[:, :], ident[:, :])
                    t.transpose(ps_t2[pb][:, 0:128], ident[:, :], ident[:, :])
                    t.transpose(ps_tt[pb][:, 0:1], identf[0:1, 0:128],
                                identf[0:1, 0:1])
                ins = t.matmul(ps_st[0][0:2, 0:128], onesb[:, :],
                               ident[:, 0:128], start=True, stop=True)
                ins.then_inc(tt.h, 1)
            PT.append(_twarm)
            tt.inc()
            PV.append(lambda v: v.wait_ge(tt.h, 1))

            v_done = {i: 0 for i in range(8)}
            xb_free = {0: 0, 1: 0}
            s_evacA = {0: 0, 1: 0}
            s_strow = {0: 0, 1: 0}
            s0_done = {0: 0, 1: 0}
            stat_free = {i: 0 for i in range(8)}
            s_evac2 = {0: 0, 1: 0}
            xn_free = [0]
            ntile = NB * N // 128
            w_s3_of = {}

            def ln_front(j):
                b, t0 = j // (N // 128), (j % (N // 128)) * 128
                buf = j % 8
                pb = j % 2
                base = (j % 8) * 8
                if v_done[buf]:
                    PY.append(lambda y, w=v_done[buf]: y.wait_ge(vv.h, w))
                PY.append(lambda y, b=b, t0=t0, buf=buf: y.dma_start(
                    out=xt[buf][:, :],
                    in_=x_in[b, t0:t0 + 128, :]).then_inc(ld.h, 16))
                myld = ld.inc(16)

                def _vc(v, buf=buf, pb=pb, myld=myld, wtf=xb_free[pb]):
                    v.wait_ge(ld.h, myld)
                    if wtf:
                        v.wait_ge(tt.h, wtf)
                    v.tensor_copy(out=xbb[pb][:, :],
                                  in_=xt[buf][:, :]).then_inc(vv.h, 1)
                PV.append(_vc)
                w_cast = vv.inc()

                def _t1(t, pb=pb, w=w_cast, wse=s_evacA[pb]):
                    t.wait_ge(vv.h, w)
                    if wse:
                        t.wait_ge(ss.h, wse)
                    for c in range(8):
                        ins = t.transpose(ps_t[pb][:, c * 128:(c + 1) * 128],
                                          xbb[pb][:, c * 128:(c + 1) * 128],
                                          ident[:, :])
                    ins.then_inc(tt.h, 1)
                PT.append(_t1)
                w_t1 = tt.inc()
                xb_free[pb] = w_t1

                def _s1(s, pb=pb, w=w_t1):
                    s.wait_ge(tt.h, w)
                    s.copy(out=xfmb[pb][:, :], in_=ps_t[pb][:, :])
                    s.activation(out=x2fmb[pb][:, :], in_=xfmb[pb][:, :],
                                 func=ACT.Square, bias=0.0,
                                 scale=1.0).then_inc(ss.h, 1)
                PS.append(_s1)
                w_s1 = ss.inc()
                s_evacA[pb] = w_s1

                def _t2(t, pb=pb, w=w_s1, wsr=s_strow[pb]):
                    t.wait_ge(ss.h, w)
                    if wsr:
                        t.wait_ge(ss.h, wsr)
                    for c in range(8):
                        t.matmul(ps_st[pb][0:2, 0:128], onesb[:, :],
                                 xfmb[pb][:, c * 128:(c + 1) * 128],
                                 start=(c == 0), stop=(c == 7))
                    for c in range(8):
                        ins = t.matmul(ps_st[pb][0:2, 128:256], onesb[:, :],
                                       x2fmb[pb][:, c * 128:(c + 1) * 128],
                                       start=(c == 0), stop=(c == 7))
                    ins.then_inc(tt.h, 1)
                PT.append(_t2)
                w_t2 = tt.inc()

                def _s2(s, pb=pb, w=w_t2):
                    s.wait_ge(tt.h, w)
                    s.copy(out=strow[pb][0:1, :],
                           in_=ps_st[pb][0:1, :]).then_inc(ss.h, 1)
                PS.append(_s2)
                w_s2 = ss.inc()
                s_strow[pb] = w_s2

                def _t3(t, pb=pb, w=w_s2, wst=s0_done[pb]):
                    t.wait_ge(ss.h, w)
                    if wst:
                        t.wait_ge(ss.h, wst)
                    t.transpose(ps_tt[pb][:, 0:1], strow[pb][0:1, 0:128],
                                identf[0:1, 0:1])
                    t.transpose(ps_tt[pb][:, 1:2], strow[pb][0:1, 128:256],
                                identf[0:1, 0:1]).then_inc(tt.h, 1)
                PT.append(_t3)
                w_t3 = tt.inc()

                def _s0(s, pb=pb, w=w_t3, base=base, wsf=stat_free[j % 8]):
                    s.wait_ge(tt.h, w)
                    if wsf:
                        s.wait_ge(vv.h, wsf)
                    s.copy(out=stat[:, base + 0:base + 2],
                           in_=ps_tt[pb][:, 0:2]).then_inc(ss.h, 1)
                PS.append(_s0)
                s0_done[pb] = ss.inc()

            def ln_stage(j, k):
                base = (j % 8) * 8

                def _sk(s, base=base, k=k, j=j):
                    if k == 1:
                        s.activation(out=stat[:, base + 2:base + 3],
                                     in_=stat[:, base + 0:base + 1],
                                     func=ACT.Copy, bias=0.0, scale=1.0 / D)
                    elif k == 2:
                        s.activation(out=stat[:, base + 3:base + 4],
                                     in_=stat[:, base + 2:base + 3],
                                     func=ACT.Square, bias=0.0, scale=1.0)
                    elif k == 3:
                        s.activation(out=stat[:, base + 4:base + 5],
                                     in_=stat[:, base + 3:base + 4],
                                     func=ACT.Copy, bias=0.0, scale=-1.0)
                    elif k == 4:
                        s.activation(out=stat[:, base + 5:base + 6],
                                     in_=stat[:, base + 1:base + 2],
                                     func=ACT.Identity,
                                     bias=stat[:, base + 4:base + 5],
                                     scale=1.0 / D)
                    elif k == 5:
                        s.activation(out=stat[:, base + 6:base + 7],
                                     in_=stat[:, base + 5:base + 6], func=ACT.Ln,
                                     bias=lens_sb[:, NQB * 4:NQB * 4 + 1],
                                     scale=1.0)
                    else:
                        ins = s.activation(out=stat[:, base + 7:base + 8],
                                           in_=stat[:, base + 6:base + 7],
                                           func=ACT.Exp, bias=0.0, scale=-0.5)
                        if dbg:
                            s.copy(out=dbg_sb[:, j * 8:(j + 1) * 8],
                                   in_=stat[:, base:base + 8])
                        ins.then_inc(ss.h, 1)
                PS.append(_sk)
                if k == 6:
                    w_s3_of[j] = ss.inc()

            def ln_back(j):
                b, t0 = j // (N // 128), (j % (N // 128)) * 128
                buf = j % 8
                pb = j % 2
                base = (j % 8) * 8

                def _vn(v, buf=buf, w=w_s3_of[j], base=base, wxf=xn_free[0]):
                    v.wait_ge(ss.h, w)
                    if wxf:
                        v.wait_ge(tt.h, wxf)
                    v.tensor_scalar(out=xn_tok[:, :], in0=xt[buf][:, :],
                                    scalar1=stat[:, base + 2:base + 3],
                                    scalar2=stat[:, base + 7:base + 8],
                                    op0=ALU.subtract,
                                    op1=ALU.mult).then_inc(vv.h, 1)
                PV.append(_vn)
                v_done[buf] = vv.inc()

                def _t4(t, pb=pb, wv=v_done[buf], wse=s_evac2[pb]):
                    t.wait_ge(vv.h, wv)
                    if wse:
                        t.wait_ge(ss.h, wse)
                    for c in range(8):
                        ins = t.transpose(ps_t2[pb][:, c * 128:(c + 1) * 128],
                                          xn_tok[:, c * 128:(c + 1) * 128],
                                          ident[:, :])
                    ins.then_inc(tt.h, 1)
                PT.append(_t4)
                w_t4 = tt.inc()
                xn_free[0] = w_t4

                def _s4(s, pb=pb, w=w_t4, b=b, t0=t0):
                    s.wait_ge(tt.h, w)
                    for c in range(8):
                        col = c * (NB * N) + b * N + t0
                        ins = s.copy(out=xn[:, col:col + 128],
                                     in_=ps_t2[pb][:, c * 128:(c + 1) * 128])
                    ins.then_inc(ss.h, 1)
                PS.append(_s4)
                s_evac2[pb] = ss.inc()
                stat_free[buf] = v_done[buf]

            for jj in range(ntile + 7):
                if jj < ntile:
                    ln_front(jj)
                for k in range(1, 7):
                    j2 = jj - k
                    if 0 <= j2 < ntile:
                        ln_stage(j2, k)
                if 0 <= jj - 6 < ntile:
                    ln_back(jj - 6)

            wv_done = {0: 0, 1: 0}
            wst_done = {0: 0, 1: 0}

            def prep_weight(src, Rr, Cc, is_int, dst):
                nrc = Rr // 128
                ncc = Cc // 128
                for rc in range(nrc):
                    buf = rc % 2
                    stg = stg_i[buf] if is_int else stg_f[buf]
                    if wv_done[buf]:
                        PY.append(lambda y, w=wv_done[buf]: y.wait_ge(vv.h, w))
                    PY.append(lambda y, src=src, rc=rc, stg=stg, Cc=Cc:
                              y.dma_start(out=stg[:, 0:Cc],
                                          in_=src[rc * 128:(rc + 1) * 128, :])
                              .then_inc(ld.h, 16))
                    myld = ld.inc(16)

                    def _vc(v, stg=stg, buf=buf, myld=myld, is_int=is_int, Cc=Cc):
                        v.wait_ge(ld.h, myld)
                        v.tensor_scalar(out=wcast[buf][:, 0:Cc], in0=stg[:, 0:Cc],
                                        scalar1=(1.0 / 127.0) if is_int else 1.0,
                                        scalar2=None, op0=ALU.mult).then_inc(vv.h, 1)
                    PV.append(_vc)
                    wv_done[buf] = vv.inc()

                    for cg in range(0, ncc, 8):
                        nblk = min(8, ncc - cg)

                        def _tt(t, buf=buf, wvc=wv_done[buf], cg=cg, nblk=nblk,
                                wse=s_evacA[buf]):
                            t.wait_ge(vv.h, wvc)
                            if wse:
                                t.wait_ge(ss.h, wse)
                            for j in range(nblk):
                                c = cg + j
                                ins = t.transpose(
                                    ps_t[buf][:, j * 128:(j + 1) * 128],
                                    wcast[buf][:, c * 128:(c + 1) * 128],
                                    ident[:, :])
                            ins.then_inc(tt.h, 1)
                        PT.append(_tt)
                        w_tp = tt.inc()

                        def _se(s, buf=buf, w=w_tp, nblk=nblk, wst=wst_done[buf]):
                            s.wait_ge(tt.h, w)
                            if wst:
                                s.wait_ge(st.h, wst)
                            s.copy(out=wstage[buf][:, 0:nblk * 128],
                                   in_=ps_t[buf][:, 0:nblk * 128]).then_inc(ss.h, 1)
                        PS.append(_se)
                        s_evacA[buf] = ss.inc()

                        def _yd(y, buf=buf, w=s_evacA[buf], cg=cg, nblk=nblk,
                                rc=rc, dst=dst):
                            y.wait_ge(ss.h, w)
                            for j in range(nblk):
                                c = cg + j
                                y.dma_start(
                                    out=dst[c * 128:(c + 1) * 128,
                                            rc * 128:(rc + 1) * 128],
                                    in_=wstage[buf][:, j * 128:(j + 1) * 128]
                                ).then_inc(st.h, 16)
                        PY.append(_yd)
                        wst_done[buf] = st.inc(16 * nblk)

            prep_weight(wqkv, 3 * D, D, False, wqkvT_sc)
            prep_weight(wo, D, D, False, woT_sc)
            prep_weight(wg, HID, D, True, wgT_sc)
            prep_weight(wu, HID, D, True, wuT_sc)
            prep_weight(wd, D, HID, True, wdT_sc)

            if dbg:
                PY.append(lambda y: y.wait_ge(ss.h, ss.n))
                PY.append(lambda y: y.dma_start(out=xn_dbg[:, :],
                                                in_=xn[:, :]).then_inc(st.h, 16))
                st.inc(16)
                PY.append(lambda y: y.dma_start(out=dbg_st_d[:, :],
                                                in_=dbg_sb[:, :]).then_inc(st.h, 16))
                st.inc(16)
            PY.append(lambda y: y.wait_ge(st.h, st.n))
            PY.append(lambda y: y.wait_ge(ss.h, ss.n))
            _run_plans(nc, PT, PV, PS, PY)

        if phases < 2:
            return _finish(nc)
        with ExitStack() as p2:
            e2 = p2.enter_context
            wqkvT = e2(nc.sbuf_tensor("wqkvT", [128, 8 * 3 * D], BF16))
            qpair = e2(nc.sbuf_tensor("qpair", [128, N], BF16))
            kpair = e2(nc.sbuf_tensor("kpair", [128, N], BF16))
            vpair = e2(nc.sbuf_tensor("vpair", [128, N], BF16))
            sgrp = e2(nc.sbuf_tensor("sgrp", [128, 28 * KT], F32))
            dscr = [e2(nc.sbuf_tensor(f"dscr{i}", [128, KT], F32))
                    for i in range(3)]
            rstrip = e2(nc.sbuf_tensor("rstrip", [128, 12 * KT], F32))
            pbf = [e2(nc.sbuf_tensor(f"pbf{i}", [128, KT], BF16)) for i in range(2)]
            ptsb = [e2(nc.sbuf_tensor(f"ptsb{i}", [128, KT], BF16)) for i in range(2)]
            othead = e2(nc.sbuf_tensor("othead", [64, N], BF16))
            rowmaxP = e2(nc.sbuf_tensor("rowmaxP", [128, 32], F32))
            lam = e2(nc.sbuf_tensor("lam", [128, 16], F32))
            lamK = e2(nc.sbuf_tensor("lamK", [128, 8], F32))
            srd = e2(nc.sbuf_tensor("srd", [128, 16], F32))
            sr2 = e2(nc.sbuf_tensor("sr2", [128, 8], F32))
            nt1 = e2(nc.sbuf_tensor("nt1", [128, 8], F32))
            nt2 = e2(nc.sbuf_tensor("nt2", [128, 8], F32))
            partR = e2(nc.sbuf_tensor("partR", [128, 8], F32))
            partR2 = e2(nc.sbuf_tensor("partR2", [128, 8], F32))
            partF = e2(nc.sbuf_tensor("partF", [128, 12], F32))
            junk1 = e2(nc.sbuf_tensor("junk1", [128, 1], F32))
            rnorm = e2(nc.sbuf_tensor("rnorm", [128, 8], F32))
            rnsc = e2(nc.sbuf_tensor("rnsc", [128, 8], F32))

            ps_q = [e2(nc.psum_tensor(f"ps_q{i}", [128, KT], F32)) for i in range(2)]
            ps_s = [e2(nc.psum_tensor(f"ps_s{i}", [128, KT], F32)) for i in range(2)]
            ps_pt = [e2(nc.psum_tensor(f"ps_pt{i}", [128, KT], BF16)) for i in range(2)]
            ps_o = [e2(nc.psum_tensor(f"ps_o{i}", [64, 128], F32)) for i in range(2)]

            ld, qmm, qev, smm, sfree = (S2["p2_ld"], S2["p2_qmm"], S2["p2_qev"],
                                        S2["p2_smm"], S2["p2_sfree"])
            pr, tpd, pte, pvd, opv = (S2["p2_pr"], S2["p2_tpd"], S2["p2_pte"],
                                      S2["p2_pvd"], S2["p2_opv"])
            oev, odma = S2["p2_oev"], S2["p2_odma"]
            PT, PV, PS, PY = [], [], [], []
            dbgs = Sem(e2(nc.semaphore("p2_dbg")))
            dbgv = Sem(e2(nc.semaphore("p2_dbgv")))
            uvm = Sem(e2(nc.semaphore("p2_uvm")))
            gup = Sem(e2(nc.semaphore("p2_gup")))
            snu = Sem(e2(nc.semaphore("p2_snu")))
            fsy = Sem(e2(nc.semaphore("p2_fsy")))
            srn = Sem(e2(nc.semaphore("p2_srn")))
            sdp = Sem(e2(nc.semaphore("p2_sdp")))
            vdf = Sem(e2(nc.semaphore("p2_vdf")))
            gup_need = {0: 0, 1: 0}
            PG = []

            def _ldw(y):
                for c in range(8):
                    y.dma_start(
                        out=wqkvT[:, c * (3 * D):(c + 1) * (3 * D)],
                        in_=wqkvT_sc[c * 128:(c + 1) * 128, :]).then_inc(ld.h, 16)
            PY.append(_ldw)
            w_wqkv = ld.inc(16 * 8)

            WARMUP = False
            def _vwarm(v):
                v.memset(dscr[0][:, :], 1.0)
                v.memset(partR2[:, 0:8], 1.0)
                v.memset(lam[:, :], 1.0)
                v.memset(nt1[:, :], 1.0)
                v.memset(nt2[:, :], 1.0)
                v.tensor_mask_reduce(
                    out=rstrip[:, 0:KT], in_=dscr[0][:, :], mask_start=0.0,
                    mask_end=lens_sb[:, 3:4], scale=1.0, accum_in=-1e30,
                    op=ALU.max, accum_out=partF[:, 0:1])
                v._custom_dve(RECIP7_SUM, out=rstrip[:, 0:KT], in0=dscr[0][:, :],
                              s0=C0_R, s1=0.0, imm2=C1_R,
                              accum_out=partR[:, 0:1])
                v._custom_dve(SQ_SUM, out=rstrip[:, KT:2 * KT],
                              in0=rstrip[:, 0:KT], s0=0.0, s1=0.0,
                              accum_out=partR[:, 1:2])
                v.reciprocal_approx_accurate(out=nt2[:, 0:4],
                                             in_=partR2[:, 0:4],
                                             scratch=srd[:, 0:4])
                v.tensor_copy(out=junk1[:, :],
                              in_=junk1[:, :]).then_inc(uvm.h, 1)
            if WARMUP:
                PV.append(_vwarm)
                uvm.inc()

            def _gwarm(g_):
                g_.wait_ge(uvm.h, 1)
                g_.tensor_tensor(out=sr2[:, 0:4], in0=nt1[:, 0:4],
                                 in1=nt2[:, 0:4], op=ALU.mult)
                g_.tensor_tensor(out=lam[:, 8:12], in0=lam[:, 0:4],
                                 in1=sr2[:, 0:4], op=ALU.add)
                g_.tensor_scalar(out=lamK[:, 0:4], in0=lam[:, 8:12],
                                 scalar1=K_R, scalar2=None,
                                 op0=ALU.mult).then_inc(gup.h, 1)
            if WARMUP:
                PG.append(_gwarm)
                gup.inc()
                PV.append(lambda v: v.wait_ge(gup.h, 1))

            qev_bank = {0: 0, 1: 0}
            pte_bank = {0: 0, 1: 0}
            pvd_slot = {0: 0, 1: 0}
            sfree_tile = {0: 0, 1: 0}
            oev_row = {0: 0, 1: 0}
            odma_buf = {0: 0, 1: 0}
            tpd_slot = {0: 0, 1: 0}

            state = {"qkv_tile": 0, "s_tile": 0, "p_tile": 0, "d_tile": 0}
            vdf_slot = {0: 0, 1: 0, 2: 0}

            def qkv_pair(b, hp):
                for jbase, scl, dstt in ((128 * hp, QPRE, qpair),
                                         (D + 128 * hp, 1.0, kpair)):
                    for tc in range(4):
                        bank = state["qkv_tile"] % 2
                        t0 = tc * KT

                        def _tm(t, jbase=jbase, t0=t0, bank=bank, b=b,
                                wfree=qev_bank[bank]):
                            t.wait_ge(ld.h, w_wqkv)
                            if wfree:
                                t.wait_ge(qev.h, wfree)
                            for c in range(8):
                                ins = t.matmul(
                                    ps_q[bank][:, :],
                                    wqkvT[:, c * 3 * D + jbase:
                                          c * 3 * D + jbase + 128],
                                    xn[:, c * NB * N + b * N + t0:
                                       c * NB * N + b * N + t0 + KT],
                                    start=(c == 0), stop=(c == 7))
                            ins.then_inc(qmm.h, 1)
                        PT.append(_tm)
                        w_mm = qmm.inc()

                        def _sev(s, bank=bank, w=w_mm, dstt=dstt, t0=t0, scl=scl):
                            s.wait_ge(qmm.h, w)
                            s.activation(out=dstt[:, t0:t0 + KT],
                                         in_=ps_q[bank][:, :], func=ACT.Copy,
                                         bias=0.0, scale=scl).then_inc(qev.h, 1)
                        PS.append(_sev)
                        qev_bank[bank] = qev.inc()
                        state["qkv_tile"] += 1
                jbase = 2 * D + 128 * hp
                for tc in range(NQB):
                    bank = state["qkv_tile"] % 2
                    t0 = tc * 128

                    def _tv(t, jbase=jbase, t0=t0, bank=bank, b=b,
                            wfree=qev_bank[bank]):
                        if wfree:
                            t.wait_ge(qev.h, wfree)
                        for c in range(8):
                            ins = t.matmul(
                                ps_q[bank][:, 0:128],
                                xn[:, c * NB * N + b * N + t0:
                                   c * NB * N + b * N + t0 + 128],
                                wqkvT[:, c * 3 * D + jbase:
                                      c * 3 * D + jbase + 128],
                                start=(c == 0), stop=(c == 7))
                        ins.then_inc(qmm.h, 1)
                    PT.append(_tv)
                    w_mm = qmm.inc()

                    def _svev(s, bank=bank, w=w_mm, tc=tc):
                        s.wait_ge(qmm.h, w)
                        s.activation(out=vpair[:, tc * 128:(tc + 1) * 128],
                                     in_=ps_q[bank][:, 0:128], func=ACT.Copy,
                                     bias=0.0, scale=1.0).then_inc(qev.h, 1)
                    PS.append(_svev)
                    qev_bank[bank] = qev.inc()
                    state["qkv_tile"] += 1

            def attn_group(b, h, g, w_pair):
                par = h % 2
                qb0 = g * 8
                rows = list(range(qb0, qb0 + 8))
                halves = [rows[0:4], rows[4:8]]
                tiles_rm = [(qb, kt) for qb in rows for kt in range(tiles_of(qb))]
                slot = {t: i for i, t in enumerate(tiles_rm)}

                tiles_cm = [(qb, kt) for kt in range(4) for qb in rows
                            if kt < tiles_of(qb)]
                for qb, kt in tiles_cm:
                    bank = state["s_tile"] % 2
                    sl = slot[(qb, kt)]

                    def _ts(t, qb=qb, kt=kt, bank=bank, par=par,
                            wfree=sfree_tile[bank], w_pair=w_pair):
                        t.wait_ge(qev.h, w_pair)
                        if wfree:
                            t.wait_ge(sfree.h, wfree)
                        t.matmul(ps_s[bank][:, :],
                                 qpair[64 * par:64 * par + 64,
                                       qb * 128:(qb + 1) * 128],
                                 kpair[64 * par:64 * par + 64,
                                       kt * KT:(kt + 1) * KT],
                                 start=True, stop=True).then_inc(smm.h, 1)
                    PT.append(_ts)
                    w_smm = smm.inc()
                    first = (kt == 0)
                    rc = qb - qb0

                    def _vp(v, bank=bank, w=w_smm, sl=sl, qb=qb, kt=kt,
                            rc=rc):
                        v.wait_ge(smm.h, w)
                        v.tensor_mask_reduce(
                            out=sgrp[:, sl * KT:(sl + 1) * KT],
                            in_=ps_s[bank][:, :],
                            mask_start=0.0,
                            mask_end=lens_sb[:, qb * 4 + kt:qb * 4 + kt + 1],
                            scale=SPOST,
                            accum_in=(-1e30 if kt == 0
                                      else rowmaxP[:, rc:rc + 1]),
                            op=ALU.max,
                            accum_out=rowmaxP[:, rc:rc + 1]
                        ).then_inc(sfree.h, 1)
                    PV.append(_vp)
                    sfree_tile[bank] = sfree.inc()
                    state["s_tile"] += 1

                def emit_lambda_init(hf):
                    hb = hf * 4

                    def _vl(v):
                        v.tensor_copy(out=junk1[:, :],
                                      in_=junk1[:, :]).then_inc(uvm.h, 1)
                    PV.append(_vl)
                    wv = uvm.inc()

                    def _gl(g_, hb=hb, wv=wv):
                        g_.wait_ge(uvm.h, wv)
                        g_.tensor_scalar(out=lam[:, hb:hb + 4],
                                         in0=rowmaxP[:, hb:hb + 4],
                                         scalar1=1.0, scalar2=None,
                                         op0=ALU.add)
                        g_.tensor_scalar(out=lamK[:, hb:hb + 4],
                                         in0=lam[:, hb:hb + 4], scalar1=K_R,
                                         scalar2=None,
                                         op0=ALU.mult).then_inc(gup.h, 1)
                    PG.append(_gl)
                    gup_need[hf] = gup.inc()

                def emit_d_pass(sl, rc):
                    pb = state["d_tile"] % 3
                    state["d_tile"] += 1

                    def _sd(s, sl=sl, rc=rc, pb=pb, wg=gup_need[rc // 4],
                            wv=vdf_slot[pb]):
                        s.wait_ge(gup.h, wg)
                        if wv:
                            s.wait_ge(vdf.h, wv)
                        s.activation(out=dscr[pb][:, :],
                                     in_=sgrp[:, sl * KT:(sl + 1) * KT],
                                     func=ACT.Identity,
                                     bias=lamK[:, rc:rc + 1],
                                     scale=NEG_KR).then_inc(sdp.h, 1)
                    PS.append(_sd)
                    return pb, sdp.inc()

                def emit_iter_tiles(hf):
                    half = halves[hf]
                    nkt = tiles_of(half[0])
                    for kt in range(nkt):
                        for qb in half:
                            sl = slot[(qb, kt)]
                            rc = qb - qb0
                            pb, w_sd = emit_d_pass(sl, rc)

                            def _vi(v, sl=sl, rc=rc, kt=kt, pb=pb, w_sd=w_sd):
                                v.wait_ge(sdp.h, w_sd)
                                v._custom_dve(
                                    RECIP7_SUM,
                                    out=rstrip[:, kt * KT:(kt + 1) * KT],
                                    in0=dscr[pb][:, :], s0=C0_R,
                                    s1=(0.0 if kt == 0 else partR[:, rc:rc + 1]),
                                    imm2=C1_R,
                                    accum_out=partR[:, rc:rc + 1]
                                ).then_inc(vdf.h, 1)
                                v._custom_dve(
                                    SQ_SUM, out=rstrip[:, 4 * KT:5 * KT],
                                    in0=rstrip[:, kt * KT:(kt + 1) * KT],
                                    s0=(0.0 if kt == 0 else partR2[:, rc:rc + 1]),
                                    s1=0.0,
                                    accum_out=partR2[:, rc:rc + 1])
                            PV.append(_vi)
                            vdf_slot[pb] = vdf.inc()

                def emit_update_marker(hf):
                    def _vn(v):
                        v.tensor_copy(out=junk1[:, :],
                                      in_=junk1[:, :]).then_inc(uvm.h, 1)
                    PV.append(_vn)
                    return uvm.inc()

                def emit_update_sg(it, hf, wv):
                    hb = hf * 4
                    sb = 8 * (it % 2)
                    db = 8 * ((it + 1) % 2)

                    def _sn(s, hb=hb, wv=wv):
                        s.wait_ge(uvm.h, wv)
                        s.activation(out=srd[:, hb:hb + 4],
                                     in_=partR[:, hb:hb + 4],
                                     func=ACT.Identity,
                                     bias=lens_sb[:, NQB * 4 + 1:NQB * 4 + 2],
                                     scale=1.0)
                        s.activation(out=srd[:, 8 + hb:12 + hb],
                                     in_=partR2[:, hb:hb + 4],
                                     func=ACT.Ln, bias=0.0, scale=1.0)
                        s.activation(out=nt1[:, hb:hb + 4],
                                     in_=srd[:, hb:hb + 4], func=ACT.Copy,
                                     bias=0.0, scale=K3)
                        s.activation(out=nt2[:, hb:hb + 4],
                                     in_=srd[:, 8 + hb:12 + hb], func=ACT.Exp,
                                     bias=0.0, scale=-1.0).then_inc(snu.h, 1)
                    PS.append(_sn)
                    ws = snu.inc()

                    def _gn(g_, hb=hb, sb=sb, db=db, ws=ws):
                        g_.wait_ge(snu.h, ws)
                        g_.tensor_tensor(out=sr2[:, hb:hb + 4],
                                         in0=nt1[:, hb:hb + 4],
                                         in1=nt2[:, hb:hb + 4], op=ALU.mult)
                        g_.tensor_tensor(out=lam[:, db + hb:db + hb + 4],
                                         in0=lam[:, sb + hb:sb + hb + 4],
                                         in1=sr2[:, hb:hb + 4], op=ALU.add)
                        g_.tensor_scalar(out=lamK[:, hb:hb + 4],
                                         in0=lam[:, db + hb:db + hb + 4],
                                         scalar1=K_R, scalar2=None,
                                         op0=ALU.mult).then_inc(gup.h, 1)
                    PG.append(_gn)
                    gup_need[hf] = gup.inc()

                def emit_f1_row(qb):
                    rc = qb - qb0
                    nkt = tiles_of(qb)
                    ps = (qb % 3) * 4
                    for kt in range(nkt):
                        sl = slot[(qb, kt)]
                        pb, w_sd = emit_d_pass(sl, rc)

                        def _vf1(v, rc=rc, kt=kt, ps=ps, pb=pb, w_sd=w_sd):
                            v.wait_ge(sdp.h, w_sd)
                            v._custom_dve(
                                RECIP7_SUM,
                                out=rstrip[:, (ps + kt) * KT:(ps + kt + 1) * KT],
                                in0=dscr[pb][:, :], s0=C0_R, s1=0.0, imm2=C1_R,
                                accum_out=partF[:, ps + kt:ps + kt + 1]
                            ).then_inc(vdf.h, 1)
                        PV.append(_vf1)
                        vdf_slot[pb] = vdf.inc()
                    PV.append(lambda v: v.tensor_copy(
                        out=junk1[:, :], in_=junk1[:, :]).then_inc(fsy.h, 1))
                    fsy.inc()

                def emit_rnorm_s(qb):
                    rc = qb - qb0
                    nkt = tiles_of(qb)
                    ps = (qb % 3) * 4
                    sl = qb % 2

                    def _sr(s, rc=rc, nkt=nkt, ps=ps, base=sl * 4, wf=fsy.n):
                        s.wait_ge(fsy.h, wf)
                        if nkt == 1:
                            src = partF[:, ps:ps + 1]
                        else:
                            s.activation(out=rnsc[:, base:base + 1],
                                         in_=partF[:, ps:ps + 1],
                                         func=ACT.Identity,
                                         bias=partF[:, ps + 1:ps + 2],
                                         scale=1.0)
                            src = rnsc[:, base:base + 1]
                            if nkt == 3:
                                s.activation(out=rnsc[:, base + 1:base + 2],
                                             in_=rnsc[:, base:base + 1],
                                             func=ACT.Identity,
                                             bias=partF[:, ps + 2:ps + 3],
                                             scale=1.0)
                                src = rnsc[:, base + 1:base + 2]
                            elif nkt == 4:
                                s.activation(out=rnsc[:, base + 1:base + 2],
                                             in_=rnsc[:, base:base + 1],
                                             func=ACT.Identity,
                                             bias=partF[:, ps + 2:ps + 3],
                                             scale=1.0)
                                s.activation(out=rnsc[:, base + 2:base + 3],
                                             in_=rnsc[:, base + 1:base + 2],
                                             func=ACT.Identity,
                                             bias=partF[:, ps + 3:ps + 4],
                                             scale=1.0)
                                src = rnsc[:, base + 2:base + 3]
                        s.activation(out=rnsc[:, base + 3:base + 4], in_=src,
                                     func=ACT.Ln, bias=0.0, scale=-1.0)
                        s.activation(out=rnorm[:, rc:rc + 1],
                                     in_=rnsc[:, base + 3:base + 4],
                                     func=ACT.Exp, bias=0.0,
                                     scale=-1.0).then_inc(srn.h, 1)
                    PS.append(_sr)
                    return srn.inc()

                dbg_here = dbg and b == 0 and h == 2

                def dbg_dump(dst, srcap):
                    def _vm(v):
                        v.tensor_copy(out=junk1[:, :],
                                      in_=junk1[:, :]).then_inc(dbgv.h, 1)
                    PV.append(_vm)
                    wv = dbgv.inc()

                    def _gd(g_, dst=dst, srcap=srcap, wv=wv):
                        g_.wait_ge(dbgv.h, wv)
                        g_.dma_start(out=dst, in_=srcap).then_inc(dbgs.h, 16)
                    PG.append(_gd)
                    wd = dbgs.inc(16)
                    PV.append(lambda v, wd=wd: v.wait_ge(dbgs.h, wd))

                if dbg_here:
                    ntot = sum(tiles_of(q) for q in rows)
                    dbg_dump(dbg_sgrp[g, :, 0:ntot * KT], sgrp[:, 0:ntot * KT])
                    dbg_dump(dbg_rmax[g, :, :], rowmaxP[:, :])
                emit_lambda_init(0)
                for i in range(4):
                    emit_iter_tiles(0)
                    wv0 = emit_update_marker(0)
                    if i == 0:
                        emit_lambda_init(1)
                    emit_iter_tiles(1)
                    emit_update_sg(i, 0, wv0)
                    wv1 = emit_update_marker(1)
                    emit_update_sg(i, 1, wv1)


                def emit_f2_row(qb):
                    rc = qb - qb0
                    nkt = tiles_of(qb)
                    ps = (qb % 3) * 4
                    w_opv = None
                    for kt in range(nkt):
                        pslot = state["p_tile"] % 2

                        def _vf2(v, kt=kt, pslot=pslot, wtp=tpd_slot[pslot],
                                 rc=rc, ps=ps, wrn=rn_need.get(qb, 0)):
                            if kt == 0 and wrn:
                                v.wait_ge(srn.h, wrn)
                            if wtp:
                                v.wait_ge(tpd.h, wtp)
                            v.tensor_scalar(out=pbf[pslot][:, :],
                                            in0=rstrip[:, (ps + kt) * KT:
                                                       (ps + kt + 1) * KT],
                                            scalar1=rnorm[:, rc:rc + 1],
                                            scalar2=-1.0,
                                            op0=ALU.mult,
                                            op1=ALU.mult).then_inc(pr.h, 1)
                        PV.append(_vf2)
                        w_pr = pr.inc()

                        def _ttp(t, pslot=pslot, w=w_pr, wpe=pte_bank[pslot]):
                            t.wait_ge(pr.h, w)
                            if wpe:
                                t.wait_ge(pte.h, wpe)
                            for c in range(4):
                                ins = t.transpose(
                                    ps_pt[pslot][:, c * 128:(c + 1) * 128],
                                    pbf[pslot][:, c * 128:(c + 1) * 128],
                                    ident[:, :])
                            ins.then_inc(tpd.h, 1)
                        PT.append(_ttp)
                        tpd_slot[pslot] = tpd.inc()

                        def _spt(s, pslot=pslot, w=tpd_slot[pslot],
                                 wpv=pvd_slot[pslot]):
                            s.wait_ge(tpd.h, w)
                            if wpv:
                                s.wait_ge(pvd.h, wpv)
                            s.copy(out=ptsb[pslot][:, :],
                                   in_=ps_pt[pslot][:, :]).then_inc(pte.h, 1)
                        PS.append(_spt)
                        pte_bank[pslot] = pte.inc()

                        obank = qb % 2

                        def _tpv(t, pslot=pslot, w=pte_bank[pslot], kt=kt,
                                 par=par, obank=obank, first=(kt == 0),
                                 last=(kt == nkt - 1),
                                 wo_free=oev_row[obank]):
                            t.wait_ge(pte.h, w)
                            if first and wo_free:
                                t.wait_ge(oev.h, wo_free)
                            for c in range(4):
                                tc = kt * 4 + c
                                ins = t.matmul(
                                    ps_o[obank][:, :],
                                    vpair[:, tc * 128 + 64 * par:
                                          tc * 128 + 64 * par + 64],
                                    ptsb[pslot][:, c * 128:(c + 1) * 128],
                                    start=(first and c == 0),
                                    stop=(last and c == 3))
                            ins.then_inc(pvd.h, 1)
                        PT.append(_tpv)
                        pvd_slot[pslot] = pvd.inc()
                        if kt == nkt - 1:
                            w_opv = pvd.n
                        state["p_tile"] += 1

                    def _soev(s, qb=qb, w=w_opv, obank=qb % 2):
                        s.wait_ge(pvd.h, w)
                        s.activation(out=othead[:, qb * 128:(qb + 1) * 128],
                                     in_=ps_o[obank][:, :], func=ACT.Copy,
                                     bias=0.0, scale=1.0).then_inc(oev.h, 1)
                    PS.append(_soev)
                    oev_row[qb % 2] = oev.inc()

                rn_need = {}
                emit_f1_row(rows[0])
                for j in range(1, 8):
                    if j == 4 and dbg_here:
                        dbg_dump(dbg_lam[g, :, :], lam[:, 0:8])
                    emit_f1_row(rows[j])
                    rn_need[rows[j - 1]] = emit_rnorm_s(rows[j - 1])
                    if j >= 2:
                        emit_f2_row(rows[j - 2])
                rn_need[rows[7]] = emit_rnorm_s(rows[7])
                emit_f2_row(rows[6])
                emit_f2_row(rows[7])
                if dbg_here:
                    dbg_dump(dbg_rn[g, :, :], rnorm[:, :])

            bh = 0
            for b in range(NB):
                for h in range(H):
                    if h % 2 == 0:
                        qkv_pair(b, h // 2)
                        if dbg and b == 0 and h == 2:
                            def _gdq(g_, w=qev.n):
                                g_.wait_ge(qev.h, w)
                                g_.dma_start(out=dbg_q[:, :],
                                             in_=qpair[:, :]).then_inc(dbgs.h, 16)
                                g_.dma_start(out=dbg_k[:, :],
                                             in_=kpair[:, :]).then_inc(dbgs.h, 16)
                                g_.dma_start(out=dbg_v[:, :],
                                             in_=vpair[:, :]).then_inc(dbgs.h, 16)
                            PG.append(_gdq)
                            dbgs.inc(48)
                    w_pair = qev.n
                    if odma_buf[0]:
                        def _sw(s, w=odma_buf[0]):
                            s.wait_ge(odma.h, w)
                        PS.append(_sw)
                    for g in range(2):
                        attn_group(b, h, g, w_pair)
                    w_all_oev = oev.n

                    def _sdma(s, b=b, h=h, w=w_all_oev):
                        s.wait_ge(oev.h, w)
                        s.dma_start(out=oT_sc[b, h, :, :],
                                    in_=othead[:, :]).then_inc(odma.h, 16)
                    PS.append(_sdma)
                    odma_buf[0] = odma.inc(16)
                    bh += 1

            PS.append(lambda s: s.wait_ge(odma.h, odma.n))
            if dbg:
                PY.append(lambda y: y.wait_ge(dbgs.h, dbgs.n))
            _run_plans(nc, PT, PV, PS, PY, PG)

        if phases < 3:
            return _finish(nc)
        with ExitStack() as p3:
            e3 = p3.enter_context
            woT = e3(nc.sbuf_tensor("woT", [128, 8 * D], BF16))
            wdT = e3(nc.sbuf_tensor("wdT", [128, 24 * D], BF16))
            oT_ld = [e3(nc.sbuf_tensor(f"oT_ld{i}", [128, 8 * KT], BF16))
                     for i in range(2)]
            wgld = [e3(nc.sbuf_tensor(f"wgld{i}", [128, 8 * 128], BF16))
                    for i in range(2)]
            wuld = [e3(nc.sbuf_tensor(f"wuld{i}", [128, 8 * 128], BF16))
                    for i in range(2)]
            hbuf = e3(nc.sbuf_tensor("hbuf", [128, 24 * KT], BF16))
            sgate = [e3(nc.sbuf_tensor(f"sgate{i}", [128, KT], F32))
                     for i in range(2)]
            gbuf = [e3(nc.sbuf_tensor(f"gbuf{i}", [128, D], F32))
                    for i in range(2)]
            sgbuf = e3(nc.sbuf_tensor("sgbuf", [128, D], F32))
            xres = [e3(nc.sbuf_tensor(f"xres{i}", [128, D], F32)) for i in range(2)]
            outb = [e3(nc.sbuf_tensor(f"outb{i}", [128, D], F32)) for i in range(2)]

            ps_g = [e3(nc.psum_tensor(f"ps_g{i}", [128, KT], F32)) for i in range(2)]
            ps_u = [e3(nc.psum_tensor(f"ps_u{i}", [128, KT], F32)) for i in range(2)]
            ps_wo = e3(nc.psum_tensor("ps_wo", [128, D], F32))
            ps_dn = e3(nc.psum_tensor("ps_dn", [128, D], F32))

            ld, gmm, hdone, sgd = (S3["p3_ld"], S3["p3_gmm"], S3["p3_hdone"],
                                   S3["p3_sgd"])
            womm, dnmm, gsil, vout, st = (S3["p3_womm"], S3["p3_dnmm"],
                                          S3["p3_gsil"], S3["p3_vout"],
                                          S3["p3_st"])
            PT, PV, PS, PY = [], [], [], []

            def _lw(y):
                for c in range(8):
                    y.dma_start(out=woT[:, c * D:(c + 1) * D],
                                in_=woT_sc[c * 128:(c + 1) * 128, :]).then_inc(ld.h, 16)
                for c in range(24):
                    y.dma_start(out=wdT[:, c * D:(c + 1) * D],
                                in_=wdT_sc[c * 128:(c + 1) * 128, :]).then_inc(ld.h, 16)
            PY.append(_lw)
            w_w = ld.inc(16 * 32)

            wg_free = {0: 0, 1: 0}
            oT_free = {0: 0, 1: 0}
            x_free = {0: 0, 1: 0}
            out_free = {0: 0, 1: 0}
            h_bank = {0: 0, 1: 0}
            sg_free = {0: 0, 1: 0}
            g_free = {0: 0, 1: 0}
            wo_prev = 0
            dn_prev = 0

            nspan = NB * N // KT
            subt = 0
            for sp in range(nspan):
                b, t0 = sp // (N // KT), (sp % (N // KT)) * KT
                obuf = sp % 2
                if oT_free[obuf]:
                    PY.append(lambda y, w=oT_free[obuf]: y.wait_ge(womm.h, w))

                def _lo(y, b=b, t0=t0, obuf=obuf):
                    for c in range(8):
                        y.dma_start(out=oT_ld[obuf][0:64, c * KT:(c + 1) * KT],
                                    in_=oT_sc[b, 2 * c, :, t0:t0 + KT]
                                    ).then_inc(ld.h, 16)
                        y.dma_start(out=oT_ld[obuf][64:128, c * KT:(c + 1) * KT],
                                    in_=oT_sc[b, 2 * c + 1, :, t0:t0 + KT]
                                    ).then_inc(ld.h, 16)
                PY.append(_lo)
                w_oT = ld.inc(16 * 16)

                for ht in range(24):
                    wbuf = ht % 2
                    if wg_free[wbuf]:
                        PY.append(lambda y, w=wg_free[wbuf]: y.wait_ge(gmm.h, w))

                    def _lwg(y, ht=ht, wbuf=wbuf):
                        ap_g = bass.AP(wgT_sc, ht * 128,
                                       [[HID, 128], [128 * HID, 8], [1, 128]])
                        ap_u = bass.AP(wuT_sc, ht * 128,
                                       [[HID, 128], [128 * HID, 8], [1, 128]])
                        y.dma_start(out=wgld[wbuf][:, :], in_=ap_g).then_inc(ld.h, 16)
                        y.dma_start(out=wuld[wbuf][:, :], in_=ap_u).then_inc(ld.h, 16)
                    PY.append(_lwg)
                    w_wg = ld.inc(32)
                    pbank = ht % 2

                    def _tg(t, wbuf=wbuf, b=b, t0=t0, w=w_wg, pbank=pbank,
                            whd=h_bank[pbank]):
                        t.wait_ge(ld.h, w)
                        if whd:
                            t.wait_ge(hdone.h, whd)
                        for c in range(8):
                            t.matmul(ps_g[pbank][:, :],
                                     wgld[wbuf][:, c * 128:(c + 1) * 128],
                                     xn[:, c * NB * N + b * N + t0:
                                        c * NB * N + b * N + t0 + KT],
                                     start=(c == 0), stop=(c == 7))
                        for c in range(8):
                            ins = t.matmul(ps_u[pbank][:, :],
                                           wuld[wbuf][:, c * 128:(c + 1) * 128],
                                           xn[:, c * NB * N + b * N + t0:
                                              c * NB * N + b * N + t0 + KT],
                                           start=(c == 0), stop=(c == 7))
                        ins.then_inc(gmm.h, 1)
                    PT.append(_tg)
                    wg_free[wbuf] = gmm.inc()

                    def _ssil(s, w=wg_free[wbuf], pbank=pbank,
                              wsg=sg_free[pbank]):
                        s.wait_ge(gmm.h, w)
                        if wsg:
                            s.wait_ge(hdone.h, wsg)
                        s.activation(out=sgate[pbank][:, :], in_=ps_g[pbank][:, :],
                                     func=ACT.Silu, bias=0.0,
                                     scale=1.0).then_inc(sgd.h, 1)
                    PS.append(_ssil)
                    w_sg = sgd.inc()

                    def _vh(v, ht=ht, w=w_sg, pbank=pbank):
                        v.wait_ge(sgd.h, w)
                        v.tensor_tensor(out=hbuf[:, ht * KT:(ht + 1) * KT],
                                        in0=sgate[pbank][:, :],
                                        in1=ps_u[pbank][:, :],
                                        op=ALU.mult).then_inc(hdone.h, 1)
                    PV.append(_vh)
                    h_bank[pbank] = hdone.inc()
                    sg_free[pbank] = h_bank[pbank]
                w_h_span = hdone.n

                for stt in range(4):
                    ts0 = stt * 128
                    xbuf = subt % 2
                    if x_free[xbuf]:
                        PY.append(lambda y, w=x_free[xbuf]: y.wait_ge(vout.h, w))

                    def _lx(y, b=b, t0=t0, ts0=ts0, xbuf=xbuf):
                        y.dma_start(out=xres[xbuf][:, :],
                                    in_=x_in[b, t0 + ts0:t0 + ts0 + 128, :]
                                    ).then_inc(ld.h, 16)
                    PY.append(_lx)
                    w_x = ld.inc(16)

                    def _two(t, obuf=obuf, ts0=ts0, w_oT=w_oT, wv=wo_prev):
                        t.wait_ge(ld.h, max(w_oT, w_w))
                        if wv:
                            t.wait_ge(vout.h, wv)
                        for dh2 in range(2):
                            for c in range(8):
                                ins = t.matmul(
                                    ps_wo[:, dh2 * 512:(dh2 + 1) * 512],
                                    oT_ld[obuf][:, c * KT + ts0:
                                                c * KT + ts0 + 128],
                                    woT[:, c * D + dh2 * 512:
                                        c * D + dh2 * 512 + 512],
                                    start=(c == 0), stop=(c == 7))
                        ins.then_inc(womm.h, 1)
                    PT.append(_two)
                    oT_free[obuf] = womm.inc()

                    def _tdn(t, ts0=ts0, w=w_h_span, wg=dn_prev):
                        t.wait_ge(hdone.h, w)
                        if wg:
                            t.wait_ge(gsil.h, wg)
                        for dh2 in range(2):
                            for c in range(24):
                                ins = t.matmul(
                                    ps_dn[:, dh2 * 512:(dh2 + 1) * 512],
                                    hbuf[:, c * KT + ts0:c * KT + ts0 + 128],
                                    wdT[:, c * D + dh2 * 512:
                                        c * D + dh2 * 512 + 512],
                                    start=(c == 0), stop=(c == 23))
                        ins.then_inc(dnmm.h, 1)
                    PT.append(_tdn)
                    w_dn = dnmm.inc()

                    gb = subt % 2

                    def _sgl(s, w=w_dn, gb=gb, wgf=g_free[gb]):
                        s.wait_ge(dnmm.h, w)
                        if wgf:
                            s.wait_ge(vout.h, wgf)
                        s.activation(out=gbuf[gb][:, :], in_=ps_dn[:, :],
                                     func=ACT.Silu, bias=0.0,
                                     scale=1.0).then_inc(gsil.h, 1)
                    PS.append(_sgl)
                    w_gs = gsil.inc()
                    dn_prev = w_gs

                    def _vo(v, xbuf=xbuf, w_gs=w_gs, w_wo=oT_free[obuf],
                            w_x=w_x, wst=out_free[xbuf], gb=gb):
                        v.wait_ge(gsil.h, w_gs)
                        v.wait_ge(womm.h, w_wo)
                        v.wait_ge(ld.h, w_x)
                        if wst:
                            v.wait_ge(st.h, wst)
                        v.tensor_tensor(out=sgbuf[:, :], in0=ps_wo[:, :],
                                        in1=gbuf[gb][:, :], op=ALU.mult)
                        v.tensor_tensor(out=outb[xbuf][:, :], in0=sgbuf[:, :],
                                        in1=xres[xbuf][:, :],
                                        op=ALU.add).then_inc(vout.h, 1)
                    PV.append(_vo)
                    x_free[xbuf] = vout.inc()
                    wo_prev = x_free[xbuf]
                    g_free[gb] = x_free[xbuf]

                    def _so(y, b=b, t0=t0, ts0=ts0, xbuf=xbuf, w=x_free[xbuf]):
                        y.wait_ge(vout.h, w)
                        y.dma_start(out=out_d[b, t0 + ts0:t0 + ts0 + 128, :],
                                    in_=outb[xbuf][:, :]).then_inc(st.h, 16)
                    PY.append(_so)
                    out_free[xbuf] = st.inc(16)
                    subt += 1

            PY.append(lambda y: y.wait_ge(st.h, st.n))
            _run_plans(nc, PT, PV, PS, PY)

    return _finish(nc)


def _finish(nc):
    mybir.codegen_inst_isa_subclasses(nc)
    return nc


_NC_CACHE = {}


def _get_nc():
    if "nc" not in _NC_CACHE:
        _NC_CACHE["nc"] = build()
    return _NC_CACHE["nc"]


def kernel(L0_attn_x, L0_attn_Wqkv, L0_attn_Wo, L0_ffn_w_gate, L0_ffn_w_up,
           L0_ffn_w_down):
    x = np.ascontiguousarray(np.asarray(L0_attn_x, dtype=np.float32))
    wqkvv = np.ascontiguousarray(np.asarray(L0_attn_Wqkv, dtype=np.float32))
    wov = np.ascontiguousarray(np.asarray(L0_attn_Wo, dtype=np.float32))
    wgv = np.ascontiguousarray(np.asarray(L0_ffn_w_gate, dtype=np.int32))
    wuv = np.ascontiguousarray(np.asarray(L0_ffn_w_up, dtype=np.int32))
    wdv = np.ascontiguousarray(np.asarray(L0_ffn_w_down, dtype=np.int32))
    ident, identf, lens = _host_consts()

    nc = _get_nc()
    in_maps = []
    for c in range(8):
        in_maps.append({
            "x": x[c * NB:(c + 1) * NB],
            "wqkv": wqkvv, "wo": wov, "wg": wgv, "wu": wuv, "wd": wdv,
            "ident": ident, "identf": identf, "lens": lens,
        })
    res = run_bass_kernel_spmd(nc, in_maps, core_ids=list(range(8)))
    out = np.concatenate([res.results[c]["out"] for c in range(8)], axis=0)
    return out


# version 04
# speedup vs baseline: 1.1038x; 1.0242x over previous
import numpy as np
import ml_dtypes
from contextlib import ExitStack
from operator import add as _op_add

import concourse.bass as bass
import concourse.mybir as mybir
from concourse.bass_utils import run_bass_kernel_spmd

import concourse.dve_ops as dvo
from concourse.dve_op import DveOp
from concourse.dve_spec import Spec, AluOp, Bin, Src0, C0, C1, Zero, lower, sq
from concourse.dve_ops_raw import _compute_dtype

F32 = mybir.dt.float32
BF16 = mybir.dt.bfloat16
I32 = mybir.dt.int32
AX = mybir.AxisListType.X
ALU = mybir.AluOpType
ACT = mybir.ActivationFunctionType

NB = 2
N = 2048
D = 1024
H = 16
DH = 64
HID = 3072
NQB = N // 128
KT = 512
QPRE = float(2.0 ** 27)
SPOST = float(2.0 ** -30)

K_R = -4.25
K2 = float(np.float32(K_R * K_R))
C0_R = float(np.float32(2.002 * K_R))
C1_R = float(np.float32(2.0 * K_R * K_R))
K3 = float(np.float64(K_R) ** 3)
NEG_KR = float(-K_R)


def _ref_recip7(in0, in1, c0, c1, c2):
    f = _compute_dtype(in0)
    seed = np.asarray(c1, f).reshape(-1, 1) if hasattr(c1, "reshape") else c1
    if f is np.float64:
        out = np.float64(K3 * K_R) / in0
        return out, seed + out.reshape(out.shape[0], -1).sum(
            axis=-1, keepdims=True)
    x = np.ascontiguousarray(in0, dtype=np.float32)
    n = (~x.view(np.int32)).view(np.float32)
    z1 = (n * (np.float32(c0) - x * n).astype(np.float32)).astype(np.float32)
    out = (z1 * (np.float32(c2) - x * z1).astype(np.float32)).astype(np.float32)
    return out, seed + out.reshape(out.shape[0], -1).astype(np.float32).sum(
        axis=-1, keepdims=True)


def _ref_sq_sum(in0, in1, c0, c1, c2):
    f = _compute_dtype(in0)
    seed = np.asarray(c0, f).reshape(-1, 1) if hasattr(c0, "reshape") else c0
    b = np.square(in0.astype(f))
    return b, seed + b.reshape(b.shape[0], -1).sum(axis=-1, keepdims=True)


def _mk_ops():
    from concourse.dve_spec import C2
    n = Bin(AluOp.BITWISE_NOT, Src0, Src0)
    z1 = n * (C0 - Src0 * n)
    body = z1 * (C2 - Src0 * z1)
    spec_r = Spec(body=body, accum=_op_add, accum_init=C1)
    sha_r = lower(spec_r, ver="v3", perf_en=False, name="RECIP7_SUM").sha("v3")
    op_r = DveOp("RECIP7_SUM", spec_r, subdim=False, uops_sha={"v3": sha_r},
                 reference=_ref_recip7)
    spec_s = Spec(body=sq(Src0), accum=_op_add, accum_init=C0)
    sha_s = lower(spec_s, ver="v3", perf_en=False, name="SQ_SUM").sha("v3")
    op_s = DveOp("SQ_SUM", spec_s, subdim=False, uops_sha={"v3": sha_s},
                 reference=_ref_sq_sum)
    return op_r, op_s


def _register(op):
    if op.name in dvo._SUB_OPCODE_FOR_NAME:
        return
    dvo.OPS.append(op)
    dvo._SUB_OPCODE_FOR_NAME[op.name] = max(dvo._SUB_OPCODE_FOR_NAME.values()) + 1
    assert dvo._SUB_OPCODE_FOR_NAME[op.name] < 0x20
    dvo.CUSTOM_DVE_SPECS[op.name] = op.spec
    dvo.CUSTOM_DVE_REFS[op.name] = op.ref


RECIP7_SUM, SQ_SUM = _mk_ops()
_register(RECIP7_SUM)
_register(SQ_SUM)


def _host_consts():
    ident = np.eye(128).astype(ml_dtypes.bfloat16)
    identf = np.eye(128).astype(np.float32)
    lens = np.zeros((128, NQB * 4 + 2), dtype=np.float32)
    p = np.arange(128)
    for qb in range(NQB):
        for kt in range(4):
            lens[:, qb * 4 + kt] = np.clip(128 * qb + p + 1 - KT * kt, 0, KT)
    lens[:, NQB * 4] = 1e-5
    lens[:, NQB * 4 + 1] = -K3
    return ident, identf, lens


class Sem:
    __slots__ = ("h", "n")

    def __init__(self, handle):
        self.h = handle
        self.n = 0

    def inc(self, k=1):
        self.n += k
        return self.n


def tiles_of(qb):
    return qb // 4 + 1


def _run_plans(nc, PT, PV, PS, PY, PG=None):
    with nc.Block() as blk:
        blk.tensor(lambda t: [f(t) for f in PT])
        blk.vector(lambda v: [f(v) for f in PV])
        blk.scalar(lambda s: [f(s) for f in PS])
        blk.sync(lambda y: [f(y) for f in PY])
        if PG:
            blk.gpsimd(lambda g_: [f(g_) for f in PG])


def build(dbg=False, phases=3):
    nc = bass.Bass("TRN2", target_bir_lowering=False,
                   dynamic_dma_scratch_size=2048)

    x_in = nc.dram_tensor("x", [NB, N, D], F32, kind="ExternalInput")
    wqkv = nc.dram_tensor("wqkv", [3 * D, D], F32, kind="ExternalInput")
    wo = nc.dram_tensor("wo", [D, D], F32, kind="ExternalInput")
    wg = nc.dram_tensor("wg", [HID, D], I32, kind="ExternalInput")
    wu = nc.dram_tensor("wu", [HID, D], I32, kind="ExternalInput")
    wd = nc.dram_tensor("wd", [D, HID], I32, kind="ExternalInput")
    ident_d = nc.dram_tensor("ident", [128, 128], BF16, kind="ExternalInput")
    identf_d = nc.dram_tensor("identf", [128, 128], F32, kind="ExternalInput")
    lens_d = nc.dram_tensor("lens", [128, NQB * 4 + 2], F32, kind="ExternalInput")
    out_d = nc.dram_tensor("out", [NB, N, D], F32, kind="ExternalOutput")

    dk = "ExternalOutput" if dbg else "Internal"
    oT_sc = nc.dram_tensor("oT_sc", [NB, H, DH, N], BF16, kind=dk)
    wqkvT_sc = nc.dram_tensor("wqkvT_sc", [D, 3 * D], BF16, kind=dk)
    woT_sc = nc.dram_tensor("woT_sc", [D, D], BF16, kind=dk)
    wgT_sc = nc.dram_tensor("wgT_sc", [D, HID], BF16, kind=dk)
    wuT_sc = nc.dram_tensor("wuT_sc", [D, HID], BF16, kind=dk)
    wdT_sc = nc.dram_tensor("wdT_sc", [HID, D], BF16, kind=dk)
    if dbg:
        xn_dbg = nc.dram_tensor("xn_dbg", [128, 8 * NB * N], BF16,
                                kind="ExternalOutput")
        dbg_st_d = nc.dram_tensor("dbg_st", [128, 32 * 8], F32,
                                  kind="ExternalOutput")
        dbg_sgrp = nc.dram_tensor("dbg_sgrp", [2, 128, 28 * KT], F32,
                                  kind="ExternalOutput")
        dbg_rmax = nc.dram_tensor("dbg_rmax", [2, 128, 32], F32,
                                  kind="ExternalOutput")
        dbg_lam = nc.dram_tensor("dbg_lam", [2, 128, 8], F32,
                                 kind="ExternalOutput")
        dbg_rn = nc.dram_tensor("dbg_rn", [2, 128, 8], F32,
                                kind="ExternalOutput")
        dbg_q = nc.dram_tensor("dbg_q", [128, N], BF16, kind="ExternalOutput")
        dbg_it = nc.dram_tensor("dbg_it", [2, 6, 128, 16], F32,
                                kind="ExternalOutput")
        dbg_p = nc.dram_tensor("dbg_p", [128, KT], BF16, kind="ExternalOutput")
        dbg_pt = nc.dram_tensor("dbg_pt", [128, KT], BF16, kind="ExternalOutput")
        dbg_pp = nc.dram_tensor("dbg_pp", [2, 6, 128, 16], F32,
                                kind="ExternalOutput")
        dbg_k = nc.dram_tensor("dbg_k", [128, N], BF16, kind="ExternalOutput")
        dbg_v = nc.dram_tensor("dbg_v", [128, N], BF16, kind="ExternalOutput")

    with ExitStack() as outer:
        oe = outer.enter_context
        xn = oe(nc.sbuf_tensor("xn", [128, 8 * NB * N], BF16))
        ident = oe(nc.sbuf_tensor("ident_sb", [128, 128], BF16))
        identf = oe(nc.sbuf_tensor("identf_sb", [128, 128], F32))
        lens_sb = oe(nc.sbuf_tensor("lens_sb", [128, NQB * 4 + 2], F32))

        def mk_sems(names):
            return {k: Sem(oe(nc.semaphore(k))) for k in names}

        S1 = mk_sems(["p1_ld", "p1_v", "p1_s", "p1_t", "p1_st"])
        S2 = mk_sems(["p2_ld", "p2_qmm", "p2_qev", "p2_smm", "p2_sfree",
                      "p2_pr", "p2_tpd", "p2_pte", "p2_pvd", "p2_opv",
                      "p2_oev", "p2_odma"])
        S3 = mk_sems(["p3_ld", "p3_gmm", "p3_hdone", "p3_sgd", "p3_womm",
                      "p3_dnmm", "p3_gsil", "p3_vout", "p3_st"])

        with ExitStack() as p1:
            e1 = p1.enter_context
            stg_f = [e1(nc.sbuf_tensor(f"stg_f{i}", [128, D], F32)) for i in range(2)]
            stg_i = [e1(nc.sbuf_tensor(f"stg_i{i}", [128, HID], I32)) for i in range(2)]
            wcast = [e1(nc.sbuf_tensor(f"wcast{i}", [128, HID], BF16)) for i in range(2)]
            wstage = [e1(nc.sbuf_tensor(f"wstage{i}", [128, 1024], BF16)) for i in range(2)]
            xt = [e1(nc.sbuf_tensor(f"xt{i}", [128, D], F32)) for i in range(8)]
            xn_tok = e1(nc.sbuf_tensor("xn_tok", [128, D], BF16))
            sqw = e1(nc.sbuf_tensor("sqw1", [128, D], F32))
            stat = e1(nc.sbuf_tensor("stat", [128, 64], F32))
            dbg_sb = e1(nc.sbuf_tensor("dbg_sb", [128, 32 * 8], F32)) if dbg else None
            xbb = [e1(nc.sbuf_tensor(f"xbb{i}", [128, D], BF16)) for i in range(2)]
            xfmb = [e1(nc.sbuf_tensor(f"xfmb{i}", [128, D], BF16)) for i in range(2)]
            x2fmb = [e1(nc.sbuf_tensor(f"x2fmb{i}", [128, D], BF16)) for i in range(2)]
            strow = [e1(nc.sbuf_tensor(f"strow{i}", [1, 256], F32)) for i in range(2)]
            onesb = e1(nc.sbuf_tensor("onesb", [128, 2], BF16))
            ps_t = [e1(nc.psum_tensor(f"ps_t{i}", [128, 1024], BF16)) for i in range(2)]
            ps_t2 = [e1(nc.psum_tensor(f"ps_t2{i}", [128, 1024], BF16)) for i in range(2)]
            ps_st = [e1(nc.psum_tensor(f"ps_st{i}", [2, 256], F32)) for i in range(2)]
            ps_tt = [e1(nc.psum_tensor(f"ps_tt{i}", [128, 2], F32)) for i in range(2)]

            ld, vv, ss, tt, st = (S1["p1_ld"], S1["p1_v"], S1["p1_s"],
                                  S1["p1_t"], S1["p1_st"])
            PT, PV, PS, PY = [], [], [], []

            PY.append(lambda y: y.dma_start(out=ident[:, :],
                                            in_=ident_d[:, :]).then_inc(ld.h, 16))
            ld.inc(16)
            PY.append(lambda y: y.dma_start(out=lens_sb[:, :],
                                            in_=lens_d[:, :]).then_inc(ld.h, 16))
            ld.inc(16)
            PY.append(lambda y: y.dma_start(out=identf[:, :],
                                            in_=identf_d[:, :]).then_inc(ld.h, 16))
            ld.inc(16)
            PT.append(lambda t: t.wait_ge(ld.h, 48))

            def _swarm(s):
                s.wait_ge(ld.h, 48)
                s.memzero(sqw[:, :])
                s.activation(out=sqw[:, :], in_=sqw[:, :], func=ACT.Copy,
                             bias=0.0, scale=1.0)
                s.activation(out=sqw[:, :], in_=sqw[:, :], func=ACT.Square,
                             bias=0.0, scale=1.0)
                s.activation(out=sqw[:, :], in_=sqw[:, :], func=ACT.Exp,
                             bias=0.0, scale=0.0)
                s.activation(out=sqw[:, :], in_=sqw[:, :], func=ACT.Ln,
                             bias=lens_sb[:, NQB * 4:NQB * 4 + 1], scale=0.0)
                s.activation(out=sqw[:, :], in_=sqw[:, :], func=ACT.Exp,
                             bias=0.0, scale=0.0)
                s.memzero(sqw[:, :])
            PS.append(_swarm)

            def _vmo(v):
                v.memset(onesb[:, 0:1], 1.0)
                v.memset(onesb[:, 1:2], 0.0).then_inc(vv.h, 1)
            PV.append(_vmo)
            vv.inc()
            PT.append(lambda t: t.wait_ge(vv.h, 1))
            def _twarm(t):
                t.wait_ge(vv.h, 1)
                for pb in range(2):
                    for c in range(4):
                        t.matmul(ps_st[pb][0:2, 0:128], onesb[:, :],
                                 ident[:, 0:128], start=(c == 0), stop=(c == 3))
                    for c in range(4):
                        t.matmul(ps_st[pb][0:2, 128:256], onesb[:, :],
                                 ident[:, 0:128], start=(c == 0), stop=(c == 3))
                    t.transpose(ps_t[pb][:, 0:128], ident[:, :], ident[:, :])
                    t.transpose(ps_t2[pb][:, 0:128], ident[:, :], ident[:, :])
                    t.transpose(ps_tt[pb][:, 0:1], identf[0:1, 0:128],
                                identf[0:1, 0:1])
                ins = t.matmul(ps_st[0][0:2, 0:128], onesb[:, :],
                               ident[:, 0:128], start=True, stop=True)
                ins.then_inc(tt.h, 1)
            PT.append(_twarm)
            tt.inc()
            PV.append(lambda v: v.wait_ge(tt.h, 1))

            v_done = {i: 0 for i in range(8)}
            xb_free = {0: 0, 1: 0}
            s_evacA = {0: 0, 1: 0}
            s_strow = {0: 0, 1: 0}
            s0_done = {0: 0, 1: 0}
            stat_free = {i: 0 for i in range(8)}
            s_evac2 = {0: 0, 1: 0}
            xn_free = [0]
            ntile = NB * N // 128
            w_s3_of = {}

            def ln_front(j):
                b, t0 = j // (N // 128), (j % (N // 128)) * 128
                buf = j % 8
                pb = j % 2
                base = (j % 8) * 8
                if v_done[buf]:
                    PY.append(lambda y, w=v_done[buf]: y.wait_ge(vv.h, w))
                PY.append(lambda y, b=b, t0=t0, buf=buf: y.dma_start(
                    out=xt[buf][:, :],
                    in_=x_in[b, t0:t0 + 128, :]).then_inc(ld.h, 16))
                myld = ld.inc(16)

                def _vc(v, buf=buf, pb=pb, myld=myld, wtf=xb_free[pb]):
                    v.wait_ge(ld.h, myld)
                    if wtf:
                        v.wait_ge(tt.h, wtf)
                    v.tensor_copy(out=xbb[pb][:, :],
                                  in_=xt[buf][:, :]).then_inc(vv.h, 1)
                PV.append(_vc)
                w_cast = vv.inc()

                def _t1(t, pb=pb, w=w_cast, wse=s_evacA[pb]):
                    t.wait_ge(vv.h, w)
                    if wse:
                        t.wait_ge(ss.h, wse)
                    for c in range(8):
                        ins = t.transpose(ps_t[pb][:, c * 128:(c + 1) * 128],
                                          xbb[pb][:, c * 128:(c + 1) * 128],
                                          ident[:, :])
                    ins.then_inc(tt.h, 1)
                PT.append(_t1)
                w_t1 = tt.inc()
                xb_free[pb] = w_t1

                def _s1(s, pb=pb, w=w_t1):
                    s.wait_ge(tt.h, w)
                    s.copy(out=xfmb[pb][:, :], in_=ps_t[pb][:, :])
                    s.activation(out=x2fmb[pb][:, :], in_=xfmb[pb][:, :],
                                 func=ACT.Square, bias=0.0,
                                 scale=1.0).then_inc(ss.h, 1)
                PS.append(_s1)
                w_s1 = ss.inc()
                s_evacA[pb] = w_s1

                def _t2(t, pb=pb, w=w_s1, wsr=s_strow[pb]):
                    t.wait_ge(ss.h, w)
                    if wsr:
                        t.wait_ge(ss.h, wsr)
                    for c in range(8):
                        t.matmul(ps_st[pb][0:2, 0:128], onesb[:, :],
                                 xfmb[pb][:, c * 128:(c + 1) * 128],
                                 start=(c == 0), stop=(c == 7))
                    for c in range(8):
                        ins = t.matmul(ps_st[pb][0:2, 128:256], onesb[:, :],
                                       x2fmb[pb][:, c * 128:(c + 1) * 128],
                                       start=(c == 0), stop=(c == 7))
                    ins.then_inc(tt.h, 1)
                PT.append(_t2)
                w_t2 = tt.inc()

                def _s2(s, pb=pb, w=w_t2):
                    s.wait_ge(tt.h, w)
                    s.copy(out=strow[pb][0:1, :],
                           in_=ps_st[pb][0:1, :]).then_inc(ss.h, 1)
                PS.append(_s2)
                w_s2 = ss.inc()
                s_strow[pb] = w_s2

                def _t3(t, pb=pb, w=w_s2, wst=s0_done[pb]):
                    t.wait_ge(ss.h, w)
                    if wst:
                        t.wait_ge(ss.h, wst)
                    t.transpose(ps_tt[pb][:, 0:1], strow[pb][0:1, 0:128],
                                identf[0:1, 0:1])
                    t.transpose(ps_tt[pb][:, 1:2], strow[pb][0:1, 128:256],
                                identf[0:1, 0:1]).then_inc(tt.h, 1)
                PT.append(_t3)
                w_t3 = tt.inc()

                def _s0(s, pb=pb, w=w_t3, base=base, wsf=stat_free[j % 8]):
                    s.wait_ge(tt.h, w)
                    if wsf:
                        s.wait_ge(vv.h, wsf)
                    s.copy(out=stat[:, base + 0:base + 2],
                           in_=ps_tt[pb][:, 0:2]).then_inc(ss.h, 1)
                PS.append(_s0)
                s0_done[pb] = ss.inc()

            def ln_stage(j, k):
                base = (j % 8) * 8

                def _sk(s, base=base, k=k, j=j):
                    if k == 1:
                        s.activation(out=stat[:, base + 2:base + 3],
                                     in_=stat[:, base + 0:base + 1],
                                     func=ACT.Copy, bias=0.0, scale=1.0 / D)
                    elif k == 2:
                        s.activation(out=stat[:, base + 3:base + 4],
                                     in_=stat[:, base + 2:base + 3],
                                     func=ACT.Square, bias=0.0, scale=1.0)
                    elif k == 3:
                        s.activation(out=stat[:, base + 4:base + 5],
                                     in_=stat[:, base + 3:base + 4],
                                     func=ACT.Copy, bias=0.0, scale=-1.0)
                    elif k == 4:
                        s.activation(out=stat[:, base + 5:base + 6],
                                     in_=stat[:, base + 1:base + 2],
                                     func=ACT.Identity,
                                     bias=stat[:, base + 4:base + 5],
                                     scale=1.0 / D)
                    elif k == 5:
                        s.activation(out=stat[:, base + 6:base + 7],
                                     in_=stat[:, base + 5:base + 6], func=ACT.Ln,
                                     bias=lens_sb[:, NQB * 4:NQB * 4 + 1],
                                     scale=1.0)
                    else:
                        ins = s.activation(out=stat[:, base + 7:base + 8],
                                           in_=stat[:, base + 6:base + 7],
                                           func=ACT.Exp, bias=0.0, scale=-0.5)
                        if dbg:
                            s.copy(out=dbg_sb[:, j * 8:(j + 1) * 8],
                                   in_=stat[:, base:base + 8])
                        ins.then_inc(ss.h, 1)
                PS.append(_sk)
                if k == 6:
                    w_s3_of[j] = ss.inc()

            def ln_back(j):
                b, t0 = j // (N // 128), (j % (N // 128)) * 128
                buf = j % 8
                pb = j % 2
                base = (j % 8) * 8

                def _vn(v, buf=buf, w=w_s3_of[j], base=base, wxf=xn_free[0]):
                    v.wait_ge(ss.h, w)
                    if wxf:
                        v.wait_ge(tt.h, wxf)
                    v.tensor_scalar(out=xn_tok[:, :], in0=xt[buf][:, :],
                                    scalar1=stat[:, base + 2:base + 3],
                                    scalar2=stat[:, base + 7:base + 8],
                                    op0=ALU.subtract,
                                    op1=ALU.mult).then_inc(vv.h, 1)
                PV.append(_vn)
                v_done[buf] = vv.inc()

                def _t4(t, pb=pb, wv=v_done[buf], wse=s_evac2[pb]):
                    t.wait_ge(vv.h, wv)
                    if wse:
                        t.wait_ge(ss.h, wse)
                    for c in range(8):
                        ins = t.transpose(ps_t2[pb][:, c * 128:(c + 1) * 128],
                                          xn_tok[:, c * 128:(c + 1) * 128],
                                          ident[:, :])
                    ins.then_inc(tt.h, 1)
                PT.append(_t4)
                w_t4 = tt.inc()
                xn_free[0] = w_t4

                def _s4(s, pb=pb, w=w_t4, b=b, t0=t0):
                    s.wait_ge(tt.h, w)
                    for c in range(8):
                        col = c * (NB * N) + b * N + t0
                        ins = s.copy(out=xn[:, col:col + 128],
                                     in_=ps_t2[pb][:, c * 128:(c + 1) * 128])
                    ins.then_inc(ss.h, 1)
                PS.append(_s4)
                s_evac2[pb] = ss.inc()
                stat_free[buf] = v_done[buf]

            for jj in range(ntile + 7):
                if jj < ntile:
                    ln_front(jj)
                for k in range(1, 7):
                    j2 = jj - k
                    if 0 <= j2 < ntile:
                        ln_stage(j2, k)
                if 0 <= jj - 6 < ntile:
                    ln_back(jj - 6)

            wv_done = {0: 0, 1: 0}
            wst_done = {0: 0, 1: 0}

            def prep_weight(src, Rr, Cc, is_int, dst):
                nrc = Rr // 128
                ncc = Cc // 128
                for rc in range(nrc):
                    buf = rc % 2
                    stg = stg_i[buf] if is_int else stg_f[buf]
                    if wv_done[buf]:
                        PY.append(lambda y, w=wv_done[buf]: y.wait_ge(vv.h, w))
                    PY.append(lambda y, src=src, rc=rc, stg=stg, Cc=Cc:
                              y.dma_start(out=stg[:, 0:Cc],
                                          in_=src[rc * 128:(rc + 1) * 128, :])
                              .then_inc(ld.h, 16))
                    myld = ld.inc(16)

                    def _vc(v, stg=stg, buf=buf, myld=myld, is_int=is_int, Cc=Cc):
                        v.wait_ge(ld.h, myld)
                        v.tensor_scalar(out=wcast[buf][:, 0:Cc], in0=stg[:, 0:Cc],
                                        scalar1=(1.0 / 127.0) if is_int else 1.0,
                                        scalar2=None, op0=ALU.mult).then_inc(vv.h, 1)
                    PV.append(_vc)
                    wv_done[buf] = vv.inc()

                    for cg in range(0, ncc, 8):
                        nblk = min(8, ncc - cg)

                        def _tt(t, buf=buf, wvc=wv_done[buf], cg=cg, nblk=nblk,
                                wse=s_evacA[buf]):
                            t.wait_ge(vv.h, wvc)
                            if wse:
                                t.wait_ge(ss.h, wse)
                            for j in range(nblk):
                                c = cg + j
                                ins = t.transpose(
                                    ps_t[buf][:, j * 128:(j + 1) * 128],
                                    wcast[buf][:, c * 128:(c + 1) * 128],
                                    ident[:, :])
                            ins.then_inc(tt.h, 1)
                        PT.append(_tt)
                        w_tp = tt.inc()

                        def _se(s, buf=buf, w=w_tp, nblk=nblk, wst=wst_done[buf]):
                            s.wait_ge(tt.h, w)
                            if wst:
                                s.wait_ge(st.h, wst)
                            s.copy(out=wstage[buf][:, 0:nblk * 128],
                                   in_=ps_t[buf][:, 0:nblk * 128]).then_inc(ss.h, 1)
                        PS.append(_se)
                        s_evacA[buf] = ss.inc()

                        def _yd(y, buf=buf, w=s_evacA[buf], cg=cg, nblk=nblk,
                                rc=rc, dst=dst):
                            y.wait_ge(ss.h, w)
                            for j in range(nblk):
                                c = cg + j
                                y.dma_start(
                                    out=dst[c * 128:(c + 1) * 128,
                                            rc * 128:(rc + 1) * 128],
                                    in_=wstage[buf][:, j * 128:(j + 1) * 128]
                                ).then_inc(st.h, 16)
                        PY.append(_yd)
                        wst_done[buf] = st.inc(16 * nblk)

            prep_weight(wqkv, 3 * D, D, False, wqkvT_sc)
            prep_weight(wo, D, D, False, woT_sc)
            prep_weight(wg, HID, D, True, wgT_sc)
            prep_weight(wu, HID, D, True, wuT_sc)
            prep_weight(wd, D, HID, True, wdT_sc)

            if dbg:
                PY.append(lambda y: y.wait_ge(ss.h, ss.n))
                PY.append(lambda y: y.dma_start(out=xn_dbg[:, :],
                                                in_=xn[:, :]).then_inc(st.h, 16))
                st.inc(16)
                PY.append(lambda y: y.dma_start(out=dbg_st_d[:, :],
                                                in_=dbg_sb[:, :]).then_inc(st.h, 16))
                st.inc(16)
            PY.append(lambda y: y.wait_ge(st.h, st.n))
            PY.append(lambda y: y.wait_ge(ss.h, ss.n))
            _run_plans(nc, PT, PV, PS, PY)

        if phases < 2:
            return _finish(nc)
        with ExitStack() as p2:
            e2 = p2.enter_context
            wqkvT = e2(nc.sbuf_tensor("wqkvT", [128, 8 * 3 * D], BF16))
            qpair = e2(nc.sbuf_tensor("qpair", [128, N], BF16))
            kpair = e2(nc.sbuf_tensor("kpair", [128, N], BF16))
            vpair = e2(nc.sbuf_tensor("vpair", [128, N], BF16))
            sgrp = e2(nc.sbuf_tensor("sgrp", [128, 28 * KT], F32))
            dscr = [e2(nc.sbuf_tensor(f"dscr{i}", [128, KT], F32))
                    for i in range(3)]
            rstrip = e2(nc.sbuf_tensor("rstrip", [128, 12 * KT], F32))
            pbf = [e2(nc.sbuf_tensor(f"pbf{i}", [128, KT], BF16)) for i in range(2)]
            ptsb = [e2(nc.sbuf_tensor(f"ptsb{i}", [128, KT], BF16)) for i in range(2)]
            othead = e2(nc.sbuf_tensor("othead", [64, N], BF16))
            rowmaxP = e2(nc.sbuf_tensor("rowmaxP", [128, 32], F32))
            lam = e2(nc.sbuf_tensor("lam", [128, 16], F32))
            lamK = e2(nc.sbuf_tensor("lamK", [128, 8], F32))
            srd = e2(nc.sbuf_tensor("srd", [128, 16], F32))
            sr2 = e2(nc.sbuf_tensor("sr2", [128, 8], F32))
            nt1 = e2(nc.sbuf_tensor("nt1", [128, 8], F32))
            nt2 = e2(nc.sbuf_tensor("nt2", [128, 8], F32))
            partR = e2(nc.sbuf_tensor("partR", [128, 8], F32))
            partR2 = e2(nc.sbuf_tensor("partR2", [128, 8], F32))
            partF = e2(nc.sbuf_tensor("partF", [128, 12], F32))
            junk1 = e2(nc.sbuf_tensor("junk1", [128, 1], F32))
            rnorm = e2(nc.sbuf_tensor("rnorm", [128, 8], F32))
            rnsc = e2(nc.sbuf_tensor("rnsc", [128, 8], F32))

            ps_q = [e2(nc.psum_tensor(f"ps_q{i}", [128, KT], F32)) for i in range(2)]
            ps_s = [e2(nc.psum_tensor(f"ps_s{i}", [128, KT], F32)) for i in range(2)]
            ps_pt = [e2(nc.psum_tensor(f"ps_pt{i}", [128, KT], BF16)) for i in range(2)]
            ps_o = [e2(nc.psum_tensor(f"ps_o{i}", [64, 128], F32)) for i in range(2)]

            ld, qmm, qev, smm, sfree = (S2["p2_ld"], S2["p2_qmm"], S2["p2_qev"],
                                        S2["p2_smm"], S2["p2_sfree"])
            pr, tpd, pte, pvd, opv = (S2["p2_pr"], S2["p2_tpd"], S2["p2_pte"],
                                      S2["p2_pvd"], S2["p2_opv"])
            oev, odma = S2["p2_oev"], S2["p2_odma"]
            PT, PV, PS, PY = [], [], [], []
            dbgs = Sem(e2(nc.semaphore("p2_dbg")))
            dbgv = Sem(e2(nc.semaphore("p2_dbgv")))
            uvm = Sem(e2(nc.semaphore("p2_uvm")))
            gup = Sem(e2(nc.semaphore("p2_gup")))
            snu = Sem(e2(nc.semaphore("p2_snu")))
            fsy = Sem(e2(nc.semaphore("p2_fsy")))
            srn = Sem(e2(nc.semaphore("p2_srn")))
            sdp = Sem(e2(nc.semaphore("p2_sdp")))
            vdf = Sem(e2(nc.semaphore("p2_vdf")))
            gup_need = {0: 0, 1: 0}
            PG = []

            def _ldw(y):
                for c in range(8):
                    y.dma_start(
                        out=wqkvT[:, c * (3 * D):(c + 1) * (3 * D)],
                        in_=wqkvT_sc[c * 128:(c + 1) * 128, :]).then_inc(ld.h, 16)
            PY.append(_ldw)
            w_wqkv = ld.inc(16 * 8)

            WARMUP = False
            def _vwarm(v):
                v.memset(dscr[0][:, :], 1.0)
                v.memset(partR2[:, 0:8], 1.0)
                v.memset(lam[:, :], 1.0)
                v.memset(nt1[:, :], 1.0)
                v.memset(nt2[:, :], 1.0)
                v.tensor_mask_reduce(
                    out=rstrip[:, 0:KT], in_=dscr[0][:, :], mask_start=0.0,
                    mask_end=lens_sb[:, 3:4], scale=1.0, accum_in=-1e30,
                    op=ALU.max, accum_out=partF[:, 0:1])
                v._custom_dve(RECIP7_SUM, out=rstrip[:, 0:KT], in0=dscr[0][:, :],
                              s0=C0_R, s1=0.0, imm2=C1_R,
                              accum_out=partR[:, 0:1])
                v._custom_dve(SQ_SUM, out=rstrip[:, KT:2 * KT],
                              in0=rstrip[:, 0:KT], s0=0.0, s1=0.0,
                              accum_out=partR[:, 1:2])
                v.reciprocal_approx_accurate(out=nt2[:, 0:4],
                                             in_=partR2[:, 0:4],
                                             scratch=srd[:, 0:4])
                v.tensor_copy(out=junk1[:, :],
                              in_=junk1[:, :]).then_inc(uvm.h, 1)
            if WARMUP:
                PV.append(_vwarm)
                uvm.inc()

            def _gwarm(g_):
                g_.wait_ge(uvm.h, 1)
                g_.tensor_tensor(out=sr2[:, 0:4], in0=nt1[:, 0:4],
                                 in1=nt2[:, 0:4], op=ALU.mult)
                g_.tensor_tensor(out=lam[:, 8:12], in0=lam[:, 0:4],
                                 in1=sr2[:, 0:4], op=ALU.add)
                g_.tensor_scalar(out=lamK[:, 0:4], in0=lam[:, 8:12],
                                 scalar1=K_R, scalar2=None,
                                 op0=ALU.mult).then_inc(gup.h, 1)
            if WARMUP:
                PG.append(_gwarm)
                gup.inc()
                PV.append(lambda v: v.wait_ge(gup.h, 1))

            qev_bank = {0: 0, 1: 0}
            pte_bank = {0: 0, 1: 0}
            pvd_slot = {0: 0, 1: 0}
            sfree_tile = {0: 0, 1: 0}
            oev_row = {0: 0, 1: 0}
            odma_buf = {0: 0, 1: 0}
            tpd_slot = {0: 0, 1: 0}

            state = {"qkv_tile": 0, "s_tile": 0, "p_tile": 0, "d_tile": 0}
            vdf_slot = {0: 0, 1: 0, 2: 0}

            def qkv_pair(b, hp):
                for jbase, scl, dstt in ((128 * hp, QPRE, qpair),
                                         (D + 128 * hp, 1.0, kpair)):
                    for tc in range(4):
                        bank = state["qkv_tile"] % 2
                        t0 = tc * KT

                        def _tm(t, jbase=jbase, t0=t0, bank=bank, b=b,
                                wfree=qev_bank[bank]):
                            t.wait_ge(ld.h, w_wqkv)
                            if wfree:
                                t.wait_ge(qev.h, wfree)
                            for c in range(8):
                                ins = t.matmul(
                                    ps_q[bank][:, :],
                                    wqkvT[:, c * 3 * D + jbase:
                                          c * 3 * D + jbase + 128],
                                    xn[:, c * NB * N + b * N + t0:
                                       c * NB * N + b * N + t0 + KT],
                                    start=(c == 0), stop=(c == 7))
                            ins.then_inc(qmm.h, 1)
                        PT.append(_tm)
                        w_mm = qmm.inc()

                        def _sev(s, bank=bank, w=w_mm, dstt=dstt, t0=t0, scl=scl):
                            s.wait_ge(qmm.h, w)
                            s.activation(out=dstt[:, t0:t0 + KT],
                                         in_=ps_q[bank][:, :], func=ACT.Copy,
                                         bias=0.0, scale=scl).then_inc(qev.h, 1)
                        PS.append(_sev)
                        qev_bank[bank] = qev.inc()
                        state["qkv_tile"] += 1
                jbase = 2 * D + 128 * hp
                for tc in range(NQB):
                    bank = state["qkv_tile"] % 2
                    t0 = tc * 128

                    def _tv(t, jbase=jbase, t0=t0, bank=bank, b=b,
                            wfree=qev_bank[bank]):
                        if wfree:
                            t.wait_ge(qev.h, wfree)
                        for c in range(8):
                            ins = t.matmul(
                                ps_q[bank][:, 0:128],
                                xn[:, c * NB * N + b * N + t0:
                                   c * NB * N + b * N + t0 + 128],
                                wqkvT[:, c * 3 * D + jbase:
                                      c * 3 * D + jbase + 128],
                                start=(c == 0), stop=(c == 7))
                        ins.then_inc(qmm.h, 1)
                    PT.append(_tv)
                    w_mm = qmm.inc()

                    def _svev(s, bank=bank, w=w_mm, tc=tc):
                        s.wait_ge(qmm.h, w)
                        s.activation(out=vpair[:, tc * 128:(tc + 1) * 128],
                                     in_=ps_q[bank][:, 0:128], func=ACT.Copy,
                                     bias=0.0, scale=1.0).then_inc(qev.h, 1)
                    PS.append(_svev)
                    qev_bank[bank] = qev.inc()
                    state["qkv_tile"] += 1

            def attn_group(b, h, g, w_pair):
                par = h % 2
                qb0 = g * 8
                rows = list(range(qb0, qb0 + 8))
                halves = [rows[0:4], rows[4:8]]
                tiles_rm = [(qb, kt) for qb in rows for kt in range(tiles_of(qb))]
                slot = {t: i for i, t in enumerate(tiles_rm)}

                tiles_cm = [(qb, kt) for kt in range(4) for qb in rows
                            if kt < tiles_of(qb)]
                for qb, kt in tiles_cm:
                    bank = state["s_tile"] % 2
                    sl = slot[(qb, kt)]

                    def _ts(t, qb=qb, kt=kt, bank=bank, par=par,
                            wfree=sfree_tile[bank], w_pair=w_pair):
                        t.wait_ge(qev.h, w_pair)
                        if wfree:
                            t.wait_ge(sfree.h, wfree)
                        t.matmul(ps_s[bank][:, :],
                                 qpair[64 * par:64 * par + 64,
                                       qb * 128:(qb + 1) * 128],
                                 kpair[64 * par:64 * par + 64,
                                       kt * KT:(kt + 1) * KT],
                                 start=True, stop=True).then_inc(smm.h, 1)
                    PT.append(_ts)
                    w_smm = smm.inc()
                    first = (kt == 0)
                    rc = qb - qb0

                    def _vp(v, bank=bank, w=w_smm, sl=sl, qb=qb, kt=kt,
                            rc=rc):
                        v.wait_ge(smm.h, w)
                        v.tensor_mask_reduce(
                            out=sgrp[:, sl * KT:(sl + 1) * KT],
                            in_=ps_s[bank][:, :],
                            mask_start=0.0,
                            mask_end=lens_sb[:, qb * 4 + kt:qb * 4 + kt + 1],
                            scale=SPOST,
                            accum_in=(-1e30 if kt == 0
                                      else rowmaxP[:, rc:rc + 1]),
                            op=ALU.max,
                            accum_out=rowmaxP[:, rc:rc + 1]
                        ).then_inc(sfree.h, 1)
                    PV.append(_vp)
                    sfree_tile[bank] = sfree.inc()
                    state["s_tile"] += 1

                def emit_lambda_init(hf):
                    hb = hf * 4

                    def _vl(v):
                        v.tensor_copy(out=junk1[:, :],
                                      in_=junk1[:, :]).then_inc(uvm.h, 1)
                    PV.append(_vl)
                    wv = uvm.inc()

                    def _gl(g_, hb=hb, wv=wv):
                        g_.wait_ge(uvm.h, wv)
                        g_.tensor_scalar(out=lam[:, hb:hb + 4],
                                         in0=rowmaxP[:, hb:hb + 4],
                                         scalar1=1.0, scalar2=None,
                                         op0=ALU.add)
                        g_.tensor_scalar(out=lamK[:, hb:hb + 4],
                                         in0=lam[:, hb:hb + 4], scalar1=K_R,
                                         scalar2=None,
                                         op0=ALU.mult).then_inc(gup.h, 1)
                    PG.append(_gl)
                    gup_need[hf] = gup.inc()

                def emit_d_pass(sl, rc):
                    pb = state["d_tile"] % 3
                    state["d_tile"] += 1

                    def _sd(s, sl=sl, rc=rc, pb=pb, wg=gup_need[rc // 4],
                            wv=vdf_slot[pb]):
                        s.wait_ge(gup.h, wg)
                        if wv:
                            s.wait_ge(vdf.h, wv)
                        s.activation(out=dscr[pb][:, :],
                                     in_=sgrp[:, sl * KT:(sl + 1) * KT],
                                     func=ACT.Identity,
                                     bias=lamK[:, rc:rc + 1],
                                     scale=NEG_KR).then_inc(sdp.h, 1)
                    PS.append(_sd)
                    return pb, sdp.inc()

                def emit_iter_tiles(hf):
                    half = halves[hf]
                    nkt = tiles_of(half[0])
                    for kt in range(nkt):
                        for qb in half:
                            sl = slot[(qb, kt)]
                            rc = qb - qb0
                            pb, w_sd = emit_d_pass(sl, rc)

                            def _vi(v, sl=sl, rc=rc, kt=kt, pb=pb, w_sd=w_sd):
                                v.wait_ge(sdp.h, w_sd)
                                v._custom_dve(
                                    RECIP7_SUM,
                                    out=rstrip[:, kt * KT:(kt + 1) * KT],
                                    in0=dscr[pb][:, :], s0=C0_R,
                                    s1=(0.0 if kt == 0 else partR[:, rc:rc + 1]),
                                    imm2=C1_R,
                                    accum_out=partR[:, rc:rc + 1]
                                ).then_inc(vdf.h, 1)
                                v._custom_dve(
                                    SQ_SUM, out=rstrip[:, 4 * KT:5 * KT],
                                    in0=rstrip[:, kt * KT:(kt + 1) * KT],
                                    s0=(0.0 if kt == 0 else partR2[:, rc:rc + 1]),
                                    s1=0.0,
                                    accum_out=partR2[:, rc:rc + 1])
                            PV.append(_vi)
                            vdf_slot[pb] = vdf.inc()

                def emit_update_marker(hf):
                    def _vn(v):
                        v.tensor_copy(out=junk1[:, :],
                                      in_=junk1[:, :]).then_inc(uvm.h, 1)
                    PV.append(_vn)
                    return uvm.inc()

                def emit_update_sg(it, hf, wv):
                    hb = hf * 4
                    sb = 8 * (it % 2)
                    db = 8 * ((it + 1) % 2)

                    def _sn(s, hb=hb, wv=wv):
                        s.wait_ge(uvm.h, wv)
                        s.activation(out=srd[:, hb:hb + 4],
                                     in_=partR[:, hb:hb + 4],
                                     func=ACT.Identity,
                                     bias=lens_sb[:, NQB * 4 + 1:NQB * 4 + 2],
                                     scale=1.0)
                        s.activation(out=srd[:, 8 + hb:12 + hb],
                                     in_=partR2[:, hb:hb + 4],
                                     func=ACT.Ln, bias=0.0, scale=1.0)
                        s.activation(out=nt1[:, hb:hb + 4],
                                     in_=srd[:, hb:hb + 4], func=ACT.Copy,
                                     bias=0.0, scale=K3)
                        s.activation(out=nt2[:, hb:hb + 4],
                                     in_=srd[:, 8 + hb:12 + hb], func=ACT.Exp,
                                     bias=0.0, scale=-1.0).then_inc(snu.h, 1)
                    PS.append(_sn)
                    ws = snu.inc()

                    def _gn(g_, hb=hb, sb=sb, db=db, ws=ws):
                        g_.wait_ge(snu.h, ws)
                        g_.tensor_tensor(out=sr2[:, hb:hb + 4],
                                         in0=nt1[:, hb:hb + 4],
                                         in1=nt2[:, hb:hb + 4], op=ALU.mult)
                        g_.tensor_tensor(out=lam[:, db + hb:db + hb + 4],
                                         in0=lam[:, sb + hb:sb + hb + 4],
                                         in1=sr2[:, hb:hb + 4], op=ALU.add)
                        g_.tensor_scalar(out=lamK[:, hb:hb + 4],
                                         in0=lam[:, db + hb:db + hb + 4],
                                         scalar1=K_R, scalar2=None,
                                         op0=ALU.mult).then_inc(gup.h, 1)
                    PG.append(_gn)
                    gup_need[hf] = gup.inc()

                def emit_f1_row(qb):
                    rc = qb - qb0
                    nkt = tiles_of(qb)
                    ps = (qb % 3) * 4
                    for kt in range(nkt):
                        sl = slot[(qb, kt)]
                        pb, w_sd = emit_d_pass(sl, rc)

                        def _vf1(v, rc=rc, kt=kt, ps=ps, pb=pb, w_sd=w_sd):
                            v.wait_ge(sdp.h, w_sd)
                            v._custom_dve(
                                RECIP7_SUM,
                                out=rstrip[:, (ps + kt) * KT:(ps + kt + 1) * KT],
                                in0=dscr[pb][:, :], s0=C0_R, s1=0.0, imm2=C1_R,
                                accum_out=partF[:, ps + kt:ps + kt + 1]
                            ).then_inc(vdf.h, 1)
                        PV.append(_vf1)
                        vdf_slot[pb] = vdf.inc()
                    PV.append(lambda v: v.tensor_copy(
                        out=junk1[:, :], in_=junk1[:, :]).then_inc(fsy.h, 1))
                    return fsy.inc()

                def emit_rnorm_s(qb, wf):
                    rc = qb - qb0
                    nkt = tiles_of(qb)
                    ps = (qb % 3) * 4
                    sl = qb % 2

                    def _sr(s, rc=rc, nkt=nkt, ps=ps, base=sl * 4, wf=wf):
                        s.wait_ge(fsy.h, wf)
                        if nkt == 1:
                            src = partF[:, ps:ps + 1]
                        else:
                            s.activation(out=rnsc[:, base:base + 1],
                                         in_=partF[:, ps:ps + 1],
                                         func=ACT.Identity,
                                         bias=partF[:, ps + 1:ps + 2],
                                         scale=1.0)
                            src = rnsc[:, base:base + 1]
                            if nkt == 3:
                                s.activation(out=rnsc[:, base + 1:base + 2],
                                             in_=rnsc[:, base:base + 1],
                                             func=ACT.Identity,
                                             bias=partF[:, ps + 2:ps + 3],
                                             scale=1.0)
                                src = rnsc[:, base + 1:base + 2]
                            elif nkt == 4:
                                s.activation(out=rnsc[:, base + 1:base + 2],
                                             in_=rnsc[:, base:base + 1],
                                             func=ACT.Identity,
                                             bias=partF[:, ps + 2:ps + 3],
                                             scale=1.0)
                                s.activation(out=rnsc[:, base + 2:base + 3],
                                             in_=rnsc[:, base + 1:base + 2],
                                             func=ACT.Identity,
                                             bias=partF[:, ps + 3:ps + 4],
                                             scale=1.0)
                                src = rnsc[:, base + 2:base + 3]
                        s.activation(out=rnsc[:, base + 3:base + 4], in_=src,
                                     func=ACT.Ln, bias=0.0, scale=-1.0)
                        s.activation(out=rnorm[:, rc:rc + 1],
                                     in_=rnsc[:, base + 3:base + 4],
                                     func=ACT.Exp, bias=0.0,
                                     scale=-1.0).then_inc(srn.h, 1)
                    PS.append(_sr)
                    return srn.inc()

                dbg_here = dbg and b == 0 and h == 2

                def dbg_dump(dst, srcap):
                    def _vm(v):
                        v.tensor_copy(out=junk1[:, :],
                                      in_=junk1[:, :]).then_inc(dbgv.h, 1)
                    PV.append(_vm)
                    wv = dbgv.inc()

                    def _gd(g_, dst=dst, srcap=srcap, wv=wv):
                        g_.wait_ge(dbgv.h, wv)
                        g_.dma_start(out=dst, in_=srcap).then_inc(dbgs.h, 16)
                    PG.append(_gd)
                    wd = dbgs.inc(16)
                    PV.append(lambda v, wd=wd: v.wait_ge(dbgs.h, wd))

                if dbg_here:
                    ntot = sum(tiles_of(q) for q in rows)
                    dbg_dump(dbg_sgrp[g, :, 0:ntot * KT], sgrp[:, 0:ntot * KT])
                    dbg_dump(dbg_rmax[g, :, :], rowmaxP[:, :])
                emit_lambda_init(0)
                pend1 = None
                for i in range(4):
                    emit_iter_tiles(0)
                    if pend1 is not None:
                        emit_update_sg(*pend1)
                    wv0 = emit_update_marker(0)
                    if i == 0:
                        emit_lambda_init(1)
                    emit_iter_tiles(1)
                    emit_update_sg(i, 0, wv0)
                    pend1 = (i, 1, emit_update_marker(1))
                emit_update_sg(*pend1)


                def emit_f2_row(qb):
                    rc = qb - qb0
                    nkt = tiles_of(qb)
                    ps = (qb % 3) * 4
                    w_opv = None
                    for kt in range(nkt):
                        pslot = state["p_tile"] % 2

                        def _vf2(v, kt=kt, pslot=pslot, wtp=tpd_slot[pslot],
                                 rc=rc, ps=ps, wrn=rn_need.get(qb, 0)):
                            if kt == 0 and wrn:
                                v.wait_ge(srn.h, wrn)
                            if wtp:
                                v.wait_ge(tpd.h, wtp)
                            v.tensor_scalar(out=pbf[pslot][:, :],
                                            in0=rstrip[:, (ps + kt) * KT:
                                                       (ps + kt + 1) * KT],
                                            scalar1=rnorm[:, rc:rc + 1],
                                            scalar2=-1.0,
                                            op0=ALU.mult,
                                            op1=ALU.mult).then_inc(pr.h, 1)
                        PV.append(_vf2)
                        w_pr = pr.inc()

                        def _ttp(t, pslot=pslot, w=w_pr, wpe=pte_bank[pslot]):
                            t.wait_ge(pr.h, w)
                            if wpe:
                                t.wait_ge(pte.h, wpe)
                            for c in range(4):
                                ins = t.transpose(
                                    ps_pt[pslot][:, c * 128:(c + 1) * 128],
                                    pbf[pslot][:, c * 128:(c + 1) * 128],
                                    ident[:, :])
                            ins.then_inc(tpd.h, 1)
                        PT.append(_ttp)
                        tpd_slot[pslot] = tpd.inc()

                        def _spt(s, pslot=pslot, w=tpd_slot[pslot],
                                 wpv=pvd_slot[pslot]):
                            s.wait_ge(tpd.h, w)
                            if wpv:
                                s.wait_ge(pvd.h, wpv)
                            s.copy(out=ptsb[pslot][:, :],
                                   in_=ps_pt[pslot][:, :]).then_inc(pte.h, 1)
                        PS.append(_spt)
                        pte_bank[pslot] = pte.inc()

                        obank = qb % 2

                        def _tpv(t, pslot=pslot, w=pte_bank[pslot], kt=kt,
                                 par=par, obank=obank, first=(kt == 0),
                                 last=(kt == nkt - 1),
                                 wo_free=oev_row[obank]):
                            t.wait_ge(pte.h, w)
                            if first and wo_free:
                                t.wait_ge(oev.h, wo_free)
                            for c in range(4):
                                tc = kt * 4 + c
                                ins = t.matmul(
                                    ps_o[obank][:, :],
                                    vpair[:, tc * 128 + 64 * par:
                                          tc * 128 + 64 * par + 64],
                                    ptsb[pslot][:, c * 128:(c + 1) * 128],
                                    start=(first and c == 0),
                                    stop=(last and c == 3))
                            ins.then_inc(pvd.h, 1)
                        PT.append(_tpv)
                        pvd_slot[pslot] = pvd.inc()
                        if kt == nkt - 1:
                            w_opv = pvd.n
                        state["p_tile"] += 1

                    def _soev(s, qb=qb, w=w_opv, obank=qb % 2):
                        s.wait_ge(pvd.h, w)
                        s.activation(out=othead[:, qb * 128:(qb + 1) * 128],
                                     in_=ps_o[obank][:, :], func=ACT.Copy,
                                     bias=0.0, scale=1.0).then_inc(oev.h, 1)
                    PS.append(_soev)
                    oev_row[qb % 2] = oev.inc()

                rn_need = {}
                wf_row = {rows[0]: emit_f1_row(rows[0])}
                for j in range(1, 8):
                    if j == 4 and dbg_here:
                        dbg_dump(dbg_lam[g, :, :], lam[:, 0:8])
                    wf_row[rows[j]] = emit_f1_row(rows[j])
                    rn_need[rows[j - 1]] = emit_rnorm_s(rows[j - 1],
                                                        wf_row[rows[j - 1]])
                    if j >= 2:
                        emit_f2_row(rows[j - 2])
                rn_need[rows[7]] = emit_rnorm_s(rows[7], wf_row[rows[7]])
                emit_f2_row(rows[6])
                emit_f2_row(rows[7])
                if dbg_here:
                    dbg_dump(dbg_rn[g, :, :], rnorm[:, :])

            bh = 0
            for b in range(NB):
                for h in range(H):
                    if h % 2 == 0:
                        qkv_pair(b, h // 2)
                        if dbg and b == 0 and h == 2:
                            def _gdq(g_, w=qev.n):
                                g_.wait_ge(qev.h, w)
                                g_.dma_start(out=dbg_q[:, :],
                                             in_=qpair[:, :]).then_inc(dbgs.h, 16)
                                g_.dma_start(out=dbg_k[:, :],
                                             in_=kpair[:, :]).then_inc(dbgs.h, 16)
                                g_.dma_start(out=dbg_v[:, :],
                                             in_=vpair[:, :]).then_inc(dbgs.h, 16)
                            PG.append(_gdq)
                            dbgs.inc(48)
                    w_pair = qev.n
                    if odma_buf[0]:
                        def _sw(s, w=odma_buf[0]):
                            s.wait_ge(odma.h, w)
                        PS.append(_sw)
                    for g in range(2):
                        attn_group(b, h, g, w_pair)
                    w_all_oev = oev.n

                    def _sdma(s, b=b, h=h, w=w_all_oev):
                        s.wait_ge(oev.h, w)
                        s.dma_start(out=oT_sc[b, h, :, :],
                                    in_=othead[:, :]).then_inc(odma.h, 16)
                    PS.append(_sdma)
                    odma_buf[0] = odma.inc(16)
                    bh += 1

            PS.append(lambda s: s.wait_ge(odma.h, odma.n))
            if dbg:
                PY.append(lambda y: y.wait_ge(dbgs.h, dbgs.n))
            _run_plans(nc, PT, PV, PS, PY, PG)

        if phases < 3:
            return _finish(nc)
        with ExitStack() as p3:
            e3 = p3.enter_context
            woT = e3(nc.sbuf_tensor("woT", [128, 8 * D], BF16))
            wdT = e3(nc.sbuf_tensor("wdT", [128, 24 * D], BF16))
            oT_ld = [e3(nc.sbuf_tensor(f"oT_ld{i}", [128, 8 * KT], BF16))
                     for i in range(2)]
            wgld = [e3(nc.sbuf_tensor(f"wgld{i}", [128, 8 * 128], BF16))
                    for i in range(2)]
            wuld = [e3(nc.sbuf_tensor(f"wuld{i}", [128, 8 * 128], BF16))
                    for i in range(2)]
            hbuf = e3(nc.sbuf_tensor("hbuf", [128, 24 * KT], BF16))
            sgate = [e3(nc.sbuf_tensor(f"sgate{i}", [128, KT], F32))
                     for i in range(2)]
            gbuf = [e3(nc.sbuf_tensor(f"gbuf{i}", [128, D], F32))
                    for i in range(2)]
            sgbuf = e3(nc.sbuf_tensor("sgbuf", [128, D], F32))
            xres = [e3(nc.sbuf_tensor(f"xres{i}", [128, D], F32)) for i in range(2)]
            outb = [e3(nc.sbuf_tensor(f"outb{i}", [128, D], F32)) for i in range(2)]

            ps_g = [e3(nc.psum_tensor(f"ps_g{i}", [128, KT], F32)) for i in range(2)]
            ps_u = [e3(nc.psum_tensor(f"ps_u{i}", [128, KT], F32)) for i in range(2)]
            ps_wo = e3(nc.psum_tensor("ps_wo", [128, D], F32))
            ps_dn = e3(nc.psum_tensor("ps_dn", [128, D], F32))

            ld, gmm, hdone, sgd = (S3["p3_ld"], S3["p3_gmm"], S3["p3_hdone"],
                                   S3["p3_sgd"])
            womm, dnmm, gsil, vout, st = (S3["p3_womm"], S3["p3_dnmm"],
                                          S3["p3_gsil"], S3["p3_vout"],
                                          S3["p3_st"])
            PT, PV, PS, PY = [], [], [], []

            def _lw(y):
                for c in range(8):
                    y.dma_start(out=woT[:, c * D:(c + 1) * D],
                                in_=woT_sc[c * 128:(c + 1) * 128, :]).then_inc(ld.h, 16)
                for c in range(24):
                    y.dma_start(out=wdT[:, c * D:(c + 1) * D],
                                in_=wdT_sc[c * 128:(c + 1) * 128, :]).then_inc(ld.h, 16)
            PY.append(_lw)
            w_w = ld.inc(16 * 32)

            wg_free = {0: 0, 1: 0}
            oT_free = {0: 0, 1: 0}
            x_free = {0: 0, 1: 0}
            out_free = {0: 0, 1: 0}
            h_bank = {0: 0, 1: 0}
            sg_free = {0: 0, 1: 0}
            g_free = {0: 0, 1: 0}
            wo_prev = 0
            dn_prev = 0

            nspan = NB * N // KT
            subt = 0
            for sp in range(nspan):
                b, t0 = sp // (N // KT), (sp % (N // KT)) * KT
                obuf = sp % 2
                if oT_free[obuf]:
                    PY.append(lambda y, w=oT_free[obuf]: y.wait_ge(womm.h, w))

                def _lo(y, b=b, t0=t0, obuf=obuf):
                    for c in range(8):
                        y.dma_start(out=oT_ld[obuf][0:64, c * KT:(c + 1) * KT],
                                    in_=oT_sc[b, 2 * c, :, t0:t0 + KT]
                                    ).then_inc(ld.h, 16)
                        y.dma_start(out=oT_ld[obuf][64:128, c * KT:(c + 1) * KT],
                                    in_=oT_sc[b, 2 * c + 1, :, t0:t0 + KT]
                                    ).then_inc(ld.h, 16)
                PY.append(_lo)
                w_oT = ld.inc(16 * 16)

                for ht in range(24):
                    wbuf = ht % 2
                    if wg_free[wbuf]:
                        PY.append(lambda y, w=wg_free[wbuf]: y.wait_ge(gmm.h, w))

                    def _lwg(y, ht=ht, wbuf=wbuf):
                        ap_g = bass.AP(wgT_sc, ht * 128,
                                       [[HID, 128], [128 * HID, 8], [1, 128]])
                        ap_u = bass.AP(wuT_sc, ht * 128,
                                       [[HID, 128], [128 * HID, 8], [1, 128]])
                        y.dma_start(out=wgld[wbuf][:, :], in_=ap_g).then_inc(ld.h, 16)
                        y.dma_start(out=wuld[wbuf][:, :], in_=ap_u).then_inc(ld.h, 16)
                    PY.append(_lwg)
                    w_wg = ld.inc(32)
                    pbank = ht % 2

                    def _tg(t, wbuf=wbuf, b=b, t0=t0, w=w_wg, pbank=pbank,
                            whd=h_bank[pbank]):
                        t.wait_ge(ld.h, w)
                        if whd:
                            t.wait_ge(hdone.h, whd)
                        for c in range(8):
                            t.matmul(ps_g[pbank][:, :],
                                     wgld[wbuf][:, c * 128:(c + 1) * 128],
                                     xn[:, c * NB * N + b * N + t0:
                                        c * NB * N + b * N + t0 + KT],
                                     start=(c == 0), stop=(c == 7))
                        for c in range(8):
                            ins = t.matmul(ps_u[pbank][:, :],
                                           wuld[wbuf][:, c * 128:(c + 1) * 128],
                                           xn[:, c * NB * N + b * N + t0:
                                              c * NB * N + b * N + t0 + KT],
                                           start=(c == 0), stop=(c == 7))
                        ins.then_inc(gmm.h, 1)
                    PT.append(_tg)
                    wg_free[wbuf] = gmm.inc()

                    def _ssil(s, w=wg_free[wbuf], pbank=pbank,
                              wsg=sg_free[pbank]):
                        s.wait_ge(gmm.h, w)
                        if wsg:
                            s.wait_ge(hdone.h, wsg)
                        s.activation(out=sgate[pbank][:, :], in_=ps_g[pbank][:, :],
                                     func=ACT.Silu, bias=0.0,
                                     scale=1.0).then_inc(sgd.h, 1)
                    PS.append(_ssil)
                    w_sg = sgd.inc()

                    def _vh(v, ht=ht, w=w_sg, pbank=pbank):
                        v.wait_ge(sgd.h, w)
                        v.tensor_tensor(out=hbuf[:, ht * KT:(ht + 1) * KT],
                                        in0=sgate[pbank][:, :],
                                        in1=ps_u[pbank][:, :],
                                        op=ALU.mult).then_inc(hdone.h, 1)
                    PV.append(_vh)
                    h_bank[pbank] = hdone.inc()
                    sg_free[pbank] = h_bank[pbank]
                w_h_span = hdone.n

                for stt in range(4):
                    ts0 = stt * 128
                    xbuf = subt % 2
                    if x_free[xbuf]:
                        PY.append(lambda y, w=x_free[xbuf]: y.wait_ge(vout.h, w))

                    def _lx(y, b=b, t0=t0, ts0=ts0, xbuf=xbuf):
                        y.dma_start(out=xres[xbuf][:, :],
                                    in_=x_in[b, t0 + ts0:t0 + ts0 + 128, :]
                                    ).then_inc(ld.h, 16)
                    PY.append(_lx)
                    w_x = ld.inc(16)

                    def _two(t, obuf=obuf, ts0=ts0, w_oT=w_oT, wv=wo_prev):
                        t.wait_ge(ld.h, max(w_oT, w_w))
                        if wv:
                            t.wait_ge(vout.h, wv)
                        for dh2 in range(2):
                            for c in range(8):
                                ins = t.matmul(
                                    ps_wo[:, dh2 * 512:(dh2 + 1) * 512],
                                    oT_ld[obuf][:, c * KT + ts0:
                                                c * KT + ts0 + 128],
                                    woT[:, c * D + dh2 * 512:
                                        c * D + dh2 * 512 + 512],
                                    start=(c == 0), stop=(c == 7))
                        ins.then_inc(womm.h, 1)
                    PT.append(_two)
                    oT_free[obuf] = womm.inc()

                    def _tdn(t, ts0=ts0, w=w_h_span, wg=dn_prev):
                        t.wait_ge(hdone.h, w)
                        if wg:
                            t.wait_ge(gsil.h, wg)
                        for dh2 in range(2):
                            for c in range(24):
                                ins = t.matmul(
                                    ps_dn[:, dh2 * 512:(dh2 + 1) * 512],
                                    hbuf[:, c * KT + ts0:c * KT + ts0 + 128],
                                    wdT[:, c * D + dh2 * 512:
                                        c * D + dh2 * 512 + 512],
                                    start=(c == 0), stop=(c == 23))
                        ins.then_inc(dnmm.h, 1)
                    PT.append(_tdn)
                    w_dn = dnmm.inc()

                    gb = subt % 2

                    def _sgl(s, w=w_dn, gb=gb, wgf=g_free[gb]):
                        s.wait_ge(dnmm.h, w)
                        if wgf:
                            s.wait_ge(vout.h, wgf)
                        s.activation(out=gbuf[gb][:, :], in_=ps_dn[:, :],
                                     func=ACT.Silu, bias=0.0,
                                     scale=1.0).then_inc(gsil.h, 1)
                    PS.append(_sgl)
                    w_gs = gsil.inc()
                    dn_prev = w_gs

                    def _vo(v, xbuf=xbuf, w_gs=w_gs, w_wo=oT_free[obuf],
                            w_x=w_x, wst=out_free[xbuf], gb=gb):
                        v.wait_ge(gsil.h, w_gs)
                        v.wait_ge(womm.h, w_wo)
                        v.wait_ge(ld.h, w_x)
                        if wst:
                            v.wait_ge(st.h, wst)
                        v.tensor_tensor(out=sgbuf[:, :], in0=ps_wo[:, :],
                                        in1=gbuf[gb][:, :], op=ALU.mult)
                        v.tensor_tensor(out=outb[xbuf][:, :], in0=sgbuf[:, :],
                                        in1=xres[xbuf][:, :],
                                        op=ALU.add).then_inc(vout.h, 1)
                    PV.append(_vo)
                    x_free[xbuf] = vout.inc()
                    wo_prev = x_free[xbuf]
                    g_free[gb] = x_free[xbuf]

                    def _so(y, b=b, t0=t0, ts0=ts0, xbuf=xbuf, w=x_free[xbuf]):
                        y.wait_ge(vout.h, w)
                        y.dma_start(out=out_d[b, t0 + ts0:t0 + ts0 + 128, :],
                                    in_=outb[xbuf][:, :]).then_inc(st.h, 16)
                    PY.append(_so)
                    out_free[xbuf] = st.inc(16)
                    subt += 1

            PY.append(lambda y: y.wait_ge(st.h, st.n))
            _run_plans(nc, PT, PV, PS, PY)

    return _finish(nc)


def _finish(nc):
    mybir.codegen_inst_isa_subclasses(nc)
    return nc


_NC_CACHE = {}


def _get_nc():
    if "nc" not in _NC_CACHE:
        _NC_CACHE["nc"] = build()
    return _NC_CACHE["nc"]


def kernel(L0_attn_x, L0_attn_Wqkv, L0_attn_Wo, L0_ffn_w_gate, L0_ffn_w_up,
           L0_ffn_w_down):
    x = np.ascontiguousarray(np.asarray(L0_attn_x, dtype=np.float32))
    wqkvv = np.ascontiguousarray(np.asarray(L0_attn_Wqkv, dtype=np.float32))
    wov = np.ascontiguousarray(np.asarray(L0_attn_Wo, dtype=np.float32))
    wgv = np.ascontiguousarray(np.asarray(L0_ffn_w_gate, dtype=np.int32))
    wuv = np.ascontiguousarray(np.asarray(L0_ffn_w_up, dtype=np.int32))
    wdv = np.ascontiguousarray(np.asarray(L0_ffn_w_down, dtype=np.int32))
    ident, identf, lens = _host_consts()

    nc = _get_nc()
    in_maps = []
    for c in range(8):
        in_maps.append({
            "x": x[c * NB:(c + 1) * NB],
            "wqkv": wqkvv, "wo": wov, "wg": wgv, "wu": wuv, "wd": wdv,
            "ident": ident, "identf": identf, "lens": lens,
        })
    res = run_bass_kernel_spmd(nc, in_maps, core_ids=list(range(8)))
    out = np.concatenate([res.results[c]["out"] for c in range(8)], axis=0)
    return out


# version 07 (final)
# speedup vs baseline: 1.1986x; 1.0859x over previous
import numpy as np
import ml_dtypes
from contextlib import ExitStack
from operator import add as _op_add

import concourse.bass as bass
import concourse.mybir as mybir
from concourse.bass_utils import run_bass_kernel_spmd

import concourse.dve_ops as dvo
from concourse.dve_op import DveOp
from concourse.dve_spec import Spec, AluOp, Bin, Src0, C0, C1, Zero, lower, sq
from concourse.dve_ops_raw import _compute_dtype

F32 = mybir.dt.float32
BF16 = mybir.dt.bfloat16
I32 = mybir.dt.int32
AX = mybir.AxisListType.X
ALU = mybir.AluOpType
ACT = mybir.ActivationFunctionType

NB = 2
N = 2048
D = 1024
H = 16
DH = 64
HID = 3072
NQB = N // 128
KT = 512
QPRE = float(2.0 ** 27)
SPOST = float(2.0 ** -30)

K_R = -4.25
K2 = float(np.float32(K_R * K_R))
C0_R = float(np.float32(2.002 * K_R))
C1_R = float(np.float32(2.0 * K_R * K_R))
K3 = float(np.float64(K_R) ** 3)
NEG_KR = float(-K_R)


def _ref_recip7(in0, in1, c0, c1, c2):
    f = _compute_dtype(in0)
    seed = np.asarray(c1, f).reshape(-1, 1) if hasattr(c1, "reshape") else c1
    if f is np.float64:
        out = np.float64(K3 * K_R) / in0
        return out, seed + out.reshape(out.shape[0], -1).sum(
            axis=-1, keepdims=True)
    x = np.ascontiguousarray(in0, dtype=np.float32)
    n = (~x.view(np.int32)).view(np.float32)
    z1 = (n * (np.float32(c0) - x * n).astype(np.float32)).astype(np.float32)
    out = (z1 * (np.float32(c2) - x * z1).astype(np.float32)).astype(np.float32)
    return out, seed + out.reshape(out.shape[0], -1).astype(np.float32).sum(
        axis=-1, keepdims=True)


def _ref_sq_sum(in0, in1, c0, c1, c2):
    f = _compute_dtype(in0)
    seed = np.asarray(c0, f).reshape(-1, 1) if hasattr(c0, "reshape") else c0
    b = np.square(in0.astype(f))
    return b, seed + b.reshape(b.shape[0], -1).sum(axis=-1, keepdims=True)


def _mk_ops():
    from concourse.dve_spec import C2
    n = Bin(AluOp.BITWISE_NOT, Src0, Src0)
    z1 = n * (C0 - Src0 * n)
    body = z1 * (C2 - Src0 * z1)
    spec_r = Spec(body=body, accum=_op_add, accum_init=C1)
    sha_r = lower(spec_r, ver="v3", perf_en=False, name="RECIP7_SUM").sha("v3")
    op_r = DveOp("RECIP7_SUM", spec_r, subdim=False, uops_sha={"v3": sha_r},
                 reference=_ref_recip7)
    spec_s = Spec(body=sq(Src0), accum=_op_add, accum_init=C0)
    sha_s = lower(spec_s, ver="v3", perf_en=False, name="SQ_SUM").sha("v3")
    op_s = DveOp("SQ_SUM", spec_s, subdim=False, uops_sha={"v3": sha_s},
                 reference=_ref_sq_sum)
    return op_r, op_s


def _register(op):
    if op.name in dvo._SUB_OPCODE_FOR_NAME:
        return
    dvo.OPS.append(op)
    dvo._SUB_OPCODE_FOR_NAME[op.name] = max(dvo._SUB_OPCODE_FOR_NAME.values()) + 1
    assert dvo._SUB_OPCODE_FOR_NAME[op.name] < 0x20
    dvo.CUSTOM_DVE_SPECS[op.name] = op.spec
    dvo.CUSTOM_DVE_REFS[op.name] = op.ref


RECIP7_SUM, SQ_SUM = _mk_ops()
_register(RECIP7_SUM)
_register(SQ_SUM)


def _host_consts():
    ident = np.eye(128).astype(ml_dtypes.bfloat16)
    identf = np.eye(128).astype(np.float32)
    lens = np.zeros((128, NQB * 4 + 2), dtype=np.float32)
    p = np.arange(128)
    for qb in range(NQB):
        for kt in range(4):
            lens[:, qb * 4 + kt] = np.clip(128 * qb + p + 1 - KT * kt, 0, KT)
    lens[:, NQB * 4] = 1e-5
    lens[:, NQB * 4 + 1] = -K3
    return ident, identf, lens


class Sem:
    __slots__ = ("h", "n")

    def __init__(self, handle):
        self.h = handle
        self.n = 0

    def inc(self, k=1):
        self.n += k
        return self.n


def tiles_of(qb):
    return qb // 4 + 1


def wid(qb, kt):
    return 128 * (qb % 4) + 128 if kt == qb // 4 else KT


def _run_plans(nc, PT, PV, PS, PY, PG=None):
    with nc.Block() as blk:
        blk.tensor(lambda t: [f(t) for f in PT])
        blk.vector(lambda v: [f(v) for f in PV])
        blk.scalar(lambda s: [f(s) for f in PS])
        blk.sync(lambda y: [f(y) for f in PY])
        if PG:
            blk.gpsimd(lambda g_: [f(g_) for f in PG])


def build(dbg=False, phases=3):
    nc = bass.Bass("TRN2", target_bir_lowering=False,
                   dynamic_dma_scratch_size=2048)

    x_in = nc.dram_tensor("x", [NB, N, D], F32, kind="ExternalInput")
    wqkv = nc.dram_tensor("wqkv", [3 * D, D], F32, kind="ExternalInput")
    wo = nc.dram_tensor("wo", [D, D], F32, kind="ExternalInput")
    wg = nc.dram_tensor("wg", [HID, D], I32, kind="ExternalInput")
    wu = nc.dram_tensor("wu", [HID, D], I32, kind="ExternalInput")
    wd = nc.dram_tensor("wd", [D, HID], I32, kind="ExternalInput")
    ident_d = nc.dram_tensor("ident", [128, 128], BF16, kind="ExternalInput")
    identf_d = nc.dram_tensor("identf", [128, 128], F32, kind="ExternalInput")
    lens_d = nc.dram_tensor("lens", [128, NQB * 4 + 2], F32, kind="ExternalInput")
    out_d = nc.dram_tensor("out", [NB, N, D], F32, kind="ExternalOutput")

    dk = "ExternalOutput" if dbg else "Internal"
    oT_sc = nc.dram_tensor("oT_sc", [NB, H, DH, N], BF16, kind=dk)
    wqkvT_sc = nc.dram_tensor("wqkvT_sc", [D, 3 * D], BF16, kind=dk)
    woT_sc = nc.dram_tensor("woT_sc", [D, D], BF16, kind=dk)
    wgT_sc = nc.dram_tensor("wgT_sc", [D, HID], BF16, kind=dk)
    wuT_sc = nc.dram_tensor("wuT_sc", [D, HID], BF16, kind=dk)
    wdT_sc = nc.dram_tensor("wdT_sc", [HID, D], BF16, kind=dk)
    if dbg:
        xn_dbg = nc.dram_tensor("xn_dbg", [128, 8 * NB * N], BF16,
                                kind="ExternalOutput")
        dbg_st_d = nc.dram_tensor("dbg_st", [128, 32 * 8], F32,
                                  kind="ExternalOutput")
        dbg_sgrp = nc.dram_tensor("dbg_sgrp", [2, 128, 28 * KT], F32,
                                  kind="ExternalOutput")
        dbg_rmax = nc.dram_tensor("dbg_rmax", [2, 128, 32], F32,
                                  kind="ExternalOutput")
        dbg_lam = nc.dram_tensor("dbg_lam", [2, 128, 8], F32,
                                 kind="ExternalOutput")
        dbg_rn = nc.dram_tensor("dbg_rn", [2, 128, 8], F32,
                                kind="ExternalOutput")
        dbg_q = nc.dram_tensor("dbg_q", [128, N], BF16, kind="ExternalOutput")
        dbg_it = nc.dram_tensor("dbg_it", [2, 6, 128, 16], F32,
                                kind="ExternalOutput")
        dbg_p = nc.dram_tensor("dbg_p", [128, KT], BF16, kind="ExternalOutput")
        dbg_pt = nc.dram_tensor("dbg_pt", [128, KT], BF16, kind="ExternalOutput")
        dbg_pp = nc.dram_tensor("dbg_pp", [2, 6, 128, 16], F32,
                                kind="ExternalOutput")
        dbg_k = nc.dram_tensor("dbg_k", [128, N], BF16, kind="ExternalOutput")
        dbg_v = nc.dram_tensor("dbg_v", [128, N], BF16, kind="ExternalOutput")

    with ExitStack() as outer:
        oe = outer.enter_context
        xn = oe(nc.sbuf_tensor("xn", [128, 8 * NB * N], BF16))
        ident = oe(nc.sbuf_tensor("ident_sb", [128, 128], BF16))
        identf = oe(nc.sbuf_tensor("identf_sb", [128, 128], F32))
        lens_sb = oe(nc.sbuf_tensor("lens_sb", [128, NQB * 4 + 2], F32))

        def mk_sems(names):
            return {k: Sem(oe(nc.semaphore(k))) for k in names}

        S1 = mk_sems(["p1_ld", "p1_v", "p1_s", "p1_t", "p1_st"])
        S2 = mk_sems(["p2_ld", "p2_qmm", "p2_qev", "p2_smm", "p2_sfree",
                      "p2_pr", "p2_tpd", "p2_pte", "p2_pvd", "p2_opv",
                      "p2_oev", "p2_odma"])
        S3 = mk_sems(["p3_ld", "p3_gmm", "p3_hdone", "p3_sgd", "p3_womm",
                      "p3_dnmm", "p3_gsil", "p3_vout", "p3_st"])

        with ExitStack() as p1:
            e1 = p1.enter_context
            stg_f = [e1(nc.sbuf_tensor(f"stg_f{i}", [128, D], F32)) for i in range(4)]
            stg_i = [e1(nc.sbuf_tensor(f"stg_i{i}", [128, HID], I32)) for i in range(4)]
            wcast = [e1(nc.sbuf_tensor(f"wcast{i}", [128, HID], BF16)) for i in range(2)]
            wstage = [e1(nc.sbuf_tensor(f"wstage{i}", [128, 1024], BF16)) for i in range(2)]
            xt = [e1(nc.sbuf_tensor(f"xt{i}", [128, D], F32)) for i in range(8)]
            xn_tok = e1(nc.sbuf_tensor("xn_tok", [128, D], BF16))
            sqw = e1(nc.sbuf_tensor("sqw1", [128, D], F32))
            stat = e1(nc.sbuf_tensor("stat", [128, 64], F32))
            dbg_sb = e1(nc.sbuf_tensor("dbg_sb", [128, 32 * 8], F32)) if dbg else None
            xbb = [e1(nc.sbuf_tensor(f"xbb{i}", [128, D], BF16)) for i in range(2)]
            xfmb = [e1(nc.sbuf_tensor(f"xfmb{i}", [128, D], BF16)) for i in range(2)]
            x2fmb = [e1(nc.sbuf_tensor(f"x2fmb{i}", [128, D], BF16)) for i in range(2)]
            strow = [e1(nc.sbuf_tensor(f"strow{i}", [1, 256], F32)) for i in range(2)]
            onesb = e1(nc.sbuf_tensor("onesb", [128, 2], BF16))
            ps_t = [e1(nc.psum_tensor(f"ps_t{i}", [128, 1024], BF16)) for i in range(2)]
            ps_t2 = [e1(nc.psum_tensor(f"ps_t2{i}", [128, 1024], BF16)) for i in range(2)]
            ps_st = [e1(nc.psum_tensor(f"ps_st{i}", [2, 256], F32)) for i in range(2)]
            ps_tt = [e1(nc.psum_tensor(f"ps_tt{i}", [128, 2], F32)) for i in range(2)]

            ld, vv, ss, tt, st = (S1["p1_ld"], S1["p1_v"], S1["p1_s"],
                                  S1["p1_t"], S1["p1_st"])
            PT, PV, PS, PY = [], [], [], []

            PY.append(lambda y: y.dma_start(out=ident[:, :],
                                            in_=ident_d[:, :]).then_inc(ld.h, 16))
            ld.inc(16)
            PY.append(lambda y: y.dma_start(out=lens_sb[:, :],
                                            in_=lens_d[:, :]).then_inc(ld.h, 16))
            ld.inc(16)
            PY.append(lambda y: y.dma_start(out=identf[:, :],
                                            in_=identf_d[:, :]).then_inc(ld.h, 16))
            ld.inc(16)
            PT.append(lambda t: t.wait_ge(ld.h, 48))

            def _swarm(s):
                s.wait_ge(ld.h, 48)
                s.memzero(sqw[:, :])
                s.activation(out=sqw[:, :], in_=sqw[:, :], func=ACT.Copy,
                             bias=0.0, scale=1.0)
                s.activation(out=sqw[:, :], in_=sqw[:, :], func=ACT.Square,
                             bias=0.0, scale=1.0)
                s.activation(out=sqw[:, :], in_=sqw[:, :], func=ACT.Exp,
                             bias=0.0, scale=0.0)
                s.activation(out=sqw[:, :], in_=sqw[:, :], func=ACT.Ln,
                             bias=lens_sb[:, NQB * 4:NQB * 4 + 1], scale=0.0)
                s.activation(out=sqw[:, :], in_=sqw[:, :], func=ACT.Exp,
                             bias=0.0, scale=0.0)
                s.memzero(sqw[:, :])
            PS.append(_swarm)

            def _vmo(v):
                v.memset(onesb[:, 0:1], 1.0)
                v.memset(onesb[:, 1:2], 0.0).then_inc(vv.h, 1)
            PV.append(_vmo)
            vv.inc()
            PT.append(lambda t: t.wait_ge(vv.h, 1))
            def _twarm(t):
                t.wait_ge(vv.h, 1)
                for pb in range(2):
                    for c in range(4):
                        t.matmul(ps_st[pb][0:2, 0:128], onesb[:, :],
                                 ident[:, 0:128], start=(c == 0), stop=(c == 3))
                    for c in range(4):
                        t.matmul(ps_st[pb][0:2, 128:256], onesb[:, :],
                                 ident[:, 0:128], start=(c == 0), stop=(c == 3))
                    t.transpose(ps_t[pb][:, 0:128], ident[:, :], ident[:, :])
                    t.transpose(ps_t2[pb][:, 0:128], ident[:, :], ident[:, :])
                    t.transpose(ps_tt[pb][:, 0:1], identf[0:1, 0:128],
                                identf[0:1, 0:1])
                ins = t.matmul(ps_st[0][0:2, 0:128], onesb[:, :],
                               ident[:, 0:128], start=True, stop=True)
                ins.then_inc(tt.h, 1)
            PT.append(_twarm)
            tt.inc()
            PV.append(lambda v: v.wait_ge(tt.h, 1))

            v_done = {i: 0 for i in range(8)}
            xb_free = {0: 0, 1: 0}
            s_evacA = {0: 0, 1: 0}
            s_strow = {0: 0, 1: 0}
            s0_done = {0: 0, 1: 0}
            stat_free = {i: 0 for i in range(8)}
            s_evac2 = {0: 0, 1: 0}
            xn_free = [0]
            ntile = NB * N // 128
            w_s3_of = {}

            def ln_front(j):
                b, t0 = j // (N // 128), (j % (N // 128)) * 128
                buf = j % 8
                pb = j % 2
                base = (j % 8) * 8
                if v_done[buf]:
                    PY.append(lambda y, w=v_done[buf]: y.wait_ge(vv.h, w))
                PY.append(lambda y, b=b, t0=t0, buf=buf: y.dma_start(
                    out=xt[buf][:, :],
                    in_=x_in[b, t0:t0 + 128, :]).then_inc(ld.h, 16))
                myld = ld.inc(16)

                def _vc(v, buf=buf, pb=pb, myld=myld, wtf=xb_free[pb]):
                    v.wait_ge(ld.h, myld)
                    if wtf:
                        v.wait_ge(tt.h, wtf)
                    v.tensor_copy(out=xbb[pb][:, :],
                                  in_=xt[buf][:, :]).then_inc(vv.h, 1)
                PV.append(_vc)
                w_cast = vv.inc()

                def _t1(t, pb=pb, w=w_cast, wse=s_evacA[pb]):
                    t.wait_ge(vv.h, w)
                    if wse:
                        t.wait_ge(ss.h, wse)
                    for c in range(8):
                        ins = t.transpose(ps_t[pb][:, c * 128:(c + 1) * 128],
                                          xbb[pb][:, c * 128:(c + 1) * 128],
                                          ident[:, :])
                    ins.then_inc(tt.h, 1)
                PT.append(_t1)
                w_t1 = tt.inc()
                xb_free[pb] = w_t1

                def _s1(s, pb=pb, w=w_t1):
                    s.wait_ge(tt.h, w)
                    s.copy(out=xfmb[pb][:, :], in_=ps_t[pb][:, :])
                    s.activation(out=x2fmb[pb][:, :], in_=xfmb[pb][:, :],
                                 func=ACT.Square, bias=0.0,
                                 scale=1.0).then_inc(ss.h, 1)
                PS.append(_s1)
                w_s1 = ss.inc()
                s_evacA[pb] = w_s1

                def _t2(t, pb=pb, w=w_s1, wsr=s_strow[pb]):
                    t.wait_ge(ss.h, w)
                    if wsr:
                        t.wait_ge(ss.h, wsr)
                    for c in range(8):
                        t.matmul(ps_st[pb][0:2, 0:128], onesb[:, :],
                                 xfmb[pb][:, c * 128:(c + 1) * 128],
                                 start=(c == 0), stop=(c == 7))
                    for c in range(8):
                        ins = t.matmul(ps_st[pb][0:2, 128:256], onesb[:, :],
                                       x2fmb[pb][:, c * 128:(c + 1) * 128],
                                       start=(c == 0), stop=(c == 7))
                    ins.then_inc(tt.h, 1)
                PT.append(_t2)
                w_t2 = tt.inc()

                def _s2(s, pb=pb, w=w_t2):
                    s.wait_ge(tt.h, w)
                    s.copy(out=strow[pb][0:1, :],
                           in_=ps_st[pb][0:1, :]).then_inc(ss.h, 1)
                PS.append(_s2)
                w_s2 = ss.inc()
                s_strow[pb] = w_s2

                def _t3(t, pb=pb, w=w_s2, wst=s0_done[pb]):
                    t.wait_ge(ss.h, w)
                    if wst:
                        t.wait_ge(ss.h, wst)
                    t.transpose(ps_tt[pb][:, 0:1], strow[pb][0:1, 0:128],
                                identf[0:1, 0:1])
                    t.transpose(ps_tt[pb][:, 1:2], strow[pb][0:1, 128:256],
                                identf[0:1, 0:1]).then_inc(tt.h, 1)
                PT.append(_t3)
                w_t3 = tt.inc()

                def _s0(s, pb=pb, w=w_t3, base=base, wsf=stat_free[j % 8]):
                    s.wait_ge(tt.h, w)
                    if wsf:
                        s.wait_ge(vv.h, wsf)
                    s.copy(out=stat[:, base + 0:base + 2],
                           in_=ps_tt[pb][:, 0:2]).then_inc(ss.h, 1)
                PS.append(_s0)
                s0_done[pb] = ss.inc()

            def ln_stage(j, k):
                base = (j % 8) * 8

                def _sk(s, base=base, k=k, j=j):
                    if k == 1:
                        s.activation(out=stat[:, base + 2:base + 3],
                                     in_=stat[:, base + 0:base + 1],
                                     func=ACT.Copy, bias=0.0, scale=1.0 / D)
                    elif k == 2:
                        s.activation(out=stat[:, base + 3:base + 4],
                                     in_=stat[:, base + 2:base + 3],
                                     func=ACT.Square, bias=0.0, scale=1.0)
                    elif k == 3:
                        s.activation(out=stat[:, base + 4:base + 5],
                                     in_=stat[:, base + 3:base + 4],
                                     func=ACT.Copy, bias=0.0, scale=-1.0)
                    elif k == 4:
                        s.activation(out=stat[:, base + 5:base + 6],
                                     in_=stat[:, base + 1:base + 2],
                                     func=ACT.Identity,
                                     bias=stat[:, base + 4:base + 5],
                                     scale=1.0 / D)
                    elif k == 5:
                        s.activation(out=stat[:, base + 6:base + 7],
                                     in_=stat[:, base + 5:base + 6], func=ACT.Ln,
                                     bias=lens_sb[:, NQB * 4:NQB * 4 + 1],
                                     scale=1.0)
                    else:
                        ins = s.activation(out=stat[:, base + 7:base + 8],
                                           in_=stat[:, base + 6:base + 7],
                                           func=ACT.Exp, bias=0.0, scale=-0.5)
                        if dbg:
                            s.copy(out=dbg_sb[:, j * 8:(j + 1) * 8],
                                   in_=stat[:, base:base + 8])
                        ins.then_inc(ss.h, 1)
                PS.append(_sk)
                if k == 6:
                    w_s3_of[j] = ss.inc()

            def ln_back(j):
                b, t0 = j // (N // 128), (j % (N // 128)) * 128
                buf = j % 8
                pb = j % 2
                base = (j % 8) * 8

                def _vn(v, buf=buf, w=w_s3_of[j], base=base, wxf=xn_free[0]):
                    v.wait_ge(ss.h, w)
                    if wxf:
                        v.wait_ge(tt.h, wxf)
                    v.tensor_scalar(out=xn_tok[:, :], in0=xt[buf][:, :],
                                    scalar1=stat[:, base + 2:base + 3],
                                    scalar2=stat[:, base + 7:base + 8],
                                    op0=ALU.subtract,
                                    op1=ALU.mult).then_inc(vv.h, 1)
                PV.append(_vn)
                v_done[buf] = vv.inc()

                def _t4(t, pb=pb, wv=v_done[buf], wse=s_evac2[pb]):
                    t.wait_ge(vv.h, wv)
                    if wse:
                        t.wait_ge(ss.h, wse)
                    for c in range(8):
                        ins = t.transpose(ps_t2[pb][:, c * 128:(c + 1) * 128],
                                          xn_tok[:, c * 128:(c + 1) * 128],
                                          ident[:, :])
                    ins.then_inc(tt.h, 1)
                PT.append(_t4)
                w_t4 = tt.inc()
                xn_free[0] = w_t4

                def _s4(s, pb=pb, w=w_t4, b=b, t0=t0):
                    s.wait_ge(tt.h, w)
                    for c in range(8):
                        col = c * (NB * N) + b * N + t0
                        ins = s.copy(out=xn[:, col:col + 128],
                                     in_=ps_t2[pb][:, c * 128:(c + 1) * 128])
                    ins.then_inc(ss.h, 1)
                PS.append(_s4)
                s_evac2[pb] = ss.inc()
                stat_free[buf] = v_done[buf]

            for jj in range(ntile + 7):
                if jj < ntile:
                    ln_front(jj)
                for k in range(1, 7):
                    j2 = jj - k
                    if 0 <= j2 < ntile:
                        ln_stage(j2, k)
                if 0 <= jj - 6 < ntile:
                    ln_back(jj - 6)

            wv_done = {0: 0, 1: 0, 2: 0, 3: 0}
            wst_done = {0: 0, 1: 0}
            t_done = {0: 0, 1: 0}
            wchunk = [0]

            def prep_weight(src, Rr, Cc, is_int, dst):
                nrc = Rr // 128
                ncc = Cc // 128
                for rc in range(nrc):
                    ch = wchunk[0]
                    wchunk[0] += 1
                    buf = ch % 4
                    wbuf = ch % 2
                    stg = stg_i[buf] if is_int else stg_f[buf]
                    if wv_done[buf]:
                        PY.append(lambda y, w=wv_done[buf]: y.wait_ge(vv.h, w))
                    PY.append(lambda y, src=src, rc=rc, stg=stg, Cc=Cc:
                              y.dma_start(out=stg[:, 0:Cc],
                                          in_=src[rc * 128:(rc + 1) * 128, :])
                              .then_inc(ld.h, 16))
                    myld = ld.inc(16)

                    def _vc(v, stg=stg, wbuf=wbuf, myld=myld, is_int=is_int,
                            Cc=Cc, wt=t_done[wbuf]):
                        v.wait_ge(ld.h, myld)
                        if wt:
                            v.wait_ge(tt.h, wt)
                        v.tensor_scalar(out=wcast[wbuf][:, 0:Cc], in0=stg[:, 0:Cc],
                                        scalar1=(1.0 / 127.0) if is_int else 1.0,
                                        scalar2=None, op0=ALU.mult).then_inc(vv.h, 1)
                    PV.append(_vc)
                    wv_done[buf] = vv.inc()
                    myvv = wv_done[buf]
                    buf = wbuf

                    for cg in range(0, ncc, 8):
                        nblk = min(8, ncc - cg)

                        def _tt(t, buf=buf, wvc=myvv, cg=cg, nblk=nblk,
                                wse=s_evacA[buf]):
                            t.wait_ge(vv.h, wvc)
                            if wse:
                                t.wait_ge(ss.h, wse)
                            for j in range(nblk):
                                c = cg + j
                                ins = t.transpose(
                                    ps_t[buf][:, j * 128:(j + 1) * 128],
                                    wcast[buf][:, c * 128:(c + 1) * 128],
                                    ident[:, :])
                            ins.then_inc(tt.h, 1)
                        PT.append(_tt)
                        w_tp = tt.inc()
                        t_done[buf] = w_tp

                        def _se(s, buf=buf, w=w_tp, nblk=nblk,
                                wst=wst_done[buf]):
                            s.wait_ge(tt.h, w)
                            if wst:
                                s.wait_ge(st.h, wst)
                            s.copy(out=wstage[buf][:, 0:nblk * 128],
                                   in_=ps_t[buf][:, 0:nblk * 128]).then_inc(ss.h, 1)
                        PS.append(_se)
                        s_evacA[buf] = ss.inc()

                        def _yd(y, buf=buf, w=s_evacA[buf], cg=cg, nblk=nblk,
                                rc=rc, dst=dst):
                            y.wait_ge(ss.h, w)
                            for j in range(nblk):
                                c = cg + j
                                y.dma_start(
                                    out=dst[c * 128:(c + 1) * 128,
                                            rc * 128:(rc + 1) * 128],
                                    in_=wstage[buf][:, j * 128:(j + 1) * 128]
                                ).then_inc(st.h, 16)
                        PY.append(_yd)
                        wst_done[buf] = st.inc(16 * nblk)

            prep_weight(wqkv, 3 * D, D, False, wqkvT_sc)
            prep_weight(wo, D, D, False, woT_sc)
            prep_weight(wg, HID, D, True, wgT_sc)
            prep_weight(wu, HID, D, True, wuT_sc)
            prep_weight(wd, D, HID, True, wdT_sc)

            if dbg:
                PY.append(lambda y: y.wait_ge(ss.h, ss.n))
                PY.append(lambda y: y.dma_start(out=xn_dbg[:, :],
                                                in_=xn[:, :]).then_inc(st.h, 16))
                st.inc(16)
                PY.append(lambda y: y.dma_start(out=dbg_st_d[:, :],
                                                in_=dbg_sb[:, :]).then_inc(st.h, 16))
                st.inc(16)
            PY.append(lambda y: y.wait_ge(st.h, st.n))
            PY.append(lambda y: y.wait_ge(ss.h, ss.n))
            _run_plans(nc, PT, PV, PS, PY)

        if phases < 2:
            return _finish(nc)
        with ExitStack() as p2:
            e2 = p2.enter_context
            wqkvT = e2(nc.sbuf_tensor("wqkvT", [128, 8 * 3 * D], BF16))
            qpair = e2(nc.sbuf_tensor("qpair", [128, N], BF16))
            kpair = e2(nc.sbuf_tensor("kpair", [128, N], BF16))
            vpair = e2(nc.sbuf_tensor("vpair", [128, N], BF16))
            sgrp = e2(nc.sbuf_tensor("sgrp", [128, 28 * KT], F32))
            dscr = [e2(nc.sbuf_tensor(f"dscr{i}", [128, KT], F32))
                    for i in range(3)]
            rstrip = e2(nc.sbuf_tensor("rstrip", [128, 12 * KT], F32))
            pbf = [e2(nc.sbuf_tensor(f"pbf{i}", [128, KT], BF16)) for i in range(2)]
            ptsb = [e2(nc.sbuf_tensor(f"ptsb{i}", [128, KT], BF16)) for i in range(2)]
            othead = e2(nc.sbuf_tensor("othead", [64, N], BF16))
            rowmaxP = e2(nc.sbuf_tensor("rowmaxP", [128, 32], F32))
            lam = e2(nc.sbuf_tensor("lam", [128, 16], F32))
            lamK = e2(nc.sbuf_tensor("lamK", [128, 8], F32))
            srd = e2(nc.sbuf_tensor("srd", [128, 16], F32))
            sr2 = e2(nc.sbuf_tensor("sr2", [128, 8], F32))
            nt1 = e2(nc.sbuf_tensor("nt1", [128, 8], F32))
            nt2 = e2(nc.sbuf_tensor("nt2", [128, 8], F32))
            partR = e2(nc.sbuf_tensor("partR", [128, 8], F32))
            partR2 = e2(nc.sbuf_tensor("partR2", [128, 8], F32))
            partF = e2(nc.sbuf_tensor("partF", [128, 12], F32))
            junk1 = e2(nc.sbuf_tensor("junk1", [128, 1], F32))
            rnorm = e2(nc.sbuf_tensor("rnorm", [128, 8], F32))
            rnsc = e2(nc.sbuf_tensor("rnsc", [128, 8], F32))

            ps_q = [e2(nc.psum_tensor(f"ps_q{i}", [128, KT], F32)) for i in range(2)]
            ps_s = [e2(nc.psum_tensor(f"ps_s{i}", [128, KT], F32)) for i in range(2)]
            ps_pt = [e2(nc.psum_tensor(f"ps_pt{i}", [128, KT], BF16)) for i in range(2)]
            ps_o = [e2(nc.psum_tensor(f"ps_o{i}", [64, 128], F32)) for i in range(2)]

            ld, qmm, qev, smm, sfree = (S2["p2_ld"], S2["p2_qmm"], S2["p2_qev"],
                                        S2["p2_smm"], S2["p2_sfree"])
            pr, tpd, pte, pvd, opv = (S2["p2_pr"], S2["p2_tpd"], S2["p2_pte"],
                                      S2["p2_pvd"], S2["p2_opv"])
            oev, odma = S2["p2_oev"], S2["p2_odma"]
            PT, PV, PS, PY = [], [], [], []
            dbgs = Sem(e2(nc.semaphore("p2_dbg")))
            dbgv = Sem(e2(nc.semaphore("p2_dbgv")))
            uvm = Sem(e2(nc.semaphore("p2_uvm")))
            gup = Sem(e2(nc.semaphore("p2_gup")))
            snu = Sem(e2(nc.semaphore("p2_snu")))
            fsy = Sem(e2(nc.semaphore("p2_fsy")))
            srn = Sem(e2(nc.semaphore("p2_srn")))
            sdp = Sem(e2(nc.semaphore("p2_sdp")))
            vdf = Sem(e2(nc.semaphore("p2_vdf")))
            gup_need = {0: 0, 1: 0}
            PG = []

            def _ldw(y):
                for c in range(8):
                    y.dma_start(
                        out=wqkvT[:, c * (3 * D):(c + 1) * (3 * D)],
                        in_=wqkvT_sc[c * 128:(c + 1) * 128, :]).then_inc(ld.h, 16)
            PY.append(_ldw)
            w_wqkv = ld.inc(16 * 8)

            WARMUP = False
            def _vwarm(v):
                v.memset(dscr[0][:, :], 1.0)
                v.memset(partR2[:, 0:8], 1.0)
                v.memset(lam[:, :], 1.0)
                v.memset(nt1[:, :], 1.0)
                v.memset(nt2[:, :], 1.0)
                v.tensor_mask_reduce(
                    out=rstrip[:, 0:KT], in_=dscr[0][:, :], mask_start=0.0,
                    mask_end=lens_sb[:, 3:4], scale=1.0, accum_in=-1e30,
                    op=ALU.max, accum_out=partF[:, 0:1])
                v._custom_dve(RECIP7_SUM, out=rstrip[:, 0:KT], in0=dscr[0][:, :],
                              s0=C0_R, s1=0.0, imm2=C1_R,
                              accum_out=partR[:, 0:1])
                v._custom_dve(SQ_SUM, out=rstrip[:, KT:2 * KT],
                              in0=rstrip[:, 0:KT], s0=0.0, s1=0.0,
                              accum_out=partR[:, 1:2])
                v.reciprocal_approx_accurate(out=nt2[:, 0:4],
                                             in_=partR2[:, 0:4],
                                             scratch=srd[:, 0:4])
                v.tensor_copy(out=junk1[:, :],
                              in_=junk1[:, :]).then_inc(uvm.h, 1)
            if WARMUP:
                PV.append(_vwarm)
                uvm.inc()

            def _gwarm(g_):
                g_.wait_ge(uvm.h, 1)
                g_.tensor_tensor(out=sr2[:, 0:4], in0=nt1[:, 0:4],
                                 in1=nt2[:, 0:4], op=ALU.mult)
                g_.tensor_tensor(out=lam[:, 8:12], in0=lam[:, 0:4],
                                 in1=sr2[:, 0:4], op=ALU.add)
                g_.tensor_scalar(out=lamK[:, 0:4], in0=lam[:, 8:12],
                                 scalar1=K_R, scalar2=None,
                                 op0=ALU.mult).then_inc(gup.h, 1)
            if WARMUP:
                PG.append(_gwarm)
                gup.inc()
                PV.append(lambda v: v.wait_ge(gup.h, 1))

            qev_bank = {0: 0, 1: 0}
            pte_bank = {0: 0, 1: 0}
            pvd_slot = {0: 0, 1: 0}
            sfree_tile = {0: 0, 1: 0}
            oev_row = {0: 0, 1: 0}
            odma_buf = {0: 0, 1: 0}
            tpd_slot = {0: 0, 1: 0}

            state = {"qkv_tile": 0, "s_tile": 0, "p_tile": 0, "d_tile": 0}
            vdf_slot = {0: 0, 1: 0, 2: 0}

            def qkv_pair(b, hp):
                for jbase, scl, dstt in ((128 * hp, QPRE, qpair),
                                         (D + 128 * hp, 1.0, kpair)):
                    for tc in range(4):
                        bank = state["qkv_tile"] % 2
                        t0 = tc * KT

                        def _tm(t, jbase=jbase, t0=t0, bank=bank, b=b,
                                wfree=qev_bank[bank]):
                            t.wait_ge(ld.h, w_wqkv)
                            if wfree:
                                t.wait_ge(qev.h, wfree)
                            for c in range(8):
                                ins = t.matmul(
                                    ps_q[bank][:, :],
                                    wqkvT[:, c * 3 * D + jbase:
                                          c * 3 * D + jbase + 128],
                                    xn[:, c * NB * N + b * N + t0:
                                       c * NB * N + b * N + t0 + KT],
                                    start=(c == 0), stop=(c == 7))
                            ins.then_inc(qmm.h, 1)
                        PT.append(_tm)
                        w_mm = qmm.inc()

                        def _sev(s, bank=bank, w=w_mm, dstt=dstt, t0=t0, scl=scl):
                            s.wait_ge(qmm.h, w)
                            s.activation(out=dstt[:, t0:t0 + KT],
                                         in_=ps_q[bank][:, :], func=ACT.Copy,
                                         bias=0.0, scale=scl).then_inc(qev.h, 1)
                        PS.append(_sev)
                        qev_bank[bank] = qev.inc()
                        state["qkv_tile"] += 1
                jbase = 2 * D + 128 * hp
                for tc in range(NQB):
                    bank = state["qkv_tile"] % 2
                    t0 = tc * 128

                    def _tv(t, jbase=jbase, t0=t0, bank=bank, b=b,
                            wfree=qev_bank[bank]):
                        if wfree:
                            t.wait_ge(qev.h, wfree)
                        for c in range(8):
                            ins = t.matmul(
                                ps_q[bank][:, 0:128],
                                xn[:, c * NB * N + b * N + t0:
                                   c * NB * N + b * N + t0 + 128],
                                wqkvT[:, c * 3 * D + jbase:
                                      c * 3 * D + jbase + 128],
                                start=(c == 0), stop=(c == 7))
                        ins.then_inc(qmm.h, 1)
                    PT.append(_tv)
                    w_mm = qmm.inc()

                    def _svev(s, bank=bank, w=w_mm, tc=tc):
                        s.wait_ge(qmm.h, w)
                        s.activation(out=vpair[:, tc * 128:(tc + 1) * 128],
                                     in_=ps_q[bank][:, 0:128], func=ACT.Copy,
                                     bias=0.0, scale=1.0).then_inc(qev.h, 1)
                    PS.append(_svev)
                    qev_bank[bank] = qev.inc()
                    state["qkv_tile"] += 1

            def attn_group(b, h, g, w_pair):
                par = h % 2
                qb0 = g * 8
                rows = list(range(qb0, qb0 + 8))
                halves = [rows[0:4], rows[4:8]]
                tiles_rm = [(qb, kt) for qb in rows for kt in range(tiles_of(qb))]
                slot = {t: i for i, t in enumerate(tiles_rm)}

                def kts_of(qb):
                    dg = qb // 4
                    return [dg] + [k for k in range(tiles_of(qb)) if k != dg]

                tiles_cm = [(qb, kts_of(qb)[ci]) for ci in range(4)
                            for qb in rows if ci < tiles_of(qb)]
                for ci_, (qb, kt) in enumerate(tiles_cm):
                    bank = state["s_tile"] % 2
                    sl = slot[(qb, kt)]

                    def _ts(t, qb=qb, kt=kt, bank=bank, par=par,
                            wfree=sfree_tile[bank], w_pair=w_pair,
                            w_=wid(qb, kt)):
                        t.wait_ge(qev.h, w_pair)
                        if wfree:
                            t.wait_ge(sfree.h, wfree)
                        t.matmul(ps_s[bank][:, 0:w_],
                                 qpair[64 * par:64 * par + 64,
                                       qb * 128:(qb + 1) * 128],
                                 kpair[64 * par:64 * par + 64,
                                       kt * KT:kt * KT + w_],
                                 start=True, stop=True).then_inc(smm.h, 1)
                    PT.append(_ts)
                    w_smm = smm.inc()
                    first = (kt == 0)
                    rc = qb - qb0

                    def _vp(v, bank=bank, w=w_smm, sl=sl, qb=qb, kt=kt,
                            rc=rc, w_=wid(qb, kt)):
                        v.wait_ge(smm.h, w)
                        v.tensor_mask_reduce(
                            out=sgrp[:, sl * KT:sl * KT + w_],
                            in_=ps_s[bank][:, 0:w_],
                            mask_start=0.0,
                            mask_end=lens_sb[:, qb * 4 + kt:qb * 4 + kt + 1],
                            scale=SPOST,
                            accum_in=(-1e30 if kt == qb // 4
                                      else rowmaxP[:, rc:rc + 1]),
                            op=ALU.max,
                            accum_out=rowmaxP[:, rc:rc + 1]
                        ).then_inc(sfree.h, 1)
                    PV.append(_vp)
                    sfree_tile[bank] = sfree.inc()
                    state["s_tile"] += 1

                def emit_lambda_init(hf):
                    hb = hf * 4

                    def _vl(v):
                        v.tensor_copy(out=junk1[:, :],
                                      in_=junk1[:, :]).then_inc(uvm.h, 1)
                    PV.append(_vl)
                    wv = uvm.inc()

                    def _gl(g_, hb=hb, wv=wv):
                        g_.wait_ge(uvm.h, wv)
                        g_.tensor_scalar(out=lam[:, hb:hb + 4],
                                         in0=rowmaxP[:, hb:hb + 4],
                                         scalar1=1.0, scalar2=None,
                                         op0=ALU.add)
                        g_.tensor_scalar(out=lamK[:, hb:hb + 4],
                                         in0=lam[:, hb:hb + 4], scalar1=K_R,
                                         scalar2=None,
                                         op0=ALU.mult).then_inc(gup.h, 1)
                    PG.append(_gl)
                    gup_need[hf] = gup.inc()

                def emit_d_pass(sl, rc, w_):
                    pb = state["d_tile"] % 3
                    state["d_tile"] += 1

                    def _sd(s, sl=sl, rc=rc, pb=pb, wg=gup_need[rc // 4],
                            wv=vdf_slot[pb], w_=w_):
                        s.wait_ge(gup.h, wg)
                        if wv:
                            s.wait_ge(vdf.h, wv)
                        s.activation(out=dscr[pb][:, 0:w_],
                                     in_=sgrp[:, sl * KT:sl * KT + w_],
                                     func=ACT.Identity,
                                     bias=lamK[:, rc:rc + 1],
                                     scale=NEG_KR).then_inc(sdp.h, 1)
                    PS.append(_sd)
                    return pb, sdp.inc()

                def emit_iter_tiles(hf):
                    half = halves[hf]
                    nkt = tiles_of(half[0])
                    for ci in range(nkt):
                        for qb in half:
                            kt = kts_of(qb)[ci]
                            sl = slot[(qb, kt)]
                            rc = qb - qb0
                            w_ = wid(qb, kt)
                            seed0 = (ci == 0)
                            pb, w_sd = emit_d_pass(sl, rc, w_)

                            def _vi(v, rc=rc, kt=kt, pb=pb, w_sd=w_sd, w_=w_,
                                    seed0=seed0):
                                v.wait_ge(sdp.h, w_sd)
                                v._custom_dve(
                                    RECIP7_SUM,
                                    out=rstrip[:, kt * KT:kt * KT + w_],
                                    in0=dscr[pb][:, 0:w_], s0=C0_R,
                                    s1=(0.0 if seed0 else partR[:, rc:rc + 1]),
                                    imm2=C1_R,
                                    accum_out=partR[:, rc:rc + 1]
                                ).then_inc(vdf.h, 1)
                                v._custom_dve(
                                    SQ_SUM, out=rstrip[:, 4 * KT:4 * KT + w_],
                                    in0=rstrip[:, kt * KT:kt * KT + w_],
                                    s0=(0.0 if seed0 else partR2[:, rc:rc + 1]),
                                    s1=0.0,
                                    accum_out=partR2[:, rc:rc + 1])
                            PV.append(_vi)
                            vdf_slot[pb] = vdf.inc()

                def emit_update_marker(hf):
                    def _vn(v):
                        v.tensor_copy(out=junk1[:, :],
                                      in_=junk1[:, :]).then_inc(uvm.h, 1)
                    PV.append(_vn)
                    return uvm.inc()

                def emit_update_sg(it, hf, wv):
                    hb = hf * 4
                    sb = 8 * (it % 2)
                    db = 8 * ((it + 1) % 2)

                    def _sn(s, hb=hb, wv=wv):
                        s.wait_ge(uvm.h, wv)
                        s.activation(out=srd[:, hb:hb + 4],
                                     in_=partR[:, hb:hb + 4],
                                     func=ACT.Identity,
                                     bias=lens_sb[:, NQB * 4 + 1:NQB * 4 + 2],
                                     scale=1.0)
                        s.activation(out=srd[:, 8 + hb:12 + hb],
                                     in_=partR2[:, hb:hb + 4],
                                     func=ACT.Ln, bias=0.0, scale=1.0)
                        s.activation(out=nt1[:, hb:hb + 4],
                                     in_=srd[:, hb:hb + 4], func=ACT.Copy,
                                     bias=0.0, scale=K3)
                        s.activation(out=nt2[:, hb:hb + 4],
                                     in_=srd[:, 8 + hb:12 + hb], func=ACT.Exp,
                                     bias=0.0, scale=-1.0).then_inc(snu.h, 1)
                    PS.append(_sn)
                    ws = snu.inc()

                    def _gn(g_, hb=hb, sb=sb, db=db, ws=ws):
                        g_.wait_ge(snu.h, ws)
                        g_.tensor_tensor(out=sr2[:, hb:hb + 4],
                                         in0=nt1[:, hb:hb + 4],
                                         in1=nt2[:, hb:hb + 4], op=ALU.mult)
                        g_.tensor_tensor(out=lam[:, db + hb:db + hb + 4],
                                         in0=lam[:, sb + hb:sb + hb + 4],
                                         in1=sr2[:, hb:hb + 4], op=ALU.add)
                        g_.tensor_scalar(out=lamK[:, hb:hb + 4],
                                         in0=lam[:, db + hb:db + hb + 4],
                                         scalar1=K_R, scalar2=None,
                                         op0=ALU.mult).then_inc(gup.h, 1)
                    PG.append(_gn)
                    gup_need[hf] = gup.inc()

                def emit_f1_row(qb):
                    rc = qb - qb0
                    nkt = tiles_of(qb)
                    ps = (qb % 3) * 4
                    for kt in kts_of(qb):
                        sl = slot[(qb, kt)]
                        w_ = wid(qb, kt)
                        pb, w_sd = emit_d_pass(sl, rc, w_)

                        def _vf1(v, rc=rc, kt=kt, ps=ps, pb=pb, w_sd=w_sd,
                                 w_=w_):
                            v.wait_ge(sdp.h, w_sd)
                            v._custom_dve(
                                RECIP7_SUM,
                                out=rstrip[:, (ps + kt) * KT:(ps + kt) * KT + w_],
                                in0=dscr[pb][:, 0:w_], s0=C0_R, s1=0.0, imm2=C1_R,
                                accum_out=partF[:, ps + kt:ps + kt + 1]
                            ).then_inc(vdf.h, 1)
                        PV.append(_vf1)
                        vdf_slot[pb] = vdf.inc()
                    PV.append(lambda v: v.tensor_copy(
                        out=junk1[:, :], in_=junk1[:, :]).then_inc(fsy.h, 1))
                    return fsy.inc()

                def emit_rnorm_s(qb, wf):
                    rc = qb - qb0
                    nkt = tiles_of(qb)
                    ps = (qb % 3) * 4
                    sl = qb % 2

                    def _sr(s, rc=rc, nkt=nkt, ps=ps, base=sl * 4, wf=wf):
                        s.wait_ge(fsy.h, wf)
                        if nkt == 1:
                            src = partF[:, ps:ps + 1]
                        else:
                            s.activation(out=rnsc[:, base:base + 1],
                                         in_=partF[:, ps:ps + 1],
                                         func=ACT.Identity,
                                         bias=partF[:, ps + 1:ps + 2],
                                         scale=1.0)
                            src = rnsc[:, base:base + 1]
                            if nkt == 3:
                                s.activation(out=rnsc[:, base + 1:base + 2],
                                             in_=rnsc[:, base:base + 1],
                                             func=ACT.Identity,
                                             bias=partF[:, ps + 2:ps + 3],
                                             scale=1.0)
                                src = rnsc[:, base + 1:base + 2]
                            elif nkt == 4:
                                s.activation(out=rnsc[:, base + 1:base + 2],
                                             in_=rnsc[:, base:base + 1],
                                             func=ACT.Identity,
                                             bias=partF[:, ps + 2:ps + 3],
                                             scale=1.0)
                                s.activation(out=rnsc[:, base + 2:base + 3],
                                             in_=rnsc[:, base + 1:base + 2],
                                             func=ACT.Identity,
                                             bias=partF[:, ps + 3:ps + 4],
                                             scale=1.0)
                                src = rnsc[:, base + 2:base + 3]
                        s.activation(out=rnsc[:, base + 3:base + 4], in_=src,
                                     func=ACT.Ln, bias=0.0, scale=-1.0)
                        s.activation(out=rnorm[:, rc:rc + 1],
                                     in_=rnsc[:, base + 3:base + 4],
                                     func=ACT.Exp, bias=0.0,
                                     scale=-1.0).then_inc(srn.h, 1)
                    PS.append(_sr)
                    return srn.inc()

                dbg_here = dbg and b == 0 and h == 2

                def dbg_dump(dst, srcap):
                    def _vm(v):
                        v.tensor_copy(out=junk1[:, :],
                                      in_=junk1[:, :]).then_inc(dbgv.h, 1)
                    PV.append(_vm)
                    wv = dbgv.inc()

                    def _gd(g_, dst=dst, srcap=srcap, wv=wv):
                        g_.wait_ge(dbgv.h, wv)
                        g_.dma_start(out=dst, in_=srcap).then_inc(dbgs.h, 16)
                    PG.append(_gd)
                    wd = dbgs.inc(16)
                    PV.append(lambda v, wd=wd: v.wait_ge(dbgs.h, wd))

                if dbg_here:
                    ntot = sum(tiles_of(q) for q in rows)
                    dbg_dump(dbg_sgrp[g, :, 0:ntot * KT], sgrp[:, 0:ntot * KT])
                    dbg_dump(dbg_rmax[g, :, :], rowmaxP[:, :])
                emit_lambda_init(0)
                pend1 = None
                for i in range(4):
                    emit_iter_tiles(0)
                    if pend1 is not None:
                        emit_update_sg(*pend1)
                    wv0 = emit_update_marker(0)
                    if i == 0:
                        emit_lambda_init(1)
                    emit_iter_tiles(1)
                    emit_update_sg(i, 0, wv0)
                    pend1 = (i, 1, emit_update_marker(1))
                emit_update_sg(*pend1)


                def emit_f2_row(qb):
                    rc = qb - qb0
                    nkt = tiles_of(qb)
                    ps = (qb % 3) * 4
                    w_opv = None
                    for kt in range(nkt):
                        pslot = state["p_tile"] % 2

                        w_ = wid(qb, kt)
                        nch = w_ // 128

                        def _vf2(v, kt=kt, pslot=pslot, wtp=tpd_slot[pslot],
                                 rc=rc, ps=ps, wrn=rn_need.get(qb, 0), w_=w_):
                            if kt == 0 and wrn:
                                v.wait_ge(srn.h, wrn)
                            if wtp:
                                v.wait_ge(tpd.h, wtp)
                            v.tensor_scalar(out=pbf[pslot][:, 0:w_],
                                            in0=rstrip[:, (ps + kt) * KT:
                                                       (ps + kt) * KT + w_],
                                            scalar1=rnorm[:, rc:rc + 1],
                                            scalar2=-1.0,
                                            op0=ALU.mult,
                                            op1=ALU.mult).then_inc(pr.h, 1)
                        PV.append(_vf2)
                        w_pr = pr.inc()

                        def _ttp(t, pslot=pslot, w=w_pr, wpe=pte_bank[pslot],
                                 nch=nch):
                            t.wait_ge(pr.h, w)
                            if wpe:
                                t.wait_ge(pte.h, wpe)
                            for c in range(nch):
                                ins = t.transpose(
                                    ps_pt[pslot][:, c * 128:(c + 1) * 128],
                                    pbf[pslot][:, c * 128:(c + 1) * 128],
                                    ident[:, :])
                            ins.then_inc(tpd.h, 1)
                        PT.append(_ttp)
                        tpd_slot[pslot] = tpd.inc()

                        def _spt(s, pslot=pslot, w=tpd_slot[pslot],
                                 wpv=pvd_slot[pslot], w_=w_):
                            s.wait_ge(tpd.h, w)
                            if wpv:
                                s.wait_ge(pvd.h, wpv)
                            s.copy(out=ptsb[pslot][:, 0:w_],
                                   in_=ps_pt[pslot][:, 0:w_]).then_inc(pte.h, 1)
                        PS.append(_spt)
                        pte_bank[pslot] = pte.inc()

                        obank = qb % 2

                        def _tpv(t, pslot=pslot, w=pte_bank[pslot], kt=kt,
                                 par=par, obank=obank, first=(kt == 0),
                                 last=(kt == nkt - 1),
                                 wo_free=oev_row[obank], nch=nch):
                            t.wait_ge(pte.h, w)
                            if first and wo_free:
                                t.wait_ge(oev.h, wo_free)
                            for c in range(nch):
                                tc = kt * 4 + c
                                ins = t.matmul(
                                    ps_o[obank][:, :],
                                    vpair[:, tc * 128 + 64 * par:
                                          tc * 128 + 64 * par + 64],
                                    ptsb[pslot][:, c * 128:(c + 1) * 128],
                                    start=(first and c == 0),
                                    stop=(last and c == nch - 1))
                            ins.then_inc(pvd.h, 1)
                        PT.append(_tpv)
                        pvd_slot[pslot] = pvd.inc()
                        if kt == nkt - 1:
                            w_opv = pvd.n
                        state["p_tile"] += 1

                    def _soev(s, qb=qb, w=w_opv, obank=qb % 2):
                        s.wait_ge(pvd.h, w)
                        s.activation(out=othead[:, qb * 128:(qb + 1) * 128],
                                     in_=ps_o[obank][:, :], func=ACT.Copy,
                                     bias=0.0, scale=1.0).then_inc(oev.h, 1)
                    PS.append(_soev)
                    oev_row[qb % 2] = oev.inc()

                rn_need = {}
                wf_row = {rows[0]: emit_f1_row(rows[0])}
                for j in range(1, 8):
                    if j == 4 and dbg_here:
                        dbg_dump(dbg_lam[g, :, :], lam[:, 0:8])
                    wf_row[rows[j]] = emit_f1_row(rows[j])
                    rn_need[rows[j - 1]] = emit_rnorm_s(rows[j - 1],
                                                        wf_row[rows[j - 1]])
                    if j >= 2:
                        emit_f2_row(rows[j - 2])
                rn_need[rows[7]] = emit_rnorm_s(rows[7], wf_row[rows[7]])
                emit_f2_row(rows[6])
                emit_f2_row(rows[7])
                if dbg_here:
                    dbg_dump(dbg_rn[g, :, :], rnorm[:, :])

            bh = 0
            for b in range(NB):
                for h in range(H):
                    if h % 2 == 0:
                        qkv_pair(b, h // 2)
                        if dbg and b == 0 and h == 2:
                            def _gdq(g_, w=qev.n):
                                g_.wait_ge(qev.h, w)
                                g_.dma_start(out=dbg_q[:, :],
                                             in_=qpair[:, :]).then_inc(dbgs.h, 16)
                                g_.dma_start(out=dbg_k[:, :],
                                             in_=kpair[:, :]).then_inc(dbgs.h, 16)
                                g_.dma_start(out=dbg_v[:, :],
                                             in_=vpair[:, :]).then_inc(dbgs.h, 16)
                            PG.append(_gdq)
                            dbgs.inc(48)
                    w_pair = qev.n
                    if odma_buf[0]:
                        def _sw(s, w=odma_buf[0]):
                            s.wait_ge(odma.h, w)
                        PS.append(_sw)
                    for g in range(2):
                        attn_group(b, h, g, w_pair)
                    w_all_oev = oev.n

                    def _sdma(s, b=b, h=h, w=w_all_oev):
                        s.wait_ge(oev.h, w)
                        s.dma_start(out=oT_sc[b, h, :, :],
                                    in_=othead[:, :]).then_inc(odma.h, 16)
                    PS.append(_sdma)
                    odma_buf[0] = odma.inc(16)
                    bh += 1

            PS.append(lambda s: s.wait_ge(odma.h, odma.n))
            if dbg:
                PY.append(lambda y: y.wait_ge(dbgs.h, dbgs.n))
            _run_plans(nc, PT, PV, PS, PY, PG)

        if phases < 3:
            return _finish(nc)
        with ExitStack() as p3:
            e3 = p3.enter_context
            woT = e3(nc.sbuf_tensor("woT", [128, 8 * D], BF16))
            wdT = e3(nc.sbuf_tensor("wdT", [128, 24 * D], BF16))
            oT_ld = [e3(nc.sbuf_tensor(f"oT_ld{i}", [128, 8 * KT], BF16))
                     for i in range(2)]
            wgld = [e3(nc.sbuf_tensor(f"wgld{i}", [128, 8 * 128], BF16))
                    for i in range(2)]
            wuld = [e3(nc.sbuf_tensor(f"wuld{i}", [128, 8 * 128], BF16))
                    for i in range(2)]
            hbuf = e3(nc.sbuf_tensor("hbuf", [128, 24 * KT], BF16))
            sgate = [e3(nc.sbuf_tensor(f"sgate{i}", [128, KT], F32))
                     for i in range(2)]
            gbuf = [e3(nc.sbuf_tensor(f"gbuf{i}", [128, D], F32))
                    for i in range(2)]
            sgbuf = e3(nc.sbuf_tensor("sgbuf", [128, D], F32))
            xres = [e3(nc.sbuf_tensor(f"xres{i}", [128, D], F32)) for i in range(2)]
            outb = [e3(nc.sbuf_tensor(f"outb{i}", [128, D], F32)) for i in range(2)]

            ps_g = [e3(nc.psum_tensor(f"ps_g{i}", [128, KT], F32)) for i in range(2)]
            ps_u = [e3(nc.psum_tensor(f"ps_u{i}", [128, KT], F32)) for i in range(2)]
            ps_wo = e3(nc.psum_tensor("ps_wo", [128, D], F32))
            ps_dn = e3(nc.psum_tensor("ps_dn", [128, D], F32))

            ld, gmm, hdone, sgd = (S3["p3_ld"], S3["p3_gmm"], S3["p3_hdone"],
                                   S3["p3_sgd"])
            womm, dnmm, gsil, vout, st = (S3["p3_womm"], S3["p3_dnmm"],
                                          S3["p3_gsil"], S3["p3_vout"],
                                          S3["p3_st"])
            PT, PV, PS, PY = [], [], [], []

            def _lw(y):
                for c in range(8):
                    y.dma_start(out=woT[:, c * D:(c + 1) * D],
                                in_=woT_sc[c * 128:(c + 1) * 128, :]).then_inc(ld.h, 16)
                for c in range(24):
                    y.dma_start(out=wdT[:, c * D:(c + 1) * D],
                                in_=wdT_sc[c * 128:(c + 1) * 128, :]).then_inc(ld.h, 16)
            PY.append(_lw)
            w_w = ld.inc(16 * 32)

            wg_free = {0: 0, 1: 0}
            oT_free = {0: 0, 1: 0}
            x_free = {0: 0, 1: 0}
            out_free = {0: 0, 1: 0}
            h_bank = {0: 0, 1: 0}
            sg_free = {0: 0, 1: 0}
            g_free = {0: 0, 1: 0}
            wo_prev = 0
            dn_prev = 0

            nspan = NB * N // KT
            subt = 0
            for sp in range(nspan):
                b, t0 = sp // (N // KT), (sp % (N // KT)) * KT
                obuf = sp % 2
                if oT_free[obuf]:
                    PY.append(lambda y, w=oT_free[obuf]: y.wait_ge(womm.h, w))

                def _lo(y, b=b, t0=t0, obuf=obuf):
                    for c in range(8):
                        y.dma_start(out=oT_ld[obuf][0:64, c * KT:(c + 1) * KT],
                                    in_=oT_sc[b, 2 * c, :, t0:t0 + KT]
                                    ).then_inc(ld.h, 16)
                        y.dma_start(out=oT_ld[obuf][64:128, c * KT:(c + 1) * KT],
                                    in_=oT_sc[b, 2 * c + 1, :, t0:t0 + KT]
                                    ).then_inc(ld.h, 16)
                PY.append(_lo)
                w_oT = ld.inc(16 * 16)

                for ht in range(24):
                    wbuf = ht % 2
                    if wg_free[wbuf]:
                        PY.append(lambda y, w=wg_free[wbuf]: y.wait_ge(gmm.h, w))

                    def _lwg(y, ht=ht, wbuf=wbuf):
                        ap_g = bass.AP(wgT_sc, ht * 128,
                                       [[HID, 128], [128 * HID, 8], [1, 128]])
                        ap_u = bass.AP(wuT_sc, ht * 128,
                                       [[HID, 128], [128 * HID, 8], [1, 128]])
                        y.dma_start(out=wgld[wbuf][:, :], in_=ap_g).then_inc(ld.h, 16)
                        y.dma_start(out=wuld[wbuf][:, :], in_=ap_u).then_inc(ld.h, 16)
                    PY.append(_lwg)
                    w_wg = ld.inc(32)
                    pbank = ht % 2

                    def _tg(t, wbuf=wbuf, b=b, t0=t0, w=w_wg, pbank=pbank,
                            whd=h_bank[pbank]):
                        t.wait_ge(ld.h, w)
                        if whd:
                            t.wait_ge(hdone.h, whd)
                        for c in range(8):
                            t.matmul(ps_g[pbank][:, :],
                                     wgld[wbuf][:, c * 128:(c + 1) * 128],
                                     xn[:, c * NB * N + b * N + t0:
                                        c * NB * N + b * N + t0 + KT],
                                     start=(c == 0), stop=(c == 7))
                        for c in range(8):
                            ins = t.matmul(ps_u[pbank][:, :],
                                           wuld[wbuf][:, c * 128:(c + 1) * 128],
                                           xn[:, c * NB * N + b * N + t0:
                                              c * NB * N + b * N + t0 + KT],
                                           start=(c == 0), stop=(c == 7))
                        ins.then_inc(gmm.h, 1)
                    PT.append(_tg)
                    wg_free[wbuf] = gmm.inc()

                    def _ssil(s, w=wg_free[wbuf], pbank=pbank,
                              wsg=sg_free[pbank]):
                        s.wait_ge(gmm.h, w)
                        if wsg:
                            s.wait_ge(hdone.h, wsg)
                        s.activation(out=sgate[pbank][:, :], in_=ps_g[pbank][:, :],
                                     func=ACT.Silu, bias=0.0,
                                     scale=1.0).then_inc(sgd.h, 1)
                    PS.append(_ssil)
                    w_sg = sgd.inc()

                    def _vh(v, ht=ht, w=w_sg, pbank=pbank):
                        v.wait_ge(sgd.h, w)
                        v.tensor_tensor(out=hbuf[:, ht * KT:(ht + 1) * KT],
                                        in0=sgate[pbank][:, :],
                                        in1=ps_u[pbank][:, :],
                                        op=ALU.mult).then_inc(hdone.h, 1)
                    PV.append(_vh)
                    h_bank[pbank] = hdone.inc()
                    sg_free[pbank] = h_bank[pbank]
                w_h_span = hdone.n

                for stt in range(4):
                    ts0 = stt * 128
                    xbuf = subt % 2
                    if x_free[xbuf]:
                        PY.append(lambda y, w=x_free[xbuf]: y.wait_ge(vout.h, w))

                    def _lx(y, b=b, t0=t0, ts0=ts0, xbuf=xbuf):
                        y.dma_start(out=xres[xbuf][:, :],
                                    in_=x_in[b, t0 + ts0:t0 + ts0 + 128, :]
                                    ).then_inc(ld.h, 16)
                    PY.append(_lx)
                    w_x = ld.inc(16)

                    def _two(t, obuf=obuf, ts0=ts0, w_oT=w_oT, wv=wo_prev):
                        t.wait_ge(ld.h, max(w_oT, w_w))
                        if wv:
                            t.wait_ge(vout.h, wv)
                        for dh2 in range(2):
                            for c in range(8):
                                ins = t.matmul(
                                    ps_wo[:, dh2 * 512:(dh2 + 1) * 512],
                                    oT_ld[obuf][:, c * KT + ts0:
                                                c * KT + ts0 + 128],
                                    woT[:, c * D + dh2 * 512:
                                        c * D + dh2 * 512 + 512],
                                    start=(c == 0), stop=(c == 7))
                        ins.then_inc(womm.h, 1)
                    PT.append(_two)
                    oT_free[obuf] = womm.inc()

                    def _tdn(t, ts0=ts0, w=w_h_span, wg=dn_prev):
                        t.wait_ge(hdone.h, w)
                        if wg:
                            t.wait_ge(gsil.h, wg)
                        for dh2 in range(2):
                            for c in range(24):
                                ins = t.matmul(
                                    ps_dn[:, dh2 * 512:(dh2 + 1) * 512],
                                    hbuf[:, c * KT + ts0:c * KT + ts0 + 128],
                                    wdT[:, c * D + dh2 * 512:
                                        c * D + dh2 * 512 + 512],
                                    start=(c == 0), stop=(c == 23))
                        ins.then_inc(dnmm.h, 1)
                    PT.append(_tdn)
                    w_dn = dnmm.inc()

                    gb = subt % 2

                    def _sgl(s, w=w_dn, gb=gb, wgf=g_free[gb]):
                        s.wait_ge(dnmm.h, w)
                        if wgf:
                            s.wait_ge(vout.h, wgf)
                        s.activation(out=gbuf[gb][:, :], in_=ps_dn[:, :],
                                     func=ACT.Silu, bias=0.0,
                                     scale=1.0).then_inc(gsil.h, 1)
                    PS.append(_sgl)
                    w_gs = gsil.inc()
                    dn_prev = w_gs

                    def _vo(v, xbuf=xbuf, w_gs=w_gs, w_wo=oT_free[obuf],
                            w_x=w_x, wst=out_free[xbuf], gb=gb):
                        v.wait_ge(gsil.h, w_gs)
                        v.wait_ge(womm.h, w_wo)
                        v.wait_ge(ld.h, w_x)
                        if wst:
                            v.wait_ge(st.h, wst)
                        v.tensor_tensor(out=sgbuf[:, :], in0=ps_wo[:, :],
                                        in1=gbuf[gb][:, :], op=ALU.mult)
                        v.tensor_tensor(out=outb[xbuf][:, :], in0=sgbuf[:, :],
                                        in1=xres[xbuf][:, :],
                                        op=ALU.add).then_inc(vout.h, 1)
                    PV.append(_vo)
                    x_free[xbuf] = vout.inc()
                    wo_prev = x_free[xbuf]
                    g_free[gb] = x_free[xbuf]

                    def _so(y, b=b, t0=t0, ts0=ts0, xbuf=xbuf, w=x_free[xbuf]):
                        y.wait_ge(vout.h, w)
                        y.dma_start(out=out_d[b, t0 + ts0:t0 + ts0 + 128, :],
                                    in_=outb[xbuf][:, :]).then_inc(st.h, 16)
                    PY.append(_so)
                    out_free[xbuf] = st.inc(16)
                    subt += 1

            PY.append(lambda y: y.wait_ge(st.h, st.n))
            _run_plans(nc, PT, PV, PS, PY)

    return _finish(nc)


def _finish(nc):
    mybir.codegen_inst_isa_subclasses(nc)
    return nc


_NC_CACHE = {}


def _get_nc():
    if "nc" not in _NC_CACHE:
        _NC_CACHE["nc"] = build()
    return _NC_CACHE["nc"]


def kernel(L0_attn_x, L0_attn_Wqkv, L0_attn_Wo, L0_ffn_w_gate, L0_ffn_w_up,
           L0_ffn_w_down):
    x = np.ascontiguousarray(np.asarray(L0_attn_x, dtype=np.float32))
    wqkvv = np.ascontiguousarray(np.asarray(L0_attn_Wqkv, dtype=np.float32))
    wov = np.ascontiguousarray(np.asarray(L0_attn_Wo, dtype=np.float32))
    wgv = np.ascontiguousarray(np.asarray(L0_ffn_w_gate, dtype=np.int32))
    wuv = np.ascontiguousarray(np.asarray(L0_ffn_w_up, dtype=np.int32))
    wdv = np.ascontiguousarray(np.asarray(L0_ffn_w_down, dtype=np.int32))
    ident, identf, lens = _host_consts()

    nc = _get_nc()
    in_maps = []
    for c in range(8):
        in_maps.append({
            "x": x[c * NB:(c + 1) * NB],
            "wqkv": wqkvv, "wo": wov, "wg": wgv, "wu": wuv, "wd": wdv,
            "ident": ident, "identf": identf, "lens": lens,
        })
    res = run_bass_kernel_spmd(nc, in_maps, core_ids=list(range(8)))
    out = np.concatenate([res.results[c]["out"] for c in range(8)], axis=0)
    return out
